# version 09
# baseline (speedup 1.0000x reference)
import numpy as np
import ml_dtypes
import concourse.bass as bass
import concourse.mybir as mybir
from concourse.bass_utils import run_bass_kernel_spmd

F32 = mybir.dt.float32
BF16 = mybir.dt.bfloat16
I32 = mybir.dt.int32
AF = mybir.ActivationFunctionType
ALU = mybir.AluOpType
AX = mybir.AxisListType.X

R = 4
S = 2048
D = 1024
SEL = 1024
IN2 = 2048
HM = 4096
E = 8
NT = S // 128
NSEL = SEL // 128
DC = D // 128
EPS = 1e-5


_GUARD_OPS = {
    "tensor_tensor", "tensor_add", "tensor_sub", "tensor_mul", "tensor_max",
    "tensor_scalar", "tensor_single_scalar", "tensor_scalar_add", "tensor_scalar_sub",
    "tensor_scalar_mul", "tensor_scalar_max", "tensor_scalar_min", "tensor_relu",
    "tensor_reduce", "reduce_max", "reduce_sum", "scalar_tensor_tensor",
    "tensor_copy", "memset", "memzero", "reciprocal", "max", "max_index",
    "select", "copy_predicated", "activation", "copy", "add", "mul", "square",
    "sqrt", "sign",
}
_OUT_KW = ("out", "accum_out", "out_max", "out_indices", "dest", "ap")


class Guard:

    def __init__(self, eng):
        self._e = eng
        self._dirty = set()

    @staticmethod
    def _name(ap):
        t = getattr(ap, "tensor", None)
        return getattr(t, "name", None)

    def __getattr__(self, k):
        fn = getattr(self._e, k)
        if k not in _GUARD_OPS:
            return fn

        def wrap(*args, **kw):
            outs, ins = [], []
            for i, a_ in enumerate(args):
                nm = Guard._name(a_)
                if nm is None:
                    continue
                (outs if i == 0 else ins).append(nm)
            for kk, a_ in kw.items():
                nm = Guard._name(a_)
                if nm is None:
                    continue
                (outs if kk in _OUT_KW else ins).append(nm)
            if any(nm in self._dirty for nm in ins):
                self._e.drain()
                self._dirty.clear()
            r = fn(*args, **kw)
            self._dirty.update(outs)
            return r

        return wrap


class Sem:
    def __init__(self, h):
        self.h = h
        self.n = 0

    def inc(self, instr, k):
        instr.then_inc(self.h, k)
        self.n += k
        return instr


class StopEmission(Exception):
    pass


class SlotSem:
    def __init__(self, mk, name, n=2):
        self.s = [mk(name + str(i)) for i in range(n)]
        self.nslots = n

    def sem(self, i):
        return self.s[i % self.nslots]

    def inc(self, i, instr, k=16):
        return self.s[i % self.nslots].inc(instr, k)

    def val(self, i):
        return self.s[i % self.nslots].n


def build_nc(debug=False, maxphase=999):
    import contextlib
    nc = bass.Bass("TRN2", target_bir_lowering=False)
    ctx = contextlib.ExitStack()

    _phc = [0]

    def CUT(label=""):
        _phc[0] += 1
        if _phc[0] > maxphase:
            raise StopEmission(label)

    dti = lambda name, shape, dtype=F32: nc.dram_tensor(name, shape, dtype, kind="ExternalInput").ap()
    xin = dti("xin", [R * S, D])
    w_m_in = dti("m_in_w", [D, 2 * IN2])
    h_conv3 = dti("h_conv3", [128, 16])
    h_convb = dti("h_convb", [128, 16])
    w_xp = dti("m_xproj_w", [IN2, 33])
    w_mout = dti("m_out_w", [IN2, D])
    w_r1 = dti("modr_w1", [D, 256])
    b_r2 = dti("modr_b2", [1])
    h_r1b = dti("h_r1b", [128, 2])
    h_r2 = dti("h_r2", [128, 2])
    ln1w = dti("ln1_w", [D]); ln1b = dti("ln1_b", [D])
    ln2w = dti("ln2_w", [D]); ln2b = dti("ln2_b", [D])
    h_mrw = dti("h_mrw", [128, 64]); b_mr = dti("moe_r_b", [E])
    w1_moe = dti("moe_w1", [E, D, HM]); h_b1 = dti("h_b1", [128, 256])
    w2_moe = dti("moe_w2", [E, HM, D]); b2_moe = dti("moe_b2", [E, D])
    h_gw = dti("h_gw", [128, 64]); b_gate = dti("gate_b", [E])
    w_ffn = dti("expert_w", [E, D, D]); b_ffn = dti("expert_b", [E, D])
    cONES = nc.dram_tensor("cONES", [1, 128], BF16, kind="ExternalInput").ap()
    cONESC = nc.dram_tensor("cONESC", [128, 1], BF16, kind="ExternalInput").ap()
    cTRIle = nc.dram_tensor("cTRIle", [128, 128], BF16, kind="ExternalInput").ap()
    cTRIst = nc.dram_tensor("cTRIst", [128, 128], BF16, kind="ExternalInput").ap()
    cID = nc.dram_tensor("cID", [128, 128], BF16, kind="ExternalInput").ap()

    out = nc.dram_tensor("out", [R * S, D], F32, kind="ExternalOutput").ap()

    dbg = {}
    if debug:
        def DB(name, shape, dtype=F32):
            dbg[name] = nc.dram_tensor("dbg_" + name, shape, dtype, kind="ExternalOutput").ap()
        DB("scores", [R, S]); DB("rank", [R, S]); DB("mw", [R, S]); DB("c", [R, S])
        DB("wgt", [R, S, E]); DB("gate", [R, SEL, E]); DB("hid", [R, S, D], BF16)
        DB("tok", [R, SEL, D], BF16); DB("n1", [R, SEL, D], BF16); DB("n2", [R, SEL, D], BF16)
        DB("hrh", [R, 128, 4096], BF16); DB("hrm", [R, 128, 4096], BF16); DB("mid", [R, S, D], BF16)
        DB("w1s", [128, 4096], BF16); DB("hiT", [R, 128, 16384], BF16); DB("midT", [R, 128, 16384], BF16)
        DB("lnB", [4, 128, D], BF16); DB("wgt2", [R, 128, 128]); DB("wgt3", [R, 128, 128])
        DB("srh", [R, 1, 2048], BF16); DB("srm", [R, 1, 2048], BF16); DB("sT", [R, 128, 16])
        DB("u", [R, IN2, SEL], BF16); DB("y", [R, IN2, SEL], BF16)
        DB("tok2", [R, SEL, D]); DB("tok3", [R, SEL, D]); DB("g", [R, S, D], BF16)

    scr = lambda name, shape, dtype: nc.dram_tensor(name, shape, dtype, kind="Internal").ap()
    hid = scr("hid", [2 * S, D], BF16)
    midd = scr("midd", [2 * S, D], BF16)
    n1d = scr("n1d", [SEL, D], BF16)
    n2d = scr("n2d", [SEL, D], BF16)
    tokd = scr("tokd", [SEL, D], BF16)
    tokfd = scr("tokfd", [SEL, D], BF16)
    gd = scr("gd", [S, D], BF16)

    sb = lambda name, shape, dtype: ctx.enter_context(nc.sbuf_tensor(name, shape, dtype))[:, :]

    GB0 = sb("GB0", [128, 32768], BF16)
    GB1 = sb("GB1", [128, 16384], BF16)
    GB2 = sb("GB2", [128, 8192], BF16)
    GF0 = sb("GF0", [128, 8192], F32)
    TOKB = sb("TOKB", [128, 8192], BF16)
    SA = sb("SA", [128, 2048], BF16)

    ones = sb("ones", [1, 128], BF16)
    onesc = sb("onesc", [128, 1], BF16)
    triLE = sb("triLE", [128, 128], BF16)
    triST = sb("triST", [128, 128], BF16)
    idmat = sb("idmat", [128, 128], BF16)
    lnw1B = sb("lnw1B", [128, D], BF16)
    lnb1B = sb("lnb1B", [128, D], BF16)
    lnw2B = sb("lnw2B", [128, D], BF16)
    lnb2B = sb("lnb2B", [128, D], BF16)
    w1hi = sb("w1hi", [128, 2048], BF16)
    w1lo = sb("w1lo", [128, 2048], BF16)
    w1f = GF0[:, 0:2048]
    w2hl = sb("w2hl", [128, 4], BF16)
    w2f = sb("w2f", [128, 2], F32)
    b1r = sb("b1r", [128, 2], F32)
    xpf = sb("xpf", [128, 33], F32)
    xpB = sb("xpB", [128, 16], BF16)
    xpC = sb("xpC", [128, 16], BF16)
    convw = sb("convw", [128, 16], F32)
    convb = sb("convb", [128, 16], F32)
    mrw = sb("mrw", [128, 64], BF16)
    mrbf = sb("mrbf", [1, 8], F32)
    mrbh = sb("mrbh", [1, 16], BF16)
    gwsb = sb("gwsb", [128, 64], BF16)
    gbf = sb("gbf", [1, 8], F32)
    r2bf = sb("r2bf", [1, 1], F32)
    r2bh = sb("r2bh", [1, 2], BF16)
    onesr = sb("onesr", [1, 512], BF16)
    gbh = sb("gbh", [1, 16], BF16)
    b1sb = sb("b1sb", [128, 256], F32)
    b2row = sb("b2row", [1, 2048], BF16)
    fbrow = sb("fbrow", [1, 1024], BF16)
    lnstg = GF0[0:1, 0:1024]
    lnsplit = SA[0:1, 0:2048]

    sT = sb("sT", [128, 16], F32)
    s12T = sb("s12T", [128, 16], F32)
    sTh = sb("sTh", [128, 16], BF16)
    sTm = sb("sTm", [128, 16], BF16)
    res16 = sb("res16", [128, 16], F32)
    srhi = sb("srhi", [1, 2048], BF16)
    srmid = sb("srmid", [1, 2048], BF16)
    srr = None
    rankT = sb("rankT", [128, 16], F32)
    maskT = sb("maskT", [128, 16], BF16)
    nmT = sb("nmT", [128, 16], BF16)
    mwT = sb("mwT", [128, 16], F32)
    eT = sb("eT", [128, 16], F32)
    erow = sb("erow", [128, 1], F32)
    erowb = sb("erowb", [128, 1], BF16)
    negm = sb("negm", [128, 1], F32)
    wz = sb("wz", [1, 4], F32)
    carry = sb("carry", [1, 48], F32)
    carryh = sb("carryh", [1, 32], BF16)
    cT = sb("cT", [128, 16], F32)
    ogat = sb("ogat", [128, 16], I32)
    osca = sb("osca", [128, 16], I32)
    oscf = sb("oscf", [128, 16], F32)
    stats = sb("stats", [128, 16], F32)
    zrow = sb("zrow", [1, 1024], BF16)
    zzr = sb("zzr", [1, 512], F32)
    zbs = sb("zbs", [128, 1024], BF16)
    silur = sb("silur", [128, 512], BF16)
    gatet = sb("gatet", [128, 64], F32)
    wgtt = sb("wgtt", [128, 128], F32)
    lgt = sb("lgt", [128, 32], F32)
    lgs = sb("lgs", [128, 192], F32)
    gtile = None

    gtile = zbs[:, 0:512]
    PS = [ctx.enter_context(nc.psum_tensor("PS%d" % i, [128, 512], F32)) for i in range(8)]

    sm = lambda name: Sem(ctx.enter_context(nc.semaphore(name)))
    dX = sm("dX")
    dG = sm("dG")
    dS = sm("dS")
    vv = sm("vv")
    ss = sm("ss")
    pp = sm("pp")
    sxr = SlotSem(sm, "sxr")
    sSP = sm("sSP")
    sSC = sm("sSC")
    sWe = SlotSem(sm, "sWe")
    sMI = SlotSem(sm, "sMI")
    sMO = SlotSem(sm, "sMO")
    sW1 = SlotSem(sm, "sW1")
    sW2 = SlotSem(sm, "sW2")
    sB2 = SlotSem(sm, "sB2")
    sGA = SlotSem(sm, "sGA")
    sXR = SlotSem(sm, "sXR")

    blk = [None]

    guards = {}

    def EN(which, fn):
        if which in ("vector", "scalar"):
            getattr(blk[0], which)(lambda eng: fn(guards.setdefault(which, Guard(eng))))
        else:
            getattr(blk[0], which)(fn)

    with nc.Block() as block:
      try:
        blk[0] = block

        def _pro_sync(q):
            dX.inc(q.dma_start(out=ones[:, :], in_=cONES[:, :]), 16)
            dX.inc(q.dma_start(out=onesc[:, :], in_=cONESC[:, :]), 16)
            dX.inc(q.dma_start(out=triLE[:, :], in_=cTRIle[:, :]), 16)
            dX.inc(q.dma_start(out=triST[:, :], in_=cTRIst[:, :]), 16)
            dX.inc(q.dma_start(out=idmat[:, :], in_=cID[:, :]), 16)
            dX.inc(q.dma_start(out=w1f.rearrange("p (c m) -> p c m", c=8), in_=w_r1.rearrange("(c p) m -> p c m", p=128)), 16)
            dX.inc(q.dma_start(out=w2f[:, :], in_=h_r2[:, :]), 16)
            dX.inc(q.dma_start(out=b1r[:, :], in_=h_r1b[:, :]), 16)
            dX.inc(q.dma_start(out=convw[:, :], in_=h_conv3[:, :]), 16)
            dX.inc(q.dma_start(out=convb[:, :], in_=h_convb[:, :]), 16)
            dX.inc(q.dma_start(out=mrbf[:, :], in_=b_mr.rearrange("(o e) -> o e", o=1)), 16)
            dX.inc(q.dma_start(out=r2bf[:, :], in_=b_r2.rearrange("(o e) -> o e", o=1)), 16)
            dX.inc(q.dma_start(out=gbf[:, :], in_=b_gate.rearrange("(o e) -> o e", o=1)), 16)
            dX.inc(q.dma_start(out=b1sb[:, :], in_=h_b1[:, :]), 16)
        EN("sync", _pro_sync)
        pro_dX0 = dX.n

        BCREG = [None]

        def _pro_g(q):
            BCREG[0] = q.to_reg(SEL - 1)
            q.memset(onesr[:, :], 1.0)
            dG.inc(q.dma_start(out=mrw[:, :], in_=h_mrw[:, :]), 16)
            dG.inc(q.dma_start(out=gwsb[:, :], in_=h_gw[:, :]), 16)
        EN("gpsimd", _pro_g)
        pro_dG0 = dG.n

        def _pro_v(v):
            v.wait_ge(dX.h, pro_dX0)
            v.wait_ge(dG.h, pro_dG0)
            import os as _os2
            if "V" in _os2.environ.get("PSKIP", ""):
                vv.inc(v.tensor_copy(out=w1hi[:, 0:8], in_=w1f[:, 0:8]), 1)
                return
            v.tensor_copy(out=w1hi[:, :], in_=w1f[:, :])
            v.tensor_tensor(out=w1f[:, :], in0=w1f[:, :], in1=w1hi[:, :], op=ALU.subtract)
            v.tensor_copy(out=w1lo[:, :], in_=w1f[:, :])
            v.tensor_copy(out=w2hl[:, 0:2], in_=w2f[:, :])
            v.tensor_tensor(out=w2f[:, :], in0=w2f[:, :], in1=w2hl[:, 0:2], op=ALU.subtract)
            v.tensor_copy(out=w2hl[:, 2:4], in_=w2f[:, :])
            v.tensor_copy(out=mrbh[:, 0:8], in_=mrbf[:, :])
            v.tensor_tensor(out=mrbf[:, :], in0=mrbf[:, :], in1=mrbh[:, 0:8], op=ALU.subtract)
            v.tensor_copy(out=mrbh[:, 8:16], in_=mrbf[:, :])
            v.tensor_copy(out=r2bh[:, 0:1], in_=r2bf[:, :])
            v.tensor_tensor(out=r2bf[:, :], in0=r2bf[:, :], in1=r2bh[:, 0:1], op=ALU.subtract)
            v.tensor_copy(out=r2bh[:, 1:2], in_=r2bf[:, :])
            v.tensor_copy(out=gbh[:, 0:8], in_=gbf[:, :])
            v.tensor_tensor(out=gbf[:, :], in0=gbf[:, :], in1=gbh[:, 0:8], op=ALU.subtract)
            vv.inc(v.tensor_copy(out=gbh[:, 8:16], in_=gbf[:, :]), 1)
        EN("vector", _pro_v)

        import os as _os
        _SKIP = _os.environ.get("PSKIP", "")
        for li, (src, dst) in enumerate([] if "L" in _SKIP else [(ln1w, lnw1B), (ln1b, lnb1B), (ln2w, lnw2B), (ln2b, lnb2B)]):
            def _ld(q, src=src, li=li):
                q.wait_ge(vv.h, vv.n)
                dX.inc(q.dma_start(out=lnstg[:, :], in_=src.rearrange("(o n) -> o n", o=1)), 16)
            EN("sync", _ld)
            ld = dX.n

            def _sp(v, ld=ld, ppn=pp.n):
                v.wait_ge(dX.h, ld)
                v.wait_ge(pp.h, ppn)
                v.tensor_copy(out=lnsplit[0:1, 0:1024], in_=lnstg[:, :])
                v.tensor_tensor(out=lnstg[:, :], in0=lnstg[:, :], in1=lnsplit[0:1, 0:1024], op=ALU.subtract)
                vv.inc(v.tensor_copy(out=lnsplit[0:1, 1024:2048], in_=lnstg[:, :]), 1)
            EN("vector", _sp)
            spm = vv.n

            for half in range(2):
                def _mm(t, half=half, spm=spm, ssn=ss.n):
                    t.wait_ge(vv.h, spm)
                    t.wait_ge(ss.h, ssn)
                    t.matmul(PS[6][:, 0:512], ones[:, :], lnsplit[0:1, 512 * half:512 * (half + 1)],
                             start=True, stop=False)
                    pp.inc(t.matmul(PS[6][:, 0:512], ones[:, :],
                                    lnsplit[0:1, 1024 + 512 * half:1024 + 512 * (half + 1)],
                                    start=False, stop=True), 1)
                EN("tensor", _mm)
                mmv = pp.n

                def _ev(s, half=half, dst=dst, mmv=mmv):
                    s.wait_ge(pp.h, mmv)
                    ss.inc(s.copy(out=dst[:, 512 * half:512 * (half + 1)], in_=PS[6][:, 0:512]), 1)
                EN("scalar", _ev)

        for c in ([] if "X" in _SKIP else range(16)):
            def _ld(q, c=c):
                q.wait_ge(vv.h, vv.n)
                dX.inc(q.dma_start(out=xpf[:, :], in_=w_xp[128 * c:128 * (c + 1), :]), 16)
            EN("sync", _ld)
            ld = dX.n

            def _pr(v, c=c, ld=ld):
                v.wait_ge(dX.h, ld)
                v.reduce_sum(stats[:, 0:1], xpf[:, 16:32], axis=AX)
                v.tensor_copy(out=xpB[:, c:c + 1], in_=stats[:, 0:1])
                vv.inc(v.tensor_copy(out=xpC[:, c:c + 1], in_=xpf[:, 32:33]), 1)
            EN("vector", _pr)

        PRO_V = vv.n
        CUT("prologue")
        PRO_S = ss.n

        hiT = lambda c: GB0[:, 2048 * c:2048 * (c + 1)]
        midT = lambda c: GB0[:, 16384 + 2048 * c:16384 + 2048 * (c + 1)]
        hisb = lambda tt: GB1[:, 1024 * tt:1024 * (tt + 1)]
        hrhi = lambda c: GB2[:, 2048 * c:2048 * (c + 1)]
        hrmid = lambda c: GB2[:, 4096 + 2048 * c:4096 + 2048 * (c + 1)]
        sjb = GF0[:, 0:2048]
        tbuf = GF0[:, 2048:4096]
        xtile = lambda i: GF0[:, 4096 + 1024 * (i % 2):4096 + 1024 * (i % 2) + 1024]
        hrr = GF0[:, 6144:6656]
        srr = GF0[0:1, 6656:7168]
        mscr = GF0[:, 7168:8192]
        hidv = lambda ra: hid[(ra % 2) * S:(ra % 2) * S + S, :]
        midv = lambda ra: midd[(ra % 2) * S:(ra % 2) * S + S, :]

        A_ctx = {}
        A_carry = {}

        def A_begin(ra, gtf, gA):
            A_ctx.clear()
            A_ctx.update(ra=ra, a_done=[], gtf=gtf, gA=gA,
                         xrow=xin[ra * S:(ra + 1) * S, :], hidr=hidv(ra), middr=midv(ra))

        def A_tile(tt):
            ad = A_ctx["a_done"]

            def _ldx(q, tt=tt):
                if tt >= 2:
                    q.wait_ge(vv.h, ad[tt - 2])
                elif A_ctx["gtf"] is not None:
                    q.wait_ge(dG.h, A_ctx["gtf"])
                sxr.inc(tt, q.dma_start(out=xtile(tt), in_=A_ctx["xrow"][128 * tt:128 * (tt + 1), :]), 16)
            EN("sync", _ldx)
            ld = sxr.val(tt)

            def _v1(v, tt=tt, ld=ld):
                v.wait_ge(sxr.sem(tt).h, ld)
                if tt == 0 and A_ctx["ra"] == 0:
                    v.wait_ge(ss.h, PRO_S)
                if tt == 0 and A_ctx["gA"] is not None:
                    v.wait_ge(pp.h, A_ctx["gA"])
                v.reduce_sum(stats[:, 0:1], xtile(tt), axis=AX)
                vv.inc(v.tensor_scalar(out=stats[:, 1:2], in0=stats[:, 0:1], scalar1=1.0 / D, scalar2=None, op0=ALU.mult), 1)
            EN("vector", _v1)
            m1 = vv.n

            def _s1(s, tt=tt, m1=m1):
                s.wait_ge(vv.h, m1)
                ss.inc(s.activation(out=SA[:, 0:1024], in_=xtile(tt), func=AF.Square,
                                    accum_out=stats[:, 2:3]), 1)
            EN("scalar", _s1)
            m2 = ss.n

            def _v2(v, tt=tt, m2=m2):
                v.wait_ge(ss.h, m2)
                v.tensor_scalar(out=stats[:, 3:4], in0=stats[:, 2:3], scalar1=1.0 / D, scalar2=None, op0=ALU.mult)
                v.tensor_tensor(out=stats[:, 4:5], in0=stats[:, 1:2], in1=stats[:, 1:2], op=ALU.mult)
                v.tensor_tensor(out=stats[:, 5:6], in0=stats[:, 3:4], in1=stats[:, 4:5], op=ALU.subtract)
                vv.inc(v.tensor_scalar(out=stats[:, 5:6], in0=stats[:, 5:6], scalar1=EPS, scalar2=None, op0=ALU.add), 1)
            EN("vector", _v2)
            m3 = vv.n

            def _s2(s, tt=tt, m3=m3):
                s.wait_ge(vv.h, m3)
                s.activation(out=stats[:, 6:7], in_=stats[:, 5:6], func=AF.Ln)
                ss.inc(s.activation(out=stats[:, 7:8], in_=stats[:, 6:7], func=AF.Exp, scale=-0.5), 1)
            EN("scalar", _s2)
            m4 = ss.n

            def _v3(v, tt=tt, m4=m4):
                v.wait_ge(ss.h, m4)
                v.tensor_tensor(out=stats[:, 8:9], in0=stats[:, 7:8], in1=stats[:, 7:8], op=ALU.mult)
                v.tensor_tensor(out=stats[:, 9:10], in0=stats[:, 8:9], in1=stats[:, 5:6], op=ALU.mult)
                v.tensor_scalar(out=stats[:, 9:10], in0=stats[:, 9:10], scalar1=-0.5, scalar2=1.5,
                                op0=ALU.mult, op1=ALU.add)
                v.tensor_tensor(out=stats[:, 10:11], in0=stats[:, 7:8], in1=stats[:, 9:10], op=ALU.mult)
                v.tensor_tensor(out=stats[:, 11:12], in0=stats[:, 1:2], in1=stats[:, 10:11], op=ALU.mult)
                v.tensor_scalar(out=mscr, in0=xtile(tt), scalar1=stats[:, 10:11], scalar2=stats[:, 11:12],
                                op0=ALU.mult, op1=ALU.subtract)
                vv.inc(v.tensor_copy(out=hisb(tt), in_=mscr), 1)
                v.tensor_tensor(out=mscr, in0=mscr, in1=hisb(tt), op=ALU.subtract)
                vv.inc(v.tensor_copy(out=SA[:, 1024:2048], in_=mscr), 1)
            EN("vector", _v3)
            m5 = vv.n

            def _spl(q, tt=tt, m5=m5):
                q.wait_ge(vv.h, m5)
                sSP.inc(q.dma_start(out=A_ctx["hidr"][128 * tt:128 * (tt + 1), :], in_=hisb(tt)), 16)
                sSP.inc(q.dma_start(out=A_ctx["middr"][128 * tt:128 * (tt + 1), :], in_=SA[:, 1024:2048]), 16)
            EN("sync", _spl)
            spl = sSP.n

            def _vw(v, spl=spl):
                v.wait_ge(sSP.h, spl)
                vv.inc(v.engine_nop(), 1)
            EN("vector", _vw)
            ad.append(vv.n)

        def emit_trl(ra, aSP, ppgate):
            def _trl(s):
                s.wait_ge(sSP.h, aSP)
                if ppgate:
                    s.wait_ge(pp.h, ppgate)
                for c in range(DC):
                    dS.inc(s.dma_start(out=hiT(c), in_=hidv(ra)[:, 128 * c:128 * (c + 1)], transpose=True), 16)
                    dS.inc(s.dma_start(out=midT(c), in_=midv(ra)[:, 128 * c:128 * (c + 1)], transpose=True), 16)
            EN("scalar", _trl)
            return dS.n

        for r in range(R):
            xrow = xin[r * S:(r + 1) * S, :]
            outrow = out[r * S:(r + 1) * S, :]

            if r == 0:
                A_begin(0, None, None)
                for tt in range(NT):
                    A_tile(tt)
                A_SP = sSP.n
            else:
                A_SP = A_carry["A_SP"]
            A_dX = dX.n
            CUT("A")

            if debug:
                def _dbh(q):
                    q.wait_ge(sSP.h, A_SP)
                    dX.inc(q.dma_start(out=dbg["hid"][r], in_=hidv(r)[:, :], allow_slow_non_contiguous=True), 16)
                    dX.inc(q.dma_start(out=dbg["mid"][r], in_=midv(r)[:, :]), 16)
                EN("sync", _dbh)

            if r == 0:
                B_dS = emit_trl(0, A_SP, 0)
            else:
                B_dS = A_carry["B_dS"]

            if debug:
                def _dbtr(q):
                    q.wait_ge(dS.h, B_dS)
                    if r == 0:
                        dX.inc(q.dma_start(out=dbg["w1s"][:, 0:2048], in_=w1hi[:, :]), 16)
                        dX.inc(q.dma_start(out=dbg["w1s"][:, 2048:4096], in_=w1lo[:, :]), 16)
                        for li2, t2 in enumerate([lnw1B, lnb1B, lnw2B, lnb2B]):
                            dX.inc(q.dma_start(out=dbg["lnB"][li2], in_=t2[:, :]), 16)
                    dX.inc(q.dma_start(out=dbg["hiT"][r], in_=GB0[:, 0:16384]), 16)
                    dX.inc(q.dma_start(out=dbg["midT"][r], in_=GB0[:, 16384:32768]), 16)
                EN("sync", _dbtr)
                dbtr = dX.n

                def _dbtrw(t, dbtr=dbtr):
                    t.wait_ge(dX.h, dbtr)
                EN("tensor", _dbtrw)
            CUT("trl")

            prodW = [(0, 0), (0, 1), (1, 0), (1, 1)]
            evhr = []
            for jm in range(8):
                jt, mc = jm // 2, jm % 2

                def _mm(t, jt=jt, mc=mc, jm=jm):
                    if jm == 0:
                        t.wait_ge(dS.h, B_dS)
                        t.wait_ge(vv.h, PRO_V)
                        if r > 0:
                            t.wait_ge(vv.h, rowjoin_v)
                    if jm >= 2:
                        t.wait_ge(vv.h, evhr[jm - 2])
                    bank = jm % 2
                    first = True
                    for k in range(DC):
                        for (wp, xp) in prodW:
                            w = w1hi if wp == 0 else w1lo
                            xs = hiT(k) if xp == 0 else midT(k)
                            last = (k == DC - 1 and wp == 1 and xp == 1)
                            mm = t.matmul(PS[bank][:, 0:512], w[:, 256 * k + 128 * mc:256 * k + 128 * (mc + 1)],
                                          xs[:, 512 * jt:512 * (jt + 1)], start=first, stop=last)
                            first = False
                    pp.inc(mm, 1)
                EN("tensor", _mm)
                mmv = pp.n
                import os as _os3
                _HRSKIP = _os3.environ.get("HRSKIP", "")

                def _ev(s, jt=jt, mc=mc, jm=jm, mmv=mmv):
                    s.wait_ge(pp.h, mmv)
                    if jm >= 1:
                        s.wait_ge(vv.h, vv.n)
                    if "E" in _HRSKIP:
                        ss.inc(s.copy(out=hrr, in_=PS[jm % 2][:, 0:512]), 1)
                    elif "B" in _HRSKIP:
                        ss.inc(s.activation(out=hrr, in_=PS[jm % 2][:, 0:512], func=AF.Relu), 1)
                    else:
                        ss.inc(s.activation(out=hrr, in_=PS[jm % 2][:, 0:512], func=AF.Relu,
                                            bias=b1r[:, mc:mc + 1]), 1)
                EN("scalar", _ev)
                evv = ss.n

                def _spv(v, jt=jt, mc=mc, evv=evv):
                    v.wait_ge(ss.h, evv)
                    if "S" in _HRSKIP:
                        vv.inc(v.tensor_copy(out=hrhi(mc)[:, 512 * jt:512 * (jt + 1)], in_=hrr), 1)
                        return
                    v.tensor_copy(out=hrhi(mc)[:, 512 * jt:512 * (jt + 1)], in_=hrr)
                    v.tensor_tensor(out=hrr, in0=hrr, in1=hrhi(mc)[:, 512 * jt:512 * (jt + 1)], op=ALU.subtract)
                    vv.inc(v.tensor_copy(out=hrmid(mc)[:, 512 * jt:512 * (jt + 1)], in_=hrr), 1)
                EN("vector", _spv)
                evhr.append(vv.n)
                CUT("hr%d" % jm)
            B_HR = vv.n
            CUT("hr")

            prodS = [(0, 0), (0, 1), (1, 0), (1, 1)]
            for jt in range(4):
                def _mm(t, jt=jt):
                    if jt == 0:
                        t.wait_ge(vv.h, B_HR)
                    else:
                        t.wait_ge(ss.h, ss.n)
                    first = True
                    for c in range(2):
                        for (wp, hp) in prodS:
                            hsrc = hrhi(c) if hp == 0 else hrmid(c)
                            t.matmul(PS[2][0:1, 0:512], w2hl[:, 2 * wp + c:2 * wp + c + 1],
                                     hsrc[:, 512 * jt:512 * (jt + 1)], start=first, stop=False)
                            first = False
                    t.matmul(PS[2][0:1, 0:512], r2bh[0:1, 0:1], onesr[0:1, 0:512],
                             start=False, stop=False)
                    mm = t.matmul(PS[2][0:1, 0:512], r2bh[0:1, 1:2], onesr[0:1, 0:512],
                                  start=False, stop=True)
                    pp.inc(mm, 1)
                EN("tensor", _mm)
                mmv = pp.n

                def _ev(s, jt=jt, mmv=mmv):
                    s.wait_ge(pp.h, mmv)
                    if jt > 0:
                        s.wait_ge(vv.h, vv.n)
                    ss.inc(s.copy(out=srr[:, :], in_=PS[2][0:1, 0:512]), 1)
                EN("scalar", _ev)
                evv = ss.n

                def _spv(v, jt=jt, evv=evv):
                    v.wait_ge(ss.h, evv)
                    v.tensor_copy(out=srhi[0:1, 512 * jt:512 * (jt + 1)], in_=srr)
                    v.tensor_tensor(out=srr, in0=srr, in1=srhi[0:1, 512 * jt:512 * (jt + 1)], op=ALU.subtract)
                    vv.inc(v.tensor_copy(out=srmid[0:1, 512 * jt:512 * (jt + 1)], in_=srr), 1)
                EN("vector", _spv)

            def _mmB(t):
                for tt in range(NT):
                    first = True
                    for c in range(2):
                        for (wp, hp) in prodS:
                            hsrc = hrhi(c) if hp == 0 else hrmid(c)
                            t.matmul(PS[3][:, tt:tt + 1], hsrc[:, 128 * tt:128 * (tt + 1)],
                                     w2hl[:, 2 * wp + c:2 * wp + c + 1], start=first, stop=False)
                            first = False
                    t.matmul(PS[3][:, tt:tt + 1], ones[:, :], r2bh[0:1, 0:1], start=False, stop=False)
                    mm = t.matmul(PS[3][:, tt:tt + 1], ones[:, :], r2bh[0:1, 1:2], start=False, stop=True)
                pp.inc(mm, 1)
            EN("tensor", _mmB)
            m_sT = pp.n

            def _evB(s):
                s.wait_ge(pp.h, m_sT)
                ss.inc(s.copy(out=sT[:, :], in_=PS[3][:, 0:16]), 1)
            EN("scalar", _evB)
            m_sTe = ss.n

            def _spB(v):
                v.wait_ge(ss.h, m_sTe)
                v.tensor_copy(out=sTh[:, :], in_=sT[:, :])
                v.tensor_tensor(out=res16[:, :], in0=sT[:, :], in1=sTh[:, :], op=ALU.subtract)
                v.tensor_copy(out=sTm[:, :], in_=res16[:, :])
                v.tensor_tensor(out=res16[:, :], in0=res16[:, :], in1=sTm[:, :], op=ALU.subtract)
                vv.inc(v.tensor_tensor(out=s12T[:, :], in0=sT[:, :], in1=res16[:, :], op=ALU.subtract), 1)
            EN("vector", _spB)
            B_S12 = vv.n
            CUT("scores")

            for jt in range(4):
                def _mm(t, jt=jt):
                    if jt == 0:
                        t.wait_ge(vv.h, B_S12)
                    else:
                        t.wait_ge(ss.h, ss.n)
                    t.matmul(PS[4][:, 0:512], ones[:, :], srhi[0:1, 512 * jt:512 * (jt + 1)], start=True, stop=False)
                    pp.inc(t.matmul(PS[4][:, 0:512], ones[:, :], srmid[0:1, 512 * jt:512 * (jt + 1)],
                                    start=False, stop=True), 1)
                EN("tensor", _mm)
                mmv = pp.n

                def _ev(s, jt=jt, mmv=mmv):
                    s.wait_ge(pp.h, mmv)
                    ss.inc(s.copy(out=sjb[:, 512 * jt:512 * (jt + 1)], in_=PS[4][:, 0:512]), 1)
                EN("scalar", _ev)
            B_SJB = ss.n

            if debug:
                def _dbhr(q):
                    q.wait_ge(ss.h, B_SJB)
                    dX.inc(q.dma_start(out=dbg["hrh"][r], in_=GB2[:, 0:4096]), 16)
                    dX.inc(q.dma_start(out=dbg["hrm"][r], in_=GB2[:, 4096:8192]), 16)
                    dX.inc(q.dma_start(out=dbg["srh"][r], in_=srhi[:, :]), 16)
                    dX.inc(q.dma_start(out=dbg["srm"][r], in_=srmid[:, :]), 16)
                    dX.inc(q.dma_start(out=dbg["sT"][r], in_=sT[:, :]), 16)
                EN("sync", _dbhr)

            def _rank(v):
                v.wait_ge(ss.h, B_SJB)
                for ic in range(NT):
                    v.tensor_scalar(out=tbuf, in0=sjb, scalar1=s12T[:, ic:ic + 1], scalar2=None, op0=ALU.subtract)
                    v.tensor_scalar(out=SA[:, 0:2048], in0=tbuf, scalar1=0.0, scalar2=0.0,
                                    op0=ALU.is_gt, op1=ALU.add, accum_out=rankT[:, ic:ic + 1])
                    if ic > 0:
                        v.tensor_scalar(out=SA[:, 0:128 * ic], in0=tbuf[:, 0:128 * ic], scalar1=0.0, scalar2=0.0,
                                        op0=ALU.is_equal, op1=ALU.add, accum_out=stats[:, 0:1])
                        v.tensor_tensor(out=rankT[:, ic:ic + 1], in0=rankT[:, ic:ic + 1], in1=stats[:, 0:1], op=ALU.add)
                    v.scalar_tensor_tensor(out=SA[:, 0:128], in0=tbuf[:, 128 * ic:128 * (ic + 1)], scalar=0.0,
                                           in1=triST[:, :], op0=ALU.is_equal, op1=ALU.mult, accum_out=stats[:, 1:2])
                    v.tensor_tensor(out=rankT[:, ic:ic + 1], in0=rankT[:, ic:ic + 1], in1=stats[:, 1:2], op=ALU.add)
                v.tensor_scalar(out=maskT[:, :], in0=rankT[:, :], scalar1=float(SEL), scalar2=None, op0=ALU.is_lt)
                vv.inc(v.tensor_scalar(out=nmT[:, :], in0=maskT[:, :], scalar1=-1.0, scalar2=1.0, op0=ALU.mult, op1=ALU.add), 1)
            EN("vector", _rank)
            B_RANK = vv.n
            CUT("rank")

            def _gmb(t):
                t.wait_ge(vv.h, B_RANK)
                pp.inc(t.matmul(PS[5][0:1, 16:32], onesc[:, :], maskT[:, :], start=True, stop=True), 1)
            EN("tensor", _gmb)
            m_gmb = pp.n

            def _neg(v):
                v.wait_ge(pp.h, m_gmb)
                vv.inc(v.reduce_max(negm[:, :], sjb, axis=AX, negate=True), 1)
                v.tensor_copy(out=carry[0:1, 0:16], in_=PS[5][0:1, 16:32])
                cur, oth = 0, 16
                for sh in (1, 2, 4, 8):
                    v.tensor_copy(out=carry[0:1, oth:oth + sh], in_=carry[0:1, cur:cur + sh])
                    v.tensor_tensor(out=carry[0:1, oth + sh:oth + 16], in0=carry[0:1, cur + sh:cur + 16],
                                    in1=carry[0:1, cur:cur + 16 - sh], op=ALU.add)
                    cur, oth = oth, cur
                v.memset(carry[0:1, oth:oth + 1], 0.0)
                v.tensor_copy(out=carry[0:1, oth + 1:oth + 16], in_=carry[0:1, cur:cur + 15])
                v.tensor_copy(out=carryh[0:1, 0:16], in_=carry[0:1, oth:oth + 16])
                v.tensor_tensor(out=carry[0:1, 32:48], in0=carry[0:1, oth:oth + 16], in1=carryh[0:1, 0:16],
                                op=ALU.subtract)
                vv.inc(v.tensor_copy(out=carryh[0:1, 16:32], in_=carry[0:1, 32:48]), 1)
            EN("vector", _neg)
            m_carry = vv.n

            def _exp(s):
                s.wait_ge(vv.h, m_carry)
                ss.inc(s.activation(out=eT[:, :], in_=sT[:, :], func=AF.Exp, bias=negm[:, :]), 1)
            EN("scalar", _exp)
            m_eT = ss.n

            def _em(v):
                v.wait_ge(ss.h, m_eT)
                v.tensor_tensor(out=eT[:, :], in0=eT[:, :], in1=maskT[:, :], op=ALU.mult)
                v.reduce_sum(erow[:, :], eT[:, :], axis=AX)
                vv.inc(v.tensor_copy(out=erowb[:, :], in_=erow[:, :]), 1)
            EN("vector", _em)
            m_er = vv.n

            def _zmm(t):
                t.wait_ge(vv.h, m_er)
                t.matmul(PS[5][0:1, 4:5], onesc[:, :], erowb[:, :], start=True, stop=True)
                t.matmul(PS[6][:, 0:16], triLE[:, :], maskT[:, :], start=True, stop=False)
                t.matmul(PS[6][:, 0:16], ones[:, :], carryh[0:1, 0:16], start=False, stop=False)
                pp.inc(t.matmul(PS[6][:, 0:16], ones[:, :], carryh[0:1, 16:32], start=False, stop=True), 1)
            EN("tensor", _zmm)
            m_Z = pp.n

            def _rz(v):
                v.wait_ge(pp.h, m_Z)
                v.tensor_copy(out=wz[0:1, 0:1], in_=PS[5][0:1, 4:5])
                v.reciprocal(out=wz[0:1, 1:2], in_=wz[0:1, 0:1])
                vv.inc(v.tensor_copy(out=SA[0:1, 256:257], in_=wz[0:1, 1:2]), 1)
                v.tensor_copy(out=cT[:, :], in_=PS[6][:, 0:16])
                v.tensor_scalar(out=res16[:, :], in0=cT[:, :], scalar1=1.0, scalar2=0.0, op0=ALU.subtract, op1=ALU.max)
                v.tensor_copy(out=ogat[:, :], in_=res16[:, :])
                v.tensor_scalar(out=res16[:, :], in0=cT[:, :], scalar1=1.0, scalar2=None, op0=ALU.subtract)
                v.tensor_tensor(out=res16[:, :], in0=res16[:, :], in1=maskT[:, :], op=ALU.mult)
                v.scalar_tensor_tensor(out=oscf[:, :], in0=nmT[:, :], scalar=4096.0, in1=res16[:, :],
                                       op0=ALU.mult, op1=ALU.add)
                vv.inc(v.tensor_copy(out=osca[:, :], in_=oscf[:, :]), 1)
            EN("vector", _rz)
            m_rzv = vv.n

            def _rzb(t):
                t.wait_ge(vv.h, m_rzv)
                pp.inc(t.matmul(PS[5][:, 2:3], ones[:, :], SA[0:1, 256:257], start=True, stop=True), 1)
            EN("tensor", _rzb)
            m_rzb = pp.n

            def _mw(v):
                v.wait_ge(pp.h, m_rzb)
                vv.inc(v.tensor_scalar(out=mwT[:, :], in0=eT[:, :], scalar1=PS[5][:, 2:3], scalar2=None,
                                       op0=ALU.mult), 1)
            EN("vector", _mw)
            C_OFF = vv.n
            CUT("C")

            if debug:
                def _dbc(q):
                    q.wait_ge(vv.h, C_OFF)
                    dX.inc(q.dma_start(out=dbg["scores"][r:r + 1, :].rearrange("o (c p) -> p (o c)", p=128), in_=sT[:, :], allow_slow_non_contiguous=True), 16)
                    dX.inc(q.dma_start(out=dbg["rank"][r:r + 1, :].rearrange("o (c p) -> p (o c)", p=128), in_=rankT[:, :], allow_slow_non_contiguous=True), 16)
                    dX.inc(q.dma_start(out=dbg["mw"][r:r + 1, :].rearrange("o (c p) -> p (o c)", p=128), in_=mwT[:, :], allow_slow_non_contiguous=True), 16)
                    dX.inc(q.dma_start(out=dbg["c"][r:r + 1, :].rearrange("o (c p) -> p (o c)", p=128), in_=cT[:, :], allow_slow_non_contiguous=True), 16)
                EN("sync", _dbc)

            def _sca(q):
                q.wait_ge(vv.h, C_OFF)
                for tt in range(NT):
                    sSC.inc(q.indirect_dma_start(
                        out=tokd[0:1, :], out_offset=bass.IndirectOffsetOnAxis(osca[:, tt:tt + 1], 0),
                        in_=hisb(tt), in_offset=None,
                        bounds_check=BCREG[0], oob_is_err=False), 16)
            EN("gpsimd", _sca)
            D_dG = sSC.n
            CUT("D-scatter")

            if debug:
                def _dbt(q):
                    q.wait_ge(sSC.h, D_dG)
                    dX.inc(q.dma_start(out=dbg["tok"][r], in_=tokd[:, :], allow_slow_non_contiguous=True), 16)
                EN("sync", _dbt)

            gate_l1 = []
            for tt in range(NT):
                par = tt % 4
                gps = PS[[5, 6, 4, 7][par]][:, 8:16]
                glt = lgt[:, 8 * par:8 * par + 8]
                gls = lambda a, b, par=par: lgs[:, 48 * par + a:48 * par + b]

                def _lmm(t, tt=tt, par=par, gps=gps):
                    if tt < 4:
                        t.wait_ge(vv.h, C_OFF)
                    else:
                        t.wait_ge(vv.h, gate_l1[tt - 4])
                    first = True
                    for k in range(DC):
                        t.matmul(gps, hiT(k)[:, 128 * tt:128 * (tt + 1)], gwsb[:, 8 * k:8 * (k + 1)],
                                 start=first, stop=False)
                        first = False
                    t.matmul(gps, ones[:, :], gbh[0:1, 0:8], start=False, stop=False)
                    pp.inc(t.matmul(gps, ones[:, :], gbh[0:1, 8:16], start=False, stop=True), 1)
                EN("tensor", _lmm)
                mmv = pp.n

                def _l1(v, tt=tt, mmv=mmv, gps=gps, glt=glt, gls=gls):
                    v.wait_ge(pp.h, mmv)
                    v.tensor_copy(out=glt, in_=gps)
                    vv.inc(v.reduce_max(gls(0, 1), glt, axis=AX, negate=True), 1)
                EN("vector", _l1)
                gate_l1.append(vv.n)
                m1 = vv.n

                def _l2(s, tt=tt, m1=m1, glt=glt, gls=gls):
                    s.wait_ge(vv.h, m1)
                    ss.inc(s.activation(out=gls(8, 16), in_=glt, func=AF.Exp, bias=gls(0, 1)), 1)
                EN("scalar", _l2)
                m2 = ss.n

                def _l3(v, tt=tt, m2=m2, gls=gls):
                    v.wait_ge(ss.h, m2)
                    v.reduce_sum(gls(1, 2), gls(8, 16), axis=AX)
                    v.reciprocal(out=gls(2, 3), in_=gls(1, 2))
                    v.tensor_scalar(out=gls(16, 24), in0=gls(8, 16), scalar1=gls(2, 3), scalar2=None, op0=ALU.mult)
                    v.tensor_scalar(out=gls(24, 32), in0=gls(16, 24), scalar1=0.1, scalar2=None, op0=ALU.is_ge)
                    v.tensor_tensor(out=gls(32, 40), in0=gls(16, 24), in1=gls(24, 32), op=ALU.mult)
                    v.reduce_sum(gls(3, 4), gls(32, 40), axis=AX)
                    v.reciprocal(out=gls(4, 5), in_=gls(3, 4))
                    vv.inc(v.tensor_scalar(out=wgtt[:, 8 * tt:8 * (tt + 1)], in0=gls(32, 40), scalar1=gls(4, 5),
                                           scalar2=None, op0=ALU.mult), 1)
                EN("vector", _l3)
            E_WGT = vv.n
            CUT("E-gate")

            def _ldt(q):
                q.wait_ge(sSC.h, D_dG)
                for st in range(NSEL):
                    dX.inc(q.dma_start(out=TOKB[:, 1024 * st:1024 * (st + 1)],
                                       in_=tokd[128 * st:128 * (st + 1), :]), 16)
            EN("sync", _ldt)
            F_TOK = dX.n

            def emit_f2_tile(st):
                def _n1a(v, st=st):
                    if st == 0:
                        v.wait_ge(dX.h, F_TOK)
                    tokt = TOKB[:, 1024 * st:1024 * (st + 1)]
                    v.reduce_sum(stats[:, 0:1], tokt, axis=AX)
                    vv.inc(v.tensor_scalar(out=stats[:, 1:2], in0=stats[:, 0:1], scalar1=1.0 / D, scalar2=None, op0=ALU.mult), 1)
                EN("vector", _n1a)
                m1 = vv.n

                def _n1b(s, st=st, m1=m1):
                    s.wait_ge(vv.h, m1)
                    ss.inc(s.activation(out=SA[:, 0:1024], in_=TOKB[:, 1024 * st:1024 * (st + 1)], func=AF.Square,
                                        accum_out=stats[:, 2:3]), 1)
                EN("scalar", _n1b)
                m2 = ss.n

                def _n1c(v, st=st, m2=m2):
                    v.wait_ge(ss.h, m2)
                    v.tensor_scalar(out=stats[:, 3:4], in0=stats[:, 2:3], scalar1=1.0 / D, scalar2=None, op0=ALU.mult)
                    v.tensor_tensor(out=stats[:, 4:5], in0=stats[:, 1:2], in1=stats[:, 1:2], op=ALU.mult)
                    v.tensor_tensor(out=stats[:, 5:6], in0=stats[:, 3:4], in1=stats[:, 4:5], op=ALU.subtract)
                    vv.inc(v.tensor_scalar(out=stats[:, 5:6], in0=stats[:, 5:6], scalar1=EPS, scalar2=None, op0=ALU.add), 1)
                EN("vector", _n1c)
                m3 = vv.n

                def _n1d_(s, st=st, m3=m3):
                    s.wait_ge(vv.h, m3)
                    s.activation(out=stats[:, 6:7], in_=stats[:, 5:6], func=AF.Ln)
                    ss.inc(s.activation(out=stats[:, 7:8], in_=stats[:, 6:7], func=AF.Exp, scale=-0.5), 1)
                EN("scalar", _n1d_)
                m4 = ss.n

                def _n1e(v, st=st, m4=m4):
                    v.wait_ge(ss.h, m4)
                    tokt = TOKB[:, 1024 * st:1024 * (st + 1)]
                    v.tensor_tensor(out=stats[:, 8:9], in0=stats[:, 1:2], in1=stats[:, 7:8], op=ALU.mult)
                    v.tensor_scalar(out=SA[:, 0:1024], in0=tokt, scalar1=stats[:, 7:8], scalar2=stats[:, 8:9],
                                    op0=ALU.mult, op1=ALU.subtract)
                    v.tensor_tensor(out=SA[:, 1024:2048], in0=SA[:, 0:1024], in1=lnw1B[:, :], op=ALU.mult)
                    vv.inc(v.tensor_tensor(out=SA[:, 1024:2048], in0=SA[:, 1024:2048], in1=lnb1B[:, :], op=ALU.add), 1)
                EN("vector", _n1e)
                m5 = vv.n

                def _n1s(q, st=st, m5=m5):
                    q.wait_ge(vv.h, m5)
                    dX.inc(q.dma_start(out=n1d[128 * st:128 * (st + 1), :], in_=SA[:, 1024:2048]), 16)
                EN("sync", _n1s)
                spl = dX.n

                def _n1w(v, spl=spl):
                    v.wait_ge(dX.h, spl)
                    vv.inc(v.engine_nop(), 1)
                EN("vector", _n1w)

            if debug:
                def _dbw(q):
                    q.wait_ge(vv.h, E_WGT)
                    dX.inc(q.dma_start(out=dbg["wgt"][r].rearrange("(tt p) e -> p tt e", p=128), in_=wgtt[:, :].rearrange("p (tt e) -> p tt e", tt=16), allow_slow_non_contiguous=True), 16)
                EN("sync", _dbw)

            WeV = lambda i: GB0[:, 16384 + 4096 * (i % 2):16384 + 4096 * (i % 2) + 4096]
            facc = lambda mt: GF0[:, 512 * mt:512 * (mt + 1)]
            ffn_mm = []
            ffn_ev = []
            ffn_sil = {}
            for ex in range(2 * E):
                nd, e = ex // E, ex % E

                def _lw(q, ex=ex, nd=nd, e=e):
                    if ex == 0:
                        q.wait_ge(pp.h, m_sT)
                    if len(ffn_mm) >= 2:
                        q.wait_ge(pp.h, ffn_mm[-2])
                    dG.inc(q.dma_start(out=WeV(ex).rearrange("p (c m) -> p c m", c=8),
                                       in_=w_ffn[e][:, 512 * nd:512 * (nd + 1)]
                                       .rearrange("(c p) m -> p c m", p=128)), 16)
                    dG.inc(q.dma_start(out=fbrow[0:1, 512 * (ex % 2):512 * (ex % 2) + 512],
                                       in_=b_ffn[e:e + 1, 512 * nd:512 * (nd + 1)]), 16)
                EN("gpsimd", _lw)
                lw = dG.n

                for mt in range(NT):
                    gi = ex * NT + mt

                    def _mm(t, ex=ex, nd=nd, e=e, mt=mt, gi=gi, lw=lw):
                        if mt == 0:
                            t.wait_ge(dG.h, lw)
                        bank = [0, 1, 2, 3, 4, 7][gi % 6]
                        if gi in (4, 5):
                            t.wait_ge(vv.h, gate_l1[-1])
                        if gi >= 6:
                            t.wait_ge(vv.h, ffn_ev[gi - 6])
                        t.matmul(PS[bank][:, 0:512], ones[:, :],
                                 fbrow[0:1, 512 * (ex % 2):512 * (ex % 2) + 512], start=True, stop=False)
                        for k in range(DC):
                            mm = t.matmul(PS[bank][:, 0:512], hiT(k)[:, 128 * mt:128 * (mt + 1)],
                                          WeV(ex)[:, 512 * k:512 * (k + 1)],
                                          start=False, stop=(k == DC - 1))
                        pp.inc(mm, 1)
                    EN("tensor", _mm)
                    if mt == NT - 1:
                        ffn_mm.append(pp.n)
                    mmv = pp.n

                    def _ev(v, ex=ex, e=e, mt=mt, gi=gi, mmv=mmv):
                        v.wait_ge(pp.h, mmv)
                        if ex == E and e == 0:
                            v.wait_ge(ss.h, ffn_sil[mt])
                        bank = [0, 1, 2, 3, 4, 7][gi % 6]
                        wsl = wgtt[:, 8 * mt + e:8 * mt + e + 1]
                        if e == 0:
                            vv.inc(v.tensor_scalar(out=facc(mt), in0=PS[bank][:, 0:512], scalar1=wsl,
                                                   scalar2=None, op0=ALU.mult), 1)
                        else:
                            vv.inc(v.scalar_tensor_tensor(out=facc(mt), in0=PS[bank][:, 0:512], scalar=wsl,
                                                          in1=facc(mt), op0=ALU.mult, op1=ALU.add), 1)
                    EN("vector", _ev)
                    ffn_ev.append(vv.n)

                if ex < NSEL:
                    emit_f2_tile(ex)

                if e == E - 1:
                    for mt in range(NT):
                        def _sg(s, mt=mt, nd=nd):
                            s.wait_ge(vv.h, ffn_ev[(nd * E + E - 1) * NT + mt])
                            if mt > 0:
                                s.wait_ge(dX.h, dX.n)
                            ss.inc(s.activation(out=gtile[:, :], in_=facc(mt), func=AF.Silu), 1)
                        EN("scalar", _sg)
                        m1 = ss.n
                        if nd == 0:
                            ffn_sil[mt] = m1

                        def _sp(q, mt=mt, nd=nd, m1=m1):
                            q.wait_ge(ss.h, m1)
                            dX.inc(q.dma_start(out=gd[128 * mt:128 * (mt + 1), 512 * nd:512 * (nd + 1)],
                                               in_=gtile[:, :]), 16)
                        EN("sync", _sp)

                if ex == NSEL - 1:
                    F_N1 = dX.n

                    def _n1t(s, F_N1=F_N1):
                        s.wait_ge(dX.h, F_N1)
                        s.wait_ge(pp.h, m_sT)
                        for c in range(DC):
                            dS.inc(s.dma_start(out=GB0[:, 24576 + 1024 * c:24576 + 1024 * (c + 1)],
                                               in_=n1d[:, 128 * c:128 * (c + 1)], transpose=True), 16)
                    EN("scalar", _n1t)
                    F_N1T = dS.n
            E_G = dX.n
            E_FFNMM = pp.n

            if debug:
                def _dbw3(q, m0=ffn_mm[0]):
                    q.wait_ge(pp.h, m0)
                    dX.inc(q.dma_start(out=dbg["wgt2"][r], in_=wgtt[:, :]), 16)
                    q.wait_ge(dX.h, E_G)
                    dX.inc(q.dma_start(out=dbg["wgt3"][r], in_=wgtt[:, :]), 16)
                EN("sync", _dbw3)
            CUT("E-experts")

            if debug:
                def _dbg2(q):
                    q.wait_ge(dX.h, E_G)
                    dX.inc(q.dma_start(out=dbg["g"][r], in_=gd[:, :], allow_slow_non_contiguous=True), 16)
                EN("sync", _dbg2)

            if debug:
                def _dbn1(q):
                    q.wait_ge(dS.h, F_N1T)
                    dX.inc(q.dma_start(out=dbg["n1"][r], in_=n1d[:, :], allow_slow_non_contiguous=True), 16)
                EN("sync", _dbn1)
            CUT("F2-n1")
            n1T = lambda c: GB0[:, 24576 + 1024 * c:24576 + 1024 * (c + 1)]

            minslot = lambda i: GB2[:, 2048 * (i % 2):2048 * (i % 2) + 2048]
            uch = lambda c: GB1[:, 1024 * c:1024 * (c + 1)]
            uv_ld = []
            uv_ev = []

            def emit_uv_block(Mg):
                def _lb(q, Mg=Mg):
                    if Mg == 0:
                        q.wait_ge(pp.h, m_sT)
                    if len(uv_ld) >= 2:
                        q.wait_ge(pp.h, uv_ld[-2][1])
                    dG.inc(q.dma_start(out=minslot(Mg).rearrange("p (c m) -> p c m", c=8),
                                       in_=w_m_in[:, 256 * Mg:256 * (Mg + 1)]
                                       .rearrange("(c p) m -> p c m", p=128)), 16)
                EN("gpsimd", _lb)
                lb = dG.n

                for sub in range(4):
                    m2_, rt = sub // 2, sub % 2
                    gi = Mg * 4 + sub
                    Mglob = Mg * 2 + m2_

                    def _mm(t, Mg=Mg, m2_=m2_, rt=rt, gi=gi, lb=lb):
                        if Mg == 0 and sub == 0:
                            t.wait_ge(dS.h, F_N1T)
                            t.wait_ge(vv.h, ffn_ev[-1])
                        t.wait_ge(dG.h, lb)
                        bank = gi % 4
                        if gi >= 4:
                            t.wait_ge(vv.h, uv_ev[gi - 4][0])
                            t.wait_ge(ss.h, uv_ev[gi - 4][1])
                        for k in range(DC):
                            mm = t.matmul(PS[bank][:, 0:512], minslot(Mg)[:, 256 * k + 128 * m2_:256 * k + 128 * (m2_ + 1)],
                                          n1T(k)[:, 512 * rt:512 * (rt + 1)], start=(k == 0), stop=(k == DC - 1))
                        pp.inc(mm, 1)
                    EN("tensor", _mm)
                    mmv = pp.n
                    if sub == 3:
                        uv_ld.append((Mg, mmv))

                    if Mglob < 16:
                        def _evu(v, Mglob=Mglob, rt=rt, gi=gi, mmv=mmv):
                            v.wait_ge(pp.h, mmv)
                            vv.inc(v.tensor_scalar(out=uch(Mglob)[:, 512 * rt:512 * (rt + 1)],
                                                   in0=PS[gi % 4][:, 0:512],
                                                   scalar1=convw[:, Mglob:Mglob + 1],
                                                   scalar2=convb[:, Mglob:Mglob + 1],
                                                   op0=ALU.mult, op1=ALU.add), 1)
                        EN("vector", _evu)
                        uv_ev.append((vv.n, ss.n))
                    else:
                        yc = Mglob - 16

                        def _evr(s, gi=gi, mmv=mmv):
                            s.wait_ge(pp.h, mmv)
                            s.wait_ge(vv.h, vv.n)
                            ss.inc(s.activation(out=silur[:, 0:512], in_=PS[gi % 4][:, 0:512], func=AF.Silu), 1)
                        EN("scalar", _evr)
                        mse = ss.n

                        def _evy(v, yc=yc, rt=rt, mse=mse):
                            v.wait_ge(ss.h, mse)
                            v.tensor_tensor(out=SA[:, 0:512], in0=silur[:, 0:512], in1=zbs[:, 512 * rt:512 * (rt + 1)],
                                            op=ALU.mult)
                            vv.inc(v.tensor_tensor(out=uch(yc)[:, 512 * rt:512 * (rt + 1)],
                                                   in0=SA[:, 0:512], in1=uch(yc)[:, 512 * rt:512 * (rt + 1)],
                                                   op=ALU.mult), 1)
                        EN("vector", _evy)
                        uv_ev.append((vv.n, ss.n))

            for Mg in range(8):
                emit_uv_block(Mg)
            F_U = vv.n
            CUT("F4-u")

            if debug:
                def _dbu(q):
                    q.wait_ge(vv.h, F_U)
                    for c2 in range(16):
                        dX.inc(q.dma_start(out=dbg["u"][r, 128 * c2:128 * (c2 + 1), :], in_=uch(c2), allow_slow_non_contiguous=True), 16)
                EN("sync", _dbu)

            for rt in range(2):
                def _zmm(t, rt=rt):
                    if rt == 0:
                        t.wait_ge(vv.h, F_U)
                    else:
                        t.wait_ge(ss.h, ss.n)
                    for k in range(16):
                        mm1 = t.matmul(PS[4][0:1, 0:512], xpB[:, k:k + 1], uch(k)[:, 512 * rt:512 * (rt + 1)],
                                       start=(k == 0), stop=(k == 15))
                    pp.inc(mm1, 1)
                    for k in range(16):
                        mm2 = t.matmul(PS[5][0:1, 0:512], xpC[:, k:k + 1], uch(k)[:, 512 * rt:512 * (rt + 1)],
                                       start=(k == 0), stop=(k == 15))
                    pp.inc(mm2, 1)
                EN("tensor", _zmm)
                mmv = pp.n

                def _zev(v, rt=rt, mmv=mmv):
                    v.wait_ge(pp.h, mmv)
                    v.tensor_copy(out=zzr[:, :], in_=PS[4][0:1, 0:512])
                    v.tensor_tensor(out=zzr[:, :], in0=zzr[:, :], in1=PS[5][0:1, 0:512], op=ALU.mult)
                    vv.inc(v.tensor_copy(out=zrow[0:1, 512 * rt:512 * (rt + 1)], in_=zzr[:, :]), 1)
                EN("vector", _zev)
                mze = vv.n

                def _zbc(t, rt=rt, mze=mze):
                    t.wait_ge(vv.h, mze)
                    pp.inc(t.matmul(PS[4][:, 0:512], ones[:, :], zrow[0:1, 512 * rt:512 * (rt + 1)], start=True, stop=True), 1)
                EN("tensor", _zbc)
                mzb = pp.n

                def _zse(s, rt=rt, mzb=mzb):
                    s.wait_ge(pp.h, mzb)
                    ss.inc(s.copy(out=zbs[:, 512 * rt:512 * (rt + 1)], in_=PS[4][:, 0:512]), 1)
                EN("scalar", _zse)

            for Mg in range(8, 16):
                emit_uv_block(Mg)
            F_Y = vv.n
            CUT("F5-y")

            if debug:
                def _dby(q):
                    q.wait_ge(vv.h, F_Y)
                    for c2 in range(16):
                        dX.inc(q.dma_start(out=dbg["y"][r, 128 * c2:128 * (c2 + 1), :], in_=uch(c2), allow_slow_non_contiguous=True), 16)
                EN("sync", _dby)

            mosl = lambda i: GB2[:, 4096 + 1024 * (i % 2):4096 + 1024 * (i % 2) + 1024]
            at_ld = []
            at_ev = []
            tok2t = lambda st: GF0[:, 1024 * st:1024 * (st + 1)]
            for it in range(4):
                nd, mrg = it // 2, it % 2
                for sg in range(8):
                    def _lmo(q, it=it, nd=nd, sg=sg):
                        gl = it * 8 + sg
                        if gl == 0:
                            q.wait_ge(vv.h, F_Y)
                        if len(at_ld) >= 2:
                            q.wait_ge(pp.h, at_ld[-2])
                        dG.inc(q.dma_start(out=mosl(gl).rearrange("p (c m) -> p c m", c=2),
                                           in_=w_mout[256 * sg:256 * (sg + 1), 512 * nd:512 * (nd + 1)]
                                           .rearrange("(c p) m -> p c m", p=128)), 16)
                    EN("gpsimd", _lmo)
                    lmo = dG.n

                    def _amm(t, it=it, nd=nd, mrg=mrg, sg=sg, lmo=lmo):
                        gl = it * 8 + sg
                        t.wait_ge(dG.h, lmo)
                        if gl == 0:
                            t.wait_ge(vv.h, F_Y)
                        if sg == 0:
                            for j in range(4):
                                g = 4 * it + j
                                if g >= 7:
                                    t.wait_ge(vv.h, at_ev[g - 7])
                        for kk in range(2):
                            kc = 2 * sg + kk
                            for j in range(4):
                                Mr = mrg * 4 + j
                                bank = (4 * it + j) % 7
                                mm = t.matmul(PS[bank][:, 0:512], uch(kc)[:, 128 * Mr:128 * (Mr + 1)],
                                              mosl(gl)[:, 512 * kk:512 * (kk + 1)],
                                              start=(sg == 0 and kk == 0), stop=(sg == 7 and kk == 1))
                        pp.inc(mm, 1)
                    EN("tensor", _amm)
                    at_ld.append(pp.n)
                mmv = pp.n

                for j in range(4):
                    def _aev(v, it=it, nd=nd, mrg=mrg, j=j, mmv=mmv):
                        v.wait_ge(pp.h, mmv)
                        if it == 0 and j == 0:
                            v.wait_ge(ss.h, ss.n)
                        Mr = mrg * 4 + j
                        bank = (4 * it + j) % 7
                        vv.inc(v.tensor_tensor(out=tok2t(Mr)[:, 512 * nd:512 * (nd + 1)],
                                               in0=TOKB[:, 1024 * Mr + 512 * nd:1024 * Mr + 512 * nd + 512],
                                               in1=PS[bank][:, 0:512], op=ALU.add), 1)
                        at_ev.append(vv.n)
                    EN("vector", _aev)
            F_T2 = vv.n
            CUT("F6-attn")

            if debug:
                def _dbt2(q):
                    q.wait_ge(vv.h, F_T2)
                    for st in range(NSEL):
                        dX.inc(q.dma_start(out=dbg["tok2"][r, 128 * st:128 * (st + 1), :], in_=tok2t(st), allow_slow_non_contiguous=True), 16)
                EN("sync", _dbt2)

            f7s = lambda st, a, b: lgs[:, 12 * st + a:12 * st + b]
            f7v0 = vv.n

            def _n2a(v):
                for st in range(NSEL):
                    v.reduce_sum(f7s(st, 0, 1), tok2t(st), axis=AX)
                    vv.inc(v.tensor_scalar(out=f7s(st, 1, 2), in0=f7s(st, 0, 1), scalar1=1.0 / D,
                                           scalar2=None, op0=ALU.mult), 1)
            EN("vector", _n2a)
            f7a = [f7v0 + st + 1 for st in range(NSEL)]

            def _n2b(s):
                for st in range(NSEL):
                    s.wait_ge(vv.h, f7a[st])
                    ss.inc(s.activation(out=SA[:, 0:1024], in_=tok2t(st), func=AF.Square,
                                        accum_out=f7s(st, 2, 3)), 1)
            EN("scalar", _n2b)
            f7b = [ss.n - NSEL + st + 1 for st in range(NSEL)]

            def _n2c(v):
                for st in range(NSEL):
                    v.wait_ge(ss.h, f7b[st])
                    v.tensor_scalar(out=f7s(st, 3, 4), in0=f7s(st, 2, 3), scalar1=1.0 / D, scalar2=None, op0=ALU.mult)
                    v.tensor_tensor(out=f7s(st, 4, 5), in0=f7s(st, 1, 2), in1=f7s(st, 1, 2), op=ALU.mult)
                    v.tensor_tensor(out=f7s(st, 5, 6), in0=f7s(st, 3, 4), in1=f7s(st, 4, 5), op=ALU.subtract)
                    vv.inc(v.tensor_scalar(out=f7s(st, 5, 6), in0=f7s(st, 5, 6), scalar1=EPS, scalar2=None, op0=ALU.add), 1)
            EN("vector", _n2c)
            f7c = [vv.n - NSEL + st + 1 for st in range(NSEL)]

            def _n2d(s):
                for st in range(NSEL):
                    s.wait_ge(vv.h, f7c[st])
                    s.activation(out=f7s(st, 6, 7), in_=f7s(st, 5, 6), func=AF.Ln)
                    ss.inc(s.activation(out=f7s(st, 7, 8), in_=f7s(st, 6, 7), func=AF.Exp, scale=-0.5), 1)
            EN("scalar", _n2d)
            f7d = [ss.n - NSEL + st + 1 for st in range(NSEL)]

            f7spl = []
            for st in range(NSEL):
                def _n2e(v, st=st):
                    v.wait_ge(ss.h, f7d[st])
                    if st >= 2:
                        v.wait_ge(dX.h, f7spl[st - 2])
                    sa = SA[:, 1024 * (st % 2):1024 * (st % 2) + 1024]
                    v.tensor_tensor(out=f7s(st, 8, 9), in0=f7s(st, 1, 2), in1=f7s(st, 7, 8), op=ALU.mult)
                    v.tensor_scalar(out=sa, in0=tok2t(st), scalar1=f7s(st, 7, 8), scalar2=f7s(st, 8, 9),
                                    op0=ALU.mult, op1=ALU.subtract)
                    v.tensor_tensor(out=sa, in0=sa, in1=lnw2B[:, :], op=ALU.mult)
                    vv.inc(v.tensor_tensor(out=sa, in0=sa, in1=lnb2B[:, :], op=ALU.add), 1)
                EN("vector", _n2e)
                m5 = vv.n

                def _n2s(q, st=st, m5=m5):
                    q.wait_ge(vv.h, m5)
                    dX.inc(q.dma_start(out=n2d[128 * st:128 * (st + 1), :],
                                       in_=SA[:, 1024 * (st % 2):1024 * (st % 2) + 1024]), 16)
                EN("sync", _n2s)
                f7spl.append(dX.n)
            F_N2 = dX.n

            if debug:
                def _dbn2(q):
                    q.wait_ge(dX.h, F_N2)
                    dX.inc(q.dma_start(out=dbg["n2"][r], in_=n2d[:, :], allow_slow_non_contiguous=True), 16)
                EN("sync", _dbn2)

            def _n2t(s):
                s.wait_ge(dX.h, F_N2)
                for c in range(DC):
                    dS.inc(s.dma_start(out=TOKB[:, 1024 * c:1024 * (c + 1)], in_=n2d[:, 128 * c:128 * (c + 1)],
                                       transpose=True), 16)
            EN("scalar", _n2t)
            F_N2T = dS.n
            CUT("F7-n2")
            n2T = lambda c: TOKB[:, 1024 * c:1024 * (c + 1)]

            rout_l1 = []
            f8v0 = vv.n
            f8s0 = ss.n
            for st in range(NSEL):
                par = st % 4
                gps = PS[[5, 6, 4, 7][par]][:, 8:16]
                glt = lgt[:, 8 * par:8 * par + 8]
                gls = lambda a, b, par=par: lgs[:, 48 * par + a:48 * par + b]

                def _rmm(t, st=st, gps=gps):
                    if st < 4:
                        t.wait_ge(dS.h, F_N2T)
                        t.wait_ge(vv.h, f8v0)
                        t.wait_ge(ss.h, f8s0)
                    else:
                        t.wait_ge(vv.h, rout_l1[st - 4])
                    for k in range(DC):
                        t.matmul(gps, n2T(k)[:, 128 * st:128 * (st + 1)], mrw[:, 8 * k:8 * (k + 1)],
                                 start=(k == 0), stop=False)
                    t.matmul(gps, ones[:, :], mrbh[0:1, 0:8], start=False, stop=False)
                    pp.inc(t.matmul(gps, ones[:, :], mrbh[0:1, 8:16], start=False, stop=True), 1)
                EN("tensor", _rmm)
                mmv = pp.n

                def _r1(v, st=st, mmv=mmv, gps=gps, glt=glt, gls=gls):
                    v.wait_ge(pp.h, mmv)
                    v.tensor_copy(out=glt, in_=gps)
                    v.max(out=gls(8, 16), in_=glt)
                    vv.inc(v.tensor_scalar(out=gls(0, 1), in0=gls(8, 9), scalar1=-1.0, scalar2=None,
                                           op0=ALU.mult), 1)
                EN("vector", _r1)
                rout_l1.append(vv.n)
                m1 = vv.n

                def _r2(s, st=st, m1=m1, glt=glt, gls=gls):
                    s.wait_ge(vv.h, m1)
                    ss.inc(s.activation(out=gls(16, 24), in_=glt, func=AF.Exp, bias=gls(0, 1)), 1)
                EN("scalar", _r2)
                m2 = ss.n

                def _r3(v, st=st, m2=m2, glt=glt, gls=gls):
                    v.wait_ge(ss.h, m2)
                    v.tensor_scalar(out=gls(24, 32), in0=glt, scalar1=gls(9, 10), scalar2=None, op0=ALU.is_ge)
                    v.tensor_tensor(out=gls(32, 40), in0=gls(16, 24), in1=gls(24, 32), op=ALU.mult)
                    v.reduce_sum(gls(1, 2), gls(32, 40), axis=AX)
                    v.reciprocal(out=gls(2, 3), in_=gls(1, 2))
                    vv.inc(v.tensor_scalar(out=gatet[:, 8 * st:8 * (st + 1)], in0=gls(32, 40), scalar1=gls(2, 3),
                                           scalar2=None, op0=ALU.mult), 1)
                EN("vector", _r3)
            F_GATE = vv.n
            CUT("F8-router")

            if debug:
                def _dbgt(q):
                    q.wait_ge(vv.h, F_GATE)
                    dX.inc(q.dma_start(out=dbg["gate"][r].rearrange("(st p) e -> p st e", p=128), in_=gatet[:, :].rearrange("p (st e) -> p st e", st=8), allow_slow_non_contiguous=True), 16)
                EN("sync", _dbgt)

            w1sl = lambda i: GB2[:, 2048 * (i % 2):2048 * (i % 2) + 2048]
            w2sl = lambda i: GB2[:, 4096 + 2048 * (i % 2):4096 + 2048 * (i % 2) + 2048]
            h1g = lambda g, kc: GB0[:, 16384 * g + 512 * kc:16384 * g + 512 * (kc + 1)]
            moe_w1ld = []
            moe_h1ev = []
            moe_eomm = []
            moe_eoev = []
            moe_b2rdy = []
            eo_prev_base = None
            for e in range(E):
                def _lb2(q, e=e):
                    if e >= 2:
                        q.wait_ge(pp.h, moe_b2rdy[e - 2])
                    dG.inc(q.dma_start(out=b2row[0:1, 1024 * (e % 2):1024 * (e % 2) + 1024],
                                       in_=b2_moe[e:e + 1, :]), 16)
                EN("gpsimd", _lb2)
                b2ld = dG.n

                for Mg in range(16):
                    def _lw1(q, e=e, Mg=Mg):
                        gl = e * 16 + Mg
                        if gl == 0:
                            q.wait_ge(pp.h, at_ld[-1])
                        if len(moe_w1ld) >= 2:
                            q.wait_ge(pp.h, moe_w1ld[-2])
                        dG.inc(q.dma_start(out=w1sl(gl).rearrange("p (c m) -> p c m", c=8),
                                           in_=w1_moe[e][:, 256 * Mg:256 * (Mg + 1)]
                                           .rearrange("(c p) m -> p c m", p=128)), 16)
                    EN("gpsimd", _lw1)
                    lw1 = dG.n

                    for sub in range(4):
                        g, m2_ = sub // 2, sub % 2
                        gi = (e * 16 + Mg) * 4 + sub
                        Mh = 2 * Mg + m2_

                        def _hmm(t, e=e, Mg=Mg, g=g, m2_=m2_, gi=gi, lw1=lw1, sub=sub):
                            if sub == 0:
                                t.wait_ge(dG.h, lw1)
                            if gi == 0:
                                t.wait_ge(vv.h, F_GATE)
                            if e > 0 and Mg == 0 and sub == 0:
                                t.wait_ge(vv.h, eo_prev_base + 4)
                            bank = gi % 4
                            if gi >= 4:
                                t.wait_ge(ss.h, moe_h1ev[gi - 4])
                            gl = e * 16 + Mg
                            for k in range(DC):
                                mm = t.matmul(PS[bank][:, 0:512],
                                              w1sl(gl)[:, 256 * k + 128 * m2_:256 * k + 128 * (m2_ + 1)],
                                              n2T(k)[:, 512 * g:512 * (g + 1)], start=(k == 0), stop=(k == DC - 1))
                            pp.inc(mm, 1)
                        EN("tensor", _hmm)
                        mmv = pp.n
                        if sub == 3:
                            moe_w1ld.append(mmv)

                        def _hev(s, e=e, g=g, Mh=Mh, gi=gi, mmv=mmv):
                            s.wait_ge(pp.h, mmv)
                            if gi == 0:
                                s.wait_ge(dS.h, F_N2T)
                                s.wait_ge(pp.h, E_FFNMM)
                            ss.inc(s.activation(out=h1g(g, Mh), in_=PS[gi % 4][:, 0:512],
                                                func=AF.Gelu, bias=b1sb[:, 32 * e + Mh:32 * e + Mh + 1]), 1)
                        EN("scalar", _hev)
                        moe_h1ev.append(ss.n)
                H1_DONE = ss.n

                for g in range(2):
                    for j in range(16):
                        glw = len(moe_eomm)

                        def _lw2(q, e=e, j=j, glw=glw):
                            if glw == 0:
                                q.wait_ge(pp.h, at_ld[-1])
                            if glw >= 2:
                                q.wait_ge(pp.h, moe_eomm[glw - 2])
                            sW2.inc(glw, q.dma_start(out=w2sl(glw).rearrange("p (c m) -> p c m", c=2),
                                                     in_=w2_moe[e][256 * j:256 * (j + 1), :]
                                                     .rearrange("(c p) m -> p c m", p=128)), 16)
                        EN("gpsimd", _lw2)
                        lw2 = (glw, sW2.val(glw))

                        def _eomm(t, e=e, g=g, j=j, glw=glw, lw2=lw2, b2ld=b2ld, prev=eo_prev_base):
                            t.wait_ge(sW2.sem(lw2[0]).h, lw2[1])
                            if j == 0:
                                t.wait_ge(ss.h, H1_DONE)
                                if prev is None:
                                    t.wait_ge(vv.h, F_GATE)
                                if g == 0:
                                    t.wait_ge(dG.h, b2ld)
                            for c in range(2):
                                kc = 2 * j + c
                                for Mr in range(4):
                                    for dh in range(2):
                                        bank = 2 * Mr + dh
                                        pv = PS[bank][:, 0:512]
                                        if j == 0 and c == 0:
                                            if prev is not None:
                                                t.wait_ge(vv.h, prev + bank + 1)
                                            t.matmul(pv, ones[:, :],
                                                     b2row[0:1, 1024 * (e % 2) + 512 * dh:1024 * (e % 2) + 512 * dh + 512],
                                                     start=True, stop=False)
                                        mm = t.matmul(pv, h1g(g, kc)[:, 128 * Mr:128 * (Mr + 1)],
                                                      w2sl(glw)[:, 1024 * c + 512 * dh:1024 * c + 512 * (dh + 1)],
                                                      start=False, stop=(j == 15 and c == 1))
                            pp.inc(mm, 1)
                        EN("tensor", _eomm)
                        moe_eomm.append(pp.n)
                    mmv = pp.n

                    eo_prev_base = vv.n
                    for Mr in range(4):
                        for dh in range(2):
                            def _eoev(v, e=e, g=g, Mr=Mr, dh=dh, mmv=mmv):
                                v.wait_ge(pp.h, mmv)
                                bank = 2 * Mr + dh
                                st = 4 * g + Mr
                                acc = GF0[:, 1024 * st + 512 * dh:1024 * st + 512 * dh + 512]
                                vv.inc(v.scalar_tensor_tensor(out=acc, in0=PS[bank][:, 0:512],
                                                              scalar=gatet[:, 8 * st + e:8 * st + e + 1],
                                                              in1=acc, op0=ALU.mult, op1=ALU.add), 1)
                            EN("vector", _eoev)
                    moe_eoev.append(vv.n)
                moe_b2rdy.append(pp.n)
            F_MOE = vv.n
            CUT("F9-moe")

            if debug:
                def _dbt3(q):
                    q.wait_ge(vv.h, F_MOE)
                    for st in range(NSEL):
                        dX.inc(q.dma_start(out=dbg["tok3"][r, 128 * st:128 * (st + 1), :], in_=tok2t(st), allow_slow_non_contiguous=True), 16)
                EN("sync", _dbt3)

            def _tfs(q):
                q.wait_ge(vv.h, F_MOE)
                dG.inc(q.dma_start(out=tokfd.rearrange("(st p) d -> p st d", p=128), in_=GF0[:, :].rearrange("p (st d) -> p st d", st=8)), 16)
            EN("gpsimd", _tfs)
            G_TF = dG.n

            ssel = lambda i: GB2[:, 1024 * (i % 2):1024 * (i % 2) + 1024]
            hirl = lambda i: GB2[:, 2048 + 1024 * (i % 2):2048 + 1024 * (i % 2) + 1024]
            grl = lambda i: GB2[:, 4096 + 1024 * (i % 2):4096 + 1024 * (i % 2) + 1024]
            xrl = lambda i: GF0[:, 1024 * (i % 2):1024 * (i % 2) + 1024]
            orl = lambda i: GF0[:, 2048 + 1024 * (i % 2):2048 + 1024 * (i % 2) + 1024]
            cp_ev = []
            if r + 1 < R:
                A_begin(r + 1, G_TF, at_ld[-1])
            for tt in range(NT):
                def _gsl(q, tt=tt):
                    if tt == 0:
                        q.wait_ge(dG.h, G_TF)
                        q.wait_ge(pp.h, moe_b2rdy[-1])
                    if tt >= 2:
                        q.wait_ge(vv.h, cp_ev[tt - 2])
                    dG.inc(q.indirect_dma_start(
                        out=ssel(tt), out_offset=None,
                        in_=tokfd[0:1, :], in_offset=bass.IndirectOffsetOnAxis(ogat[:, tt:tt + 1], 0)), 16)
                EN("gpsimd", _gsl)
                gsl = dG.n

                def _xrl(q, tt=tt):
                    if tt == 0:
                        q.wait_ge(dG.h, G_TF)
                        q.wait_ge(dX.h, E_G)
                    if tt >= 2:
                        q.wait_ge(vv.h, cp_ev[tt - 2])
                    dX.inc(q.dma_start(out=xrl(tt), in_=xrow[128 * tt:128 * (tt + 1), :]), 16)
                    dX.inc(q.dma_start(out=hirl(tt), in_=hidv(r)[128 * tt:128 * (tt + 1), :]), 16)
                    dX.inc(q.dma_start(out=grl(tt), in_=gd[128 * tt:128 * (tt + 1), :]), 16)
                EN("sync", _xrl)
                xld = dX.n

                def _cmp(v, tt=tt, gsl=gsl, xld=xld):
                    v.wait_ge(dG.h, gsl)
                    v.wait_ge(dX.h, xld)
                    v.tensor_scalar(out=ssel(tt), in0=ssel(tt), scalar1=mwT[:, tt:tt + 1], scalar2=None,
                                    op0=ALU.mult)
                    v.scalar_tensor_tensor(out=ssel(tt), in0=hirl(tt), scalar=nmT[:, tt:tt + 1],
                                           in1=ssel(tt), op0=ALU.mult, op1=ALU.add)
                    v.tensor_tensor(out=ssel(tt), in0=ssel(tt), in1=grl(tt), op=ALU.mult)
                    vv.inc(v.tensor_tensor(out=orl(tt), in0=xrl(tt), in1=ssel(tt), op=ALU.add), 1)
                EN("vector", _cmp)
                cpm = vv.n

                def _ost(q, tt=tt, cpm=cpm):
                    q.wait_ge(vv.h, cpm)
                    dX.inc(q.dma_start(out=outrow[128 * tt:128 * (tt + 1), :], in_=orl(tt)), 16)
                EN("sync", _ost)
                ostm = dX.n

                def _cpw(v, ostm=ostm):
                    v.wait_ge(dX.h, ostm)
                    vv.inc(v.engine_nop(), 1)
                EN("vector", _cpw)
                cp_ev.append(vv.n)
                if r + 1 < R:
                    A_tile(tt)

            if r + 1 < R:
                A_carry["A_SP"] = sSP.n
                A_carry["B_dS"] = emit_trl(r + 1, sSP.n, moe_b2rdy[-1])

            jx, jg, js2, jv, jsc, jp = dX.n, dG.n, dS.n, vv.n, ss.n, pp.n

            def _j(q):
                q.wait_ge(dX.h, jx); q.wait_ge(dG.h, jg); q.wait_ge(dS.h, js2)
                q.wait_ge(vv.h, jv); q.wait_ge(ss.h, jsc); q.wait_ge(pp.h, jp)
            EN("sync", _j)

            def _jg(q):
                q.wait_ge(dX.h, jx); q.wait_ge(dG.h, jg); q.wait_ge(dS.h, js2)
                q.wait_ge(vv.h, jv); q.wait_ge(ss.h, jsc); q.wait_ge(pp.h, jp)
            EN("gpsimd", _jg)

            def _jv(v):
                v.wait_ge(dX.h, jx); v.wait_ge(dG.h, jg); v.wait_ge(dS.h, js2)
                v.wait_ge(ss.h, jsc); v.wait_ge(pp.h, jp)
                vv.inc(v.engine_nop(), 1)
            EN("vector", _jv)
            rowjoin_v = vv.n
            CUT("G-compose")

            def _js(sx):
                sx.wait_ge(dX.h, jx); sx.wait_ge(dG.h, jg); sx.wait_ge(dS.h, js2)
                sx.wait_ge(vv.h, rowjoin_v); sx.wait_ge(pp.h, jp)
            EN("scalar", _js)

            def _jt(t):
                t.wait_ge(dX.h, jx); t.wait_ge(dG.h, jg); t.wait_ge(dS.h, js2)
                t.wait_ge(vv.h, rowjoin_v); t.wait_ge(ss.h, jsc)
            EN("tensor", _jt)

      except StopEmission as e:
        print("EMISSION CUT AT:", e)
      finally:
        jx, jg, js2, jv, jsc, jp = dX.n, dG.n, dS.n, vv.n, ss.n, pp.n

        def _fj(q):
            q.wait_ge(dX.h, jx); q.wait_ge(dG.h, jg); q.wait_ge(dS.h, js2)
            q.wait_ge(vv.h, jv); q.wait_ge(ss.h, jsc); q.wait_ge(pp.h, jp)
            q.wait_ge(sSP.h, sSP.n); q.wait_ge(sSC.h, sSC.n)
            for _sl in (sxr, sW2):
                for _s in _sl.s:
                    q.wait_ge(_s.h, _s.n)
        EN("sync", _fj)
    ctx.close()
    return nc, dbg



def _consts():
    q = np.arange(128)
    return {
        "cONES": np.ones((1, 128), dtype=ml_dtypes.bfloat16),
        "cONESC": np.ones((128, 1), dtype=ml_dtypes.bfloat16),
        "cTRIle": (q[:, None] <= q[None, :]).astype(ml_dtypes.bfloat16),
        "cTRIst": (q[None, :] < q[:, None]).astype(ml_dtypes.bfloat16),
        "cID": np.eye(128, dtype=ml_dtypes.bfloat16),
    }


_NC_CACHE = {}


def _get_nc(debug=False):
    if debug not in _NC_CACHE:
        _NC_CACHE[debug] = build_nc(debug)
    return _NC_CACHE[debug]


def kernel(**inputs):
    nc, _ = _get_nc(False)
    consts = _consts()
    x = np.ascontiguousarray(np.asarray(inputs["L0_attn_x"], dtype=np.float32))
    B = x.shape[0]
    percore = B // 8
    wmap = {
        "m_in_w": "L0_attn_m_in_w", "m_conv_w": "L0_attn_m_conv_w", "m_conv_b": "L0_attn_m_conv_b",
        "m_xproj_w": "L0_attn_m_xproj_w", "m_out_w": "L0_attn_m_out_w",
        "modr_w1": "L0_attn_modr_w1", "modr_b1": "L0_attn_modr_b1", "modr_w2": "L0_attn_modr_w2",
        "modr_b2": "L0_attn_modr_b2",
        "ln1_w": "L0_attn_ln1_w", "ln1_b": "L0_attn_ln1_b", "ln2_w": "L0_attn_ln2_w", "ln2_b": "L0_attn_ln2_b",
        "moe_r_w": "L0_attn_moe_r_w", "moe_r_b": "L0_attn_moe_r_b",
        "moe_w1": "L0_attn_moe_w1", "moe_b1": "L0_attn_moe_b1",
        "moe_w2": "L0_attn_moe_w2", "moe_b2": "L0_attn_moe_b2",
        "gate_w": "L0_ffn_gate_w", "gate_b": "L0_ffn_gate_b",
        "expert_w": "L0_ffn_expert_w", "expert_b": "L0_ffn_expert_b",
    }
    base = {k: np.ascontiguousarray(np.asarray(inputs[v], dtype=np.float32)) for k, v in wmap.items()}
    f32c = lambda a_: np.ascontiguousarray(np.asarray(a_, dtype=np.float32))
    base["h_conv3"] = f32c(base.pop("m_conv_w")[:, 3].reshape(16, 128).T)
    base["h_convb"] = f32c(base.pop("m_conv_b").reshape(16, 128).T)
    base["h_r1b"] = f32c(base.pop("modr_b1").reshape(2, 128).T)
    base["h_r2"] = f32c(base.pop("modr_w2").reshape(2, 128).T)
    base["h_b1"] = f32c(base.pop("moe_b1").reshape(8, 32, 128).transpose(2, 0, 1).reshape(128, 256))
    base["h_mrw"] = f32c(base.pop("moe_r_w").reshape(8, 128, 8).transpose(1, 0, 2).reshape(128, 64))
    base["h_gw"] = f32c(base.pop("gate_w").reshape(8, 128, 8).transpose(1, 0, 2).reshape(128, 64))
    base.update(consts)
    in_maps = []
    for c in range(8):
        m = dict(base)
        m["xin"] = np.ascontiguousarray(x[c * percore:(c + 1) * percore].reshape(percore * S, D))
        in_maps.append(m)
    res = run_bass_kernel_spmd(nc, in_maps, core_ids=list(range(8)))
    outs = [res.results[c]["out"].reshape(percore, S, D) for c in range(8)]
    return np.concatenate(outs, axis=0)


# version 10
# speedup vs baseline: 1.1255x; 1.1255x over previous
import numpy as np
import ml_dtypes
import concourse.bass as bass
import concourse.mybir as mybir
from concourse.bass_utils import run_bass_kernel_spmd

F32 = mybir.dt.float32
BF16 = mybir.dt.bfloat16
I32 = mybir.dt.int32
AF = mybir.ActivationFunctionType
ALU = mybir.AluOpType
AX = mybir.AxisListType.X

R = 4
S = 2048
D = 1024
SEL = 1024
IN2 = 2048
HM = 4096
E = 8
NT = S // 128
NSEL = SEL // 128
DC = D // 128
EPS = 1e-5


_GUARD_OPS = {
    "tensor_tensor", "tensor_add", "tensor_sub", "tensor_mul", "tensor_max",
    "tensor_scalar", "tensor_single_scalar", "tensor_scalar_add", "tensor_scalar_sub",
    "tensor_scalar_mul", "tensor_scalar_max", "tensor_scalar_min", "tensor_relu",
    "tensor_reduce", "reduce_max", "reduce_sum", "scalar_tensor_tensor",
    "tensor_copy", "memset", "memzero", "reciprocal", "max", "max_index",
    "select", "copy_predicated", "activation", "copy", "add", "mul", "square",
    "sqrt", "sign",
}
_OUT_KW = ("out", "accum_out", "out_max", "out_indices", "dest", "ap")


class Guard:

    def __init__(self, eng):
        self._e = eng
        self._dirty = {}

    @staticmethod
    def _rng(a):
        t = getattr(a, "tensor", None)
        nm = getattr(t, "name", None)
        if nm is None:
            return None
        try:
            pat = a.ap
            bp = int(a.base_partition())
            off = int(a.offset) - bp * int(pat[0][0])
            npart = int(pat[0][1])
            span = 1
            for st, n in pat[1:]:
                span += (int(n) - 1) * int(st)
            if off < 0:
                return (nm, 0, 1 << 30, 0, 1 << 30)
            return (nm, bp, bp + npart, off, off + span)
        except Exception:
            return (nm, 0, 1 << 30, 0, 1 << 30)

    def __getattr__(self, k):
        fn = getattr(self._e, k)
        if k not in _GUARD_OPS:
            return fn

        def wrap(*args, **kw):
            outs, ins = [], []
            for i, a_ in enumerate(args):
                rg = Guard._rng(a_)
                if rg is None:
                    continue
                (outs if i == 0 else ins).append(rg)
            for kk, a_ in kw.items():
                rg = Guard._rng(a_)
                if rg is None:
                    continue
                (outs if kk in _OUT_KW else ins).append(rg)
            hit = False
            for nm, p0, p1, c0, c1 in ins:
                for (q0, q1, d0, d1) in self._dirty.get(nm, ()):
                    if p0 < q1 and q0 < p1 and c0 < d1 and d0 < c1:
                        hit = True
                        break
                if hit:
                    break
            if hit:
                self._e.drain()
                self._dirty.clear()
            r = fn(*args, **kw)
            for nm, p0, p1, c0, c1 in outs:
                lst = self._dirty.setdefault(nm, [])
                lst.append((p0, p1, c0, c1))
                if len(lst) > 48:
                    del lst[:]
                    lst.append((0, 1 << 30, 0, 1 << 30))
            return r

        return wrap


class Sem:
    def __init__(self, h):
        self.h = h
        self.n = 0

    def inc(self, instr, k):
        instr.then_inc(self.h, k)
        self.n += k
        return instr


class StopEmission(Exception):
    pass


class SlotSem:
    def __init__(self, mk, name, n=2):
        self.s = [mk(name + str(i)) for i in range(n)]
        self.nslots = n

    def sem(self, i):
        return self.s[i % self.nslots]

    def inc(self, i, instr, k=16):
        return self.s[i % self.nslots].inc(instr, k)

    def val(self, i):
        return self.s[i % self.nslots].n


def build_nc(debug=False, maxphase=999):
    import contextlib
    nc = bass.Bass("TRN2", target_bir_lowering=False)
    ctx = contextlib.ExitStack()

    _phc = [0]

    def CUT(label=""):
        _phc[0] += 1
        if _phc[0] > maxphase:
            raise StopEmission(label)

    dti = lambda name, shape, dtype=F32: nc.dram_tensor(name, shape, dtype, kind="ExternalInput").ap()
    xin = dti("xin", [R * S, D])
    w_m_in = dti("m_in_w", [D, 2 * IN2])
    h_conv3 = dti("h_conv3", [128, 16])
    h_convb = dti("h_convb", [128, 16])
    w_xp = dti("m_xproj_w", [IN2, 33])
    w_mout = dti("m_out_w", [IN2, D])
    w_r1 = dti("modr_w1", [D, 256])
    b_r2 = dti("modr_b2", [1])
    h_r1b = dti("h_r1b", [128, 2])
    h_r2 = dti("h_r2", [128, 2])
    ln1w = dti("ln1_w", [D]); ln1b = dti("ln1_b", [D])
    ln2w = dti("ln2_w", [D]); ln2b = dti("ln2_b", [D])
    h_mrw = dti("h_mrw", [128, 64]); b_mr = dti("moe_r_b", [E])
    w1_moe = dti("moe_w1", [E, D, HM]); h_b1 = dti("h_b1", [128, 256])
    w2_moe = dti("moe_w2", [E, HM, D]); b2_moe = dti("moe_b2", [E, D])
    h_gw = dti("h_gw", [128, 64]); b_gate = dti("gate_b", [E])
    w_ffn = dti("expert_w", [E, D, D]); b_ffn = dti("expert_b", [E, D])
    cONES = nc.dram_tensor("cONES", [1, 128], BF16, kind="ExternalInput").ap()
    cONESC = nc.dram_tensor("cONESC", [128, 1], BF16, kind="ExternalInput").ap()
    cTRIle = nc.dram_tensor("cTRIle", [128, 128], BF16, kind="ExternalInput").ap()
    cTRIst = nc.dram_tensor("cTRIst", [128, 128], BF16, kind="ExternalInput").ap()
    cID = nc.dram_tensor("cID", [128, 128], BF16, kind="ExternalInput").ap()

    out = nc.dram_tensor("out", [R * S, D], F32, kind="ExternalOutput").ap()

    dbg = {}
    if debug:
        def DB(name, shape, dtype=F32):
            dbg[name] = nc.dram_tensor("dbg_" + name, shape, dtype, kind="ExternalOutput").ap()
        DB("scores", [R, S]); DB("rank", [R, S]); DB("mw", [R, S]); DB("c", [R, S])
        DB("wgt", [R, S, E]); DB("gate", [R, SEL, E]); DB("hid", [R, S, D], BF16)
        DB("tok", [R, SEL, D], BF16); DB("n1", [R, SEL, D], BF16); DB("n2", [R, SEL, D], BF16)
        DB("hrh", [R, 128, 4096], BF16); DB("hrm", [R, 128, 4096], BF16); DB("mid", [R, S, D], BF16)
        DB("w1s", [128, 4096], BF16); DB("hiT", [R, 128, 16384], BF16); DB("midT", [R, 128, 16384], BF16)
        DB("lnB", [4, 128, D], BF16); DB("wgt2", [R, 128, 128]); DB("wgt3", [R, 128, 128])
        DB("srh", [R, 1, 2048], BF16); DB("srm", [R, 1, 2048], BF16); DB("sT", [R, 128, 16])
        DB("u", [R, IN2, SEL], BF16); DB("y", [R, IN2, SEL], BF16)
        DB("tok2", [R, SEL, D]); DB("tok3", [R, SEL, D]); DB("g", [R, S, D], BF16)

    scr = lambda name, shape, dtype: nc.dram_tensor(name, shape, dtype, kind="Internal").ap()
    hid = scr("hid", [S, D], BF16)
    midd = scr("midd", [S, D], BF16)
    n1d = scr("n1d", [SEL, D], BF16)
    n2d = scr("n2d", [SEL, D], BF16)
    tokd = scr("tokd", [SEL, D], BF16)
    tokfd = scr("tokfd", [SEL, D], BF16)
    gd = scr("gd", [S, D], BF16)

    sb = lambda name, shape, dtype: ctx.enter_context(nc.sbuf_tensor(name, shape, dtype))[:, :]

    GB0 = sb("GB0", [128, 32768], BF16)
    GB1 = sb("GB1", [128, 16384], BF16)
    GB2 = sb("GB2", [128, 8192], BF16)
    GF0 = sb("GF0", [128, 8192], F32)
    TOKB = sb("TOKB", [128, 8192], BF16)
    SA = sb("SA", [128, 2048], BF16)

    ones = sb("ones", [1, 128], BF16)
    onesc = sb("onesc", [128, 1], BF16)
    triLE = sb("triLE", [128, 128], BF16)
    triST = sb("triST", [128, 128], BF16)
    idmat = sb("idmat", [128, 128], BF16)
    lnw1B = sb("lnw1B", [128, D], BF16)
    lnb1B = sb("lnb1B", [128, D], BF16)
    lnw2B = sb("lnw2B", [128, D], BF16)
    lnb2B = sb("lnb2B", [128, D], BF16)
    w1hi = sb("w1hi", [128, 2048], BF16)
    w1lo = sb("w1lo", [128, 2048], BF16)
    w1f = GF0[:, 0:2048]
    w2hl = sb("w2hl", [128, 4], BF16)
    w2f = sb("w2f", [128, 2], F32)
    b1r = sb("b1r", [128, 2], F32)
    xpf = sb("xpf", [128, 33], F32)
    xpB = sb("xpB", [128, 16], BF16)
    xpC = sb("xpC", [128, 16], BF16)
    convw = sb("convw", [128, 16], F32)
    convb = sb("convb", [128, 16], F32)
    mrw = sb("mrw", [128, 64], BF16)
    mrbf = sb("mrbf", [1, 8], F32)
    mrbh = sb("mrbh", [1, 16], BF16)
    gwsb = sb("gwsb", [128, 64], BF16)
    gbf = sb("gbf", [1, 8], F32)
    r2bf = sb("r2bf", [1, 1], F32)
    r2bh = sb("r2bh", [1, 2], BF16)
    onesr = sb("onesr", [1, 512], BF16)
    gbh = sb("gbh", [1, 16], BF16)
    b1sb = sb("b1sb", [128, 256], F32)
    b2row = sb("b2row", [1, 2048], BF16)
    fbrow = sb("fbrow", [1, 1024], BF16)
    lnstg = GF0[0:1, 0:1024]
    lnsplit = SA[0:1, 0:2048]

    sT = sb("sT", [128, 16], F32)
    s12T = sb("s12T", [128, 16], F32)
    sTh = sb("sTh", [128, 16], BF16)
    sTm = sb("sTm", [128, 16], BF16)
    res16 = sb("res16", [128, 16], F32)
    srhi = sb("srhi", [1, 2048], BF16)
    srmid = sb("srmid", [1, 2048], BF16)
    srr = None
    rankT = sb("rankT", [128, 16], F32)
    maskT = sb("maskT", [128, 16], BF16)
    nmT = sb("nmT", [128, 16], BF16)
    mwT = sb("mwT", [128, 16], F32)
    eT = sb("eT", [128, 16], F32)
    erow = sb("erow", [128, 1], F32)
    erowb = sb("erowb", [128, 1], BF16)
    negm = sb("negm", [128, 1], F32)
    wz = sb("wz", [1, 4], F32)
    carry = sb("carry", [1, 48], F32)
    carryh = sb("carryh", [1, 32], BF16)
    cT = sb("cT", [128, 16], F32)
    ogat = sb("ogat", [128, 16], I32)
    osca = sb("osca", [128, 16], I32)
    oscf = sb("oscf", [128, 16], F32)
    stats = sb("stats", [128, 16], F32)
    zrow = sb("zrow", [1, 1024], BF16)
    zzr = sb("zzr", [1, 512], F32)
    zbs = sb("zbs", [128, 1024], BF16)
    silur = sb("silur", [128, 512], BF16)
    gatet = sb("gatet", [128, 64], F32)
    wgtt = sb("wgtt", [128, 128], F32)
    lgt = sb("lgt", [128, 32], F32)
    lgs = sb("lgs", [128, 192], F32)
    gtile = None

    gtile = zbs[:, 0:512]
    PS = [ctx.enter_context(nc.psum_tensor("PS%d" % i, [128, 512], F32)) for i in range(8)]

    sm = lambda name: Sem(ctx.enter_context(nc.semaphore(name)))
    dX = sm("dX")
    dG = sm("dG")
    dS = sm("dS")
    vv = sm("vv")
    ss = sm("ss")
    pp = sm("pp")
    sxr = SlotSem(sm, "sxr")
    sSP = sm("sSP")
    sSC = sm("sSC")
    sWe = SlotSem(sm, "sWe")
    sMI = SlotSem(sm, "sMI")
    sMO = SlotSem(sm, "sMO")
    sW1 = SlotSem(sm, "sW1")
    sW2 = SlotSem(sm, "sW2")
    sB2 = SlotSem(sm, "sB2")
    sGA = SlotSem(sm, "sGA")
    sXR = SlotSem(sm, "sXR")

    blk = [None]

    guards = {}

    def EN(which, fn):
        if which in ("vector", "scalar"):
            getattr(blk[0], which)(lambda eng: fn(guards.setdefault(which, Guard(eng))))
        else:
            getattr(blk[0], which)(fn)

    with nc.Block() as block:
      try:
        blk[0] = block

        def _pro_sync(q):
            dX.inc(q.dma_start(out=ones[:, :], in_=cONES[:, :]), 16)
            dX.inc(q.dma_start(out=onesc[:, :], in_=cONESC[:, :]), 16)
            dX.inc(q.dma_start(out=triLE[:, :], in_=cTRIle[:, :]), 16)
            dX.inc(q.dma_start(out=triST[:, :], in_=cTRIst[:, :]), 16)
            dX.inc(q.dma_start(out=idmat[:, :], in_=cID[:, :]), 16)
            dX.inc(q.dma_start(out=w1f.rearrange("p (c m) -> p c m", c=8), in_=w_r1.rearrange("(c p) m -> p c m", p=128)), 16)
            dX.inc(q.dma_start(out=w2f[:, :], in_=h_r2[:, :]), 16)
            dX.inc(q.dma_start(out=b1r[:, :], in_=h_r1b[:, :]), 16)
            dX.inc(q.dma_start(out=convw[:, :], in_=h_conv3[:, :]), 16)
            dX.inc(q.dma_start(out=convb[:, :], in_=h_convb[:, :]), 16)
            dX.inc(q.dma_start(out=mrbf[:, :], in_=b_mr.rearrange("(o e) -> o e", o=1)), 16)
            dX.inc(q.dma_start(out=r2bf[:, :], in_=b_r2.rearrange("(o e) -> o e", o=1)), 16)
            dX.inc(q.dma_start(out=gbf[:, :], in_=b_gate.rearrange("(o e) -> o e", o=1)), 16)
            dX.inc(q.dma_start(out=b1sb[:, :], in_=h_b1[:, :]), 16)
        EN("sync", _pro_sync)
        pro_dX0 = dX.n

        BCREG = [None]

        def _pro_g(q):
            BCREG[0] = q.to_reg(SEL - 1)
            q.memset(onesr[:, :], 1.0)
            dG.inc(q.dma_start(out=mrw[:, :], in_=h_mrw[:, :]), 16)
            dG.inc(q.dma_start(out=gwsb[:, :], in_=h_gw[:, :]), 16)
        EN("gpsimd", _pro_g)
        pro_dG0 = dG.n

        def _pro_v(v):
            v.wait_ge(dX.h, pro_dX0)
            v.wait_ge(dG.h, pro_dG0)
            import os as _os2
            if "V" in _os2.environ.get("PSKIP", ""):
                vv.inc(v.tensor_copy(out=w1hi[:, 0:8], in_=w1f[:, 0:8]), 1)
                return
            v.tensor_copy(out=w1hi[:, :], in_=w1f[:, :])
            v.tensor_tensor(out=w1f[:, :], in0=w1f[:, :], in1=w1hi[:, :], op=ALU.subtract)
            v.tensor_copy(out=w1lo[:, :], in_=w1f[:, :])
            v.tensor_copy(out=w2hl[:, 0:2], in_=w2f[:, :])
            v.tensor_tensor(out=w2f[:, :], in0=w2f[:, :], in1=w2hl[:, 0:2], op=ALU.subtract)
            v.tensor_copy(out=w2hl[:, 2:4], in_=w2f[:, :])
            v.tensor_copy(out=mrbh[:, 0:8], in_=mrbf[:, :])
            v.tensor_tensor(out=mrbf[:, :], in0=mrbf[:, :], in1=mrbh[:, 0:8], op=ALU.subtract)
            v.tensor_copy(out=mrbh[:, 8:16], in_=mrbf[:, :])
            v.tensor_copy(out=r2bh[:, 0:1], in_=r2bf[:, :])
            v.tensor_tensor(out=r2bf[:, :], in0=r2bf[:, :], in1=r2bh[:, 0:1], op=ALU.subtract)
            v.tensor_copy(out=r2bh[:, 1:2], in_=r2bf[:, :])
            v.tensor_copy(out=gbh[:, 0:8], in_=gbf[:, :])
            v.tensor_tensor(out=gbf[:, :], in0=gbf[:, :], in1=gbh[:, 0:8], op=ALU.subtract)
            vv.inc(v.tensor_copy(out=gbh[:, 8:16], in_=gbf[:, :]), 1)
        EN("vector", _pro_v)

        import os as _os
        _SKIP = _os.environ.get("PSKIP", "")
        for li, (src, dst) in enumerate([] if "L" in _SKIP else [(ln1w, lnw1B), (ln1b, lnb1B), (ln2w, lnw2B), (ln2b, lnb2B)]):
            def _ld(q, src=src, li=li):
                q.wait_ge(vv.h, vv.n)
                dX.inc(q.dma_start(out=lnstg[:, :], in_=src.rearrange("(o n) -> o n", o=1)), 16)
            EN("sync", _ld)
            ld = dX.n

            def _sp(v, ld=ld, ppn=pp.n):
                v.wait_ge(dX.h, ld)
                v.wait_ge(pp.h, ppn)
                v.tensor_copy(out=lnsplit[0:1, 0:1024], in_=lnstg[:, :])
                v.tensor_tensor(out=lnstg[:, :], in0=lnstg[:, :], in1=lnsplit[0:1, 0:1024], op=ALU.subtract)
                vv.inc(v.tensor_copy(out=lnsplit[0:1, 1024:2048], in_=lnstg[:, :]), 1)
            EN("vector", _sp)
            spm = vv.n

            for half in range(2):
                def _mm(t, half=half, spm=spm, ssn=ss.n):
                    t.wait_ge(vv.h, spm)
                    t.wait_ge(ss.h, ssn)
                    t.matmul(PS[6][:, 0:512], ones[:, :], lnsplit[0:1, 512 * half:512 * (half + 1)],
                             start=True, stop=False)
                    pp.inc(t.matmul(PS[6][:, 0:512], ones[:, :],
                                    lnsplit[0:1, 1024 + 512 * half:1024 + 512 * (half + 1)],
                                    start=False, stop=True), 1)
                EN("tensor", _mm)
                mmv = pp.n

                def _ev(s, half=half, dst=dst, mmv=mmv):
                    s.wait_ge(pp.h, mmv)
                    ss.inc(s.copy(out=dst[:, 512 * half:512 * (half + 1)], in_=PS[6][:, 0:512]), 1)
                EN("scalar", _ev)

        for c in ([] if "X" in _SKIP else range(16)):
            def _ld(q, c=c):
                q.wait_ge(vv.h, vv.n)
                dX.inc(q.dma_start(out=xpf[:, :], in_=w_xp[128 * c:128 * (c + 1), :]), 16)
            EN("sync", _ld)
            ld = dX.n

            def _pr(v, c=c, ld=ld):
                v.wait_ge(dX.h, ld)
                v.reduce_sum(stats[:, 0:1], xpf[:, 16:32], axis=AX)
                v.tensor_copy(out=xpB[:, c:c + 1], in_=stats[:, 0:1])
                vv.inc(v.tensor_copy(out=xpC[:, c:c + 1], in_=xpf[:, 32:33]), 1)
            EN("vector", _pr)

        PRO_V = vv.n
        CUT("prologue")
        PRO_S = ss.n

        for r in range(R):
            xrow = xin[r * S:(r + 1) * S, :]
            outrow = out[r * S:(r + 1) * S, :]

            hiT = lambda c: GB0[:, 2048 * c:2048 * (c + 1)]
            midT = lambda c: GB0[:, 16384 + 2048 * c:16384 + 2048 * (c + 1)]
            hisb = lambda tt: GB1[:, 1024 * tt:1024 * (tt + 1)]
            hrhi = lambda c: GB2[:, 2048 * c:2048 * (c + 1)]
            hrmid = lambda c: GB2[:, 4096 + 2048 * c:4096 + 2048 * (c + 1)]
            sjb = GF0[:, 0:2048]
            tbuf = GF0[:, 2048:4096]
            xtile = lambda i: GF0[:, 4096 + 1024 * (i % 2):4096 + 1024 * (i % 2) + 1024]
            hrr = GF0[:, 6144:6656]
            srr = GF0[0:1, 6656:7168]
            mscr = GF0[:, 7168:8192]

            vA0 = vv.n
            import os
            NTA = int(os.environ.get('NTA', NT))
            for tt in range(NTA):
                def _ldx(q, tt=tt):
                    if tt >= 2:
                        q.wait_ge(vv.h, vA0 + 5 * (tt - 1))
                    sxr.inc(tt, q.dma_start(out=xtile(tt), in_=xrow[128 * tt:128 * (tt + 1), :]), 16)
                EN("sync", _ldx)
                ld = sxr.val(tt)

                def _v1(v, tt=tt, ld=ld):
                    v.wait_ge(sxr.sem(tt).h, ld)
                    if tt == 0 and r == 0:
                        v.wait_ge(ss.h, PRO_S)
                    v.reduce_sum(stats[:, 0:1], xtile(tt), axis=AX)
                    vv.inc(v.tensor_scalar(out=stats[:, 1:2], in0=stats[:, 0:1], scalar1=1.0 / D, scalar2=None, op0=ALU.mult), 1)
                EN("vector", _v1)
                m1 = vv.n

                def _s1(s, tt=tt, m1=m1):
                    s.wait_ge(vv.h, m1)
                    ss.inc(s.activation(out=SA[:, 0:1024], in_=xtile(tt), func=AF.Square,
                                        accum_out=stats[:, 2:3]), 1)
                EN("scalar", _s1)
                m2 = ss.n

                def _v2(v, tt=tt, m2=m2):
                    v.wait_ge(ss.h, m2)
                    v.tensor_scalar(out=stats[:, 3:4], in0=stats[:, 2:3], scalar1=1.0 / D, scalar2=None, op0=ALU.mult)
                    v.tensor_tensor(out=stats[:, 4:5], in0=stats[:, 1:2], in1=stats[:, 1:2], op=ALU.mult)
                    v.tensor_tensor(out=stats[:, 5:6], in0=stats[:, 3:4], in1=stats[:, 4:5], op=ALU.subtract)
                    vv.inc(v.tensor_scalar(out=stats[:, 5:6], in0=stats[:, 5:6], scalar1=EPS, scalar2=None, op0=ALU.add), 1)
                EN("vector", _v2)
                m3 = vv.n

                def _s2(s, tt=tt, m3=m3):
                    s.wait_ge(vv.h, m3)
                    s.activation(out=stats[:, 6:7], in_=stats[:, 5:6], func=AF.Ln)
                    ss.inc(s.activation(out=stats[:, 7:8], in_=stats[:, 6:7], func=AF.Exp, scale=-0.5), 1)
                EN("scalar", _s2)
                m4 = ss.n

                def _v3(v, tt=tt, m4=m4):
                    v.wait_ge(ss.h, m4)
                    v.tensor_tensor(out=stats[:, 8:9], in0=stats[:, 7:8], in1=stats[:, 7:8], op=ALU.mult)
                    v.tensor_tensor(out=stats[:, 9:10], in0=stats[:, 8:9], in1=stats[:, 5:6], op=ALU.mult)
                    v.tensor_scalar(out=stats[:, 9:10], in0=stats[:, 9:10], scalar1=-0.5, scalar2=1.5,
                                    op0=ALU.mult, op1=ALU.add)
                    v.tensor_tensor(out=stats[:, 10:11], in0=stats[:, 7:8], in1=stats[:, 9:10], op=ALU.mult)
                    v.tensor_tensor(out=stats[:, 11:12], in0=stats[:, 1:2], in1=stats[:, 10:11], op=ALU.mult)
                    v.tensor_scalar(out=mscr, in0=xtile(tt), scalar1=stats[:, 10:11], scalar2=stats[:, 11:12],
                                    op0=ALU.mult, op1=ALU.subtract)
                    vv.inc(v.tensor_copy(out=hisb(tt), in_=mscr), 1)
                    v.tensor_tensor(out=mscr, in0=mscr, in1=hisb(tt), op=ALU.subtract)
                    vv.inc(v.tensor_copy(out=SA[:, 1024:2048], in_=mscr), 1)
                EN("vector", _v3)
                m5 = vv.n

                def _spl(q, tt=tt, m5=m5):
                    q.wait_ge(vv.h, m5)
                    sSP.inc(q.dma_start(out=hid[128 * tt:128 * (tt + 1), :], in_=hisb(tt)), 16)
                    sSP.inc(q.dma_start(out=midd[128 * tt:128 * (tt + 1), :], in_=SA[:, 1024:2048]), 16)
                EN("sync", _spl)
                spl = sSP.n

                def _vw(v, spl=spl):
                    v.wait_ge(sSP.h, spl)
                    vv.inc(v.engine_nop(), 1)
                EN("vector", _vw)
            A_dX = dX.n
            A_SP = sSP.n
            assert NTA == NT or maxphase <= 2
            CUT("A")

            if debug:
                def _dbh(q):
                    q.wait_ge(sSP.h, A_SP)
                    dX.inc(q.dma_start(out=dbg["hid"][r], in_=hid[:, :], allow_slow_non_contiguous=True), 16)
                    dX.inc(q.dma_start(out=dbg["mid"][r], in_=midd[:, :]), 16)
                EN("sync", _dbh)

            import os as _os
            _NOTR = int(_os.environ.get("NOTR", "0"))

            def _trl(s):
                s.wait_ge(sSP.h, A_SP)
                for c in range(DC):
                    if _NOTR:
                        dS.inc(s.dma_start(out=hiT(c)[:, 0:1024], in_=hid[0:128, :]), 16)
                        dS.inc(s.dma_start(out=midT(c)[:, 0:1024], in_=midd[0:128, :]), 16)
                    else:
                        dS.inc(s.dma_start(out=hiT(c), in_=hid[:, 128 * c:128 * (c + 1)], transpose=True), 16)
                        dS.inc(s.dma_start(out=midT(c), in_=midd[:, 128 * c:128 * (c + 1)], transpose=True), 16)
            EN("scalar", _trl)
            B_dS = dS.n

            if debug:
                def _dbtr(q):
                    q.wait_ge(dS.h, B_dS)
                    if r == 0:
                        dX.inc(q.dma_start(out=dbg["w1s"][:, 0:2048], in_=w1hi[:, :]), 16)
                        dX.inc(q.dma_start(out=dbg["w1s"][:, 2048:4096], in_=w1lo[:, :]), 16)
                        for li2, t2 in enumerate([lnw1B, lnb1B, lnw2B, lnb2B]):
                            dX.inc(q.dma_start(out=dbg["lnB"][li2], in_=t2[:, :]), 16)
                    dX.inc(q.dma_start(out=dbg["hiT"][r], in_=GB0[:, 0:16384]), 16)
                    dX.inc(q.dma_start(out=dbg["midT"][r], in_=GB0[:, 16384:32768]), 16)
                EN("sync", _dbtr)
                dbtr = dX.n

                def _dbtrw(t, dbtr=dbtr):
                    t.wait_ge(dX.h, dbtr)
                EN("tensor", _dbtrw)
            CUT("trl")

            prodW = [(0, 0), (0, 1), (1, 0), (1, 1)]
            evhr = []
            for jm in range(8):
                jt, mc = jm // 2, jm % 2

                def _mm(t, jt=jt, mc=mc, jm=jm):
                    if jm == 0:
                        t.wait_ge(dS.h, B_dS)
                        t.wait_ge(vv.h, PRO_V)
                        if r > 0:
                            t.wait_ge(vv.h, rowjoin_v)
                    if jm >= 2:
                        t.wait_ge(vv.h, evhr[jm - 2])
                    bank = jm % 2
                    first = True
                    for k in range(DC):
                        for (wp, xp) in prodW:
                            w = w1hi if wp == 0 else w1lo
                            xs = hiT(k) if xp == 0 else midT(k)
                            last = (k == DC - 1 and wp == 1 and xp == 1)
                            mm = t.matmul(PS[bank][:, 0:512], w[:, 256 * k + 128 * mc:256 * k + 128 * (mc + 1)],
                                          xs[:, 512 * jt:512 * (jt + 1)], start=first, stop=last)
                            first = False
                    pp.inc(mm, 1)
                EN("tensor", _mm)
                mmv = pp.n
                import os as _os3
                _HRSKIP = _os3.environ.get("HRSKIP", "")

                def _ev(s, jt=jt, mc=mc, jm=jm, mmv=mmv):
                    s.wait_ge(pp.h, mmv)
                    if jm >= 1:
                        s.wait_ge(vv.h, vv.n)
                    if "E" in _HRSKIP:
                        ss.inc(s.copy(out=hrr, in_=PS[jm % 2][:, 0:512]), 1)
                    elif "B" in _HRSKIP:
                        ss.inc(s.activation(out=hrr, in_=PS[jm % 2][:, 0:512], func=AF.Relu), 1)
                    else:
                        ss.inc(s.activation(out=hrr, in_=PS[jm % 2][:, 0:512], func=AF.Relu,
                                            bias=b1r[:, mc:mc + 1]), 1)
                EN("scalar", _ev)
                evv = ss.n

                def _spv(v, jt=jt, mc=mc, evv=evv):
                    v.wait_ge(ss.h, evv)
                    if "S" in _HRSKIP:
                        vv.inc(v.tensor_copy(out=hrhi(mc)[:, 512 * jt:512 * (jt + 1)], in_=hrr), 1)
                        return
                    v.tensor_copy(out=hrhi(mc)[:, 512 * jt:512 * (jt + 1)], in_=hrr)
                    v.tensor_tensor(out=hrr, in0=hrr, in1=hrhi(mc)[:, 512 * jt:512 * (jt + 1)], op=ALU.subtract)
                    vv.inc(v.tensor_copy(out=hrmid(mc)[:, 512 * jt:512 * (jt + 1)], in_=hrr), 1)
                EN("vector", _spv)
                evhr.append(vv.n)
                CUT("hr%d" % jm)
            B_HR = vv.n
            CUT("hr")

            prodS = [(0, 0), (0, 1), (1, 0), (1, 1)]
            for jt in range(4):
                def _mm(t, jt=jt):
                    if jt == 0:
                        t.wait_ge(vv.h, B_HR)
                    else:
                        t.wait_ge(ss.h, ss.n)
                    first = True
                    for c in range(2):
                        for (wp, hp) in prodS:
                            hsrc = hrhi(c) if hp == 0 else hrmid(c)
                            t.matmul(PS[2][0:1, 0:512], w2hl[:, 2 * wp + c:2 * wp + c + 1],
                                     hsrc[:, 512 * jt:512 * (jt + 1)], start=first, stop=False)
                            first = False
                    t.matmul(PS[2][0:1, 0:512], r2bh[0:1, 0:1], onesr[0:1, 0:512],
                             start=False, stop=False)
                    mm = t.matmul(PS[2][0:1, 0:512], r2bh[0:1, 1:2], onesr[0:1, 0:512],
                                  start=False, stop=True)
                    pp.inc(mm, 1)
                EN("tensor", _mm)
                mmv = pp.n

                def _ev(s, jt=jt, mmv=mmv):
                    s.wait_ge(pp.h, mmv)
                    if jt > 0:
                        s.wait_ge(vv.h, vv.n)
                    ss.inc(s.copy(out=srr[:, :], in_=PS[2][0:1, 0:512]), 1)
                EN("scalar", _ev)
                evv = ss.n

                def _spv(v, jt=jt, evv=evv):
                    v.wait_ge(ss.h, evv)
                    v.tensor_copy(out=srhi[0:1, 512 * jt:512 * (jt + 1)], in_=srr)
                    v.tensor_tensor(out=srr, in0=srr, in1=srhi[0:1, 512 * jt:512 * (jt + 1)], op=ALU.subtract)
                    vv.inc(v.tensor_copy(out=srmid[0:1, 512 * jt:512 * (jt + 1)], in_=srr), 1)
                EN("vector", _spv)

            def _mmB(t):
                for tt in range(NT):
                    first = True
                    for c in range(2):
                        for (wp, hp) in prodS:
                            hsrc = hrhi(c) if hp == 0 else hrmid(c)
                            t.matmul(PS[3][:, tt:tt + 1], hsrc[:, 128 * tt:128 * (tt + 1)],
                                     w2hl[:, 2 * wp + c:2 * wp + c + 1], start=first, stop=False)
                            first = False
                    t.matmul(PS[3][:, tt:tt + 1], ones[:, :], r2bh[0:1, 0:1], start=False, stop=False)
                    mm = t.matmul(PS[3][:, tt:tt + 1], ones[:, :], r2bh[0:1, 1:2], start=False, stop=True)
                pp.inc(mm, 1)
            EN("tensor", _mmB)
            m_sT = pp.n

            def _evB(s):
                s.wait_ge(pp.h, m_sT)
                ss.inc(s.copy(out=sT[:, :], in_=PS[3][:, 0:16]), 1)
            EN("scalar", _evB)
            m_sTe = ss.n

            def _spB(v):
                v.wait_ge(ss.h, m_sTe)
                v.tensor_copy(out=sTh[:, :], in_=sT[:, :])
                v.tensor_tensor(out=res16[:, :], in0=sT[:, :], in1=sTh[:, :], op=ALU.subtract)
                v.tensor_copy(out=sTm[:, :], in_=res16[:, :])
                v.tensor_tensor(out=res16[:, :], in0=res16[:, :], in1=sTm[:, :], op=ALU.subtract)
                vv.inc(v.tensor_tensor(out=s12T[:, :], in0=sT[:, :], in1=res16[:, :], op=ALU.subtract), 1)
            EN("vector", _spB)
            B_S12 = vv.n
            CUT("scores")

            for jt in range(4):
                def _mm(t, jt=jt):
                    if jt == 0:
                        t.wait_ge(vv.h, B_S12)
                    else:
                        t.wait_ge(ss.h, ss.n)
                    t.matmul(PS[4][:, 0:512], ones[:, :], srhi[0:1, 512 * jt:512 * (jt + 1)], start=True, stop=False)
                    pp.inc(t.matmul(PS[4][:, 0:512], ones[:, :], srmid[0:1, 512 * jt:512 * (jt + 1)],
                                    start=False, stop=True), 1)
                EN("tensor", _mm)
                mmv = pp.n

                def _ev(s, jt=jt, mmv=mmv):
                    s.wait_ge(pp.h, mmv)
                    ss.inc(s.copy(out=sjb[:, 512 * jt:512 * (jt + 1)], in_=PS[4][:, 0:512]), 1)
                EN("scalar", _ev)
            B_SJB = ss.n

            if debug:
                def _dbhr(q):
                    q.wait_ge(ss.h, B_SJB)
                    dX.inc(q.dma_start(out=dbg["hrh"][r], in_=GB2[:, 0:4096]), 16)
                    dX.inc(q.dma_start(out=dbg["hrm"][r], in_=GB2[:, 4096:8192]), 16)
                    dX.inc(q.dma_start(out=dbg["srh"][r], in_=srhi[:, :]), 16)
                    dX.inc(q.dma_start(out=dbg["srm"][r], in_=srmid[:, :]), 16)
                    dX.inc(q.dma_start(out=dbg["sT"][r], in_=sT[:, :]), 16)
                EN("sync", _dbhr)

            def _rank(v):
                v.wait_ge(ss.h, B_SJB)
                for ic in range(NT):
                    v.tensor_scalar(out=tbuf, in0=sjb, scalar1=s12T[:, ic:ic + 1], scalar2=None, op0=ALU.subtract)
                    v.tensor_scalar(out=SA[:, 0:2048], in0=tbuf, scalar1=0.0, scalar2=0.0,
                                    op0=ALU.is_gt, op1=ALU.add, accum_out=rankT[:, ic:ic + 1])
                    if ic > 0:
                        v.tensor_scalar(out=SA[:, 0:128 * ic], in0=tbuf[:, 0:128 * ic], scalar1=0.0, scalar2=0.0,
                                        op0=ALU.is_equal, op1=ALU.add, accum_out=stats[:, 0:1])
                        v.tensor_tensor(out=rankT[:, ic:ic + 1], in0=rankT[:, ic:ic + 1], in1=stats[:, 0:1], op=ALU.add)
                    v.scalar_tensor_tensor(out=SA[:, 0:128], in0=tbuf[:, 128 * ic:128 * (ic + 1)], scalar=0.0,
                                           in1=triST[:, :], op0=ALU.is_equal, op1=ALU.mult, accum_out=stats[:, 1:2])
                    v.tensor_tensor(out=rankT[:, ic:ic + 1], in0=rankT[:, ic:ic + 1], in1=stats[:, 1:2], op=ALU.add)
                v.tensor_scalar(out=maskT[:, :], in0=rankT[:, :], scalar1=float(SEL), scalar2=None, op0=ALU.is_lt)
                vv.inc(v.tensor_scalar(out=nmT[:, :], in0=maskT[:, :], scalar1=-1.0, scalar2=1.0, op0=ALU.mult, op1=ALU.add), 1)
            EN("vector", _rank)
            B_RANK = vv.n
            CUT("rank")

            def _gmb(t):
                t.wait_ge(vv.h, B_RANK)
                pp.inc(t.matmul(PS[5][0:1, 16:32], onesc[:, :], maskT[:, :], start=True, stop=True), 1)
            EN("tensor", _gmb)
            m_gmb = pp.n

            def _neg(v):
                v.wait_ge(pp.h, m_gmb)
                vv.inc(v.reduce_max(negm[:, :], sjb, axis=AX, negate=True), 1)
                v.tensor_copy(out=carry[0:1, 0:16], in_=PS[5][0:1, 16:32])
                cur, oth = 0, 16
                for sh in (1, 2, 4, 8):
                    v.tensor_copy(out=carry[0:1, oth:oth + sh], in_=carry[0:1, cur:cur + sh])
                    v.tensor_tensor(out=carry[0:1, oth + sh:oth + 16], in0=carry[0:1, cur + sh:cur + 16],
                                    in1=carry[0:1, cur:cur + 16 - sh], op=ALU.add)
                    cur, oth = oth, cur
                v.memset(carry[0:1, oth:oth + 1], 0.0)
                v.tensor_copy(out=carry[0:1, oth + 1:oth + 16], in_=carry[0:1, cur:cur + 15])
                v.tensor_copy(out=carryh[0:1, 0:16], in_=carry[0:1, oth:oth + 16])
                v.tensor_tensor(out=carry[0:1, 32:48], in0=carry[0:1, oth:oth + 16], in1=carryh[0:1, 0:16],
                                op=ALU.subtract)
                vv.inc(v.tensor_copy(out=carryh[0:1, 16:32], in_=carry[0:1, 32:48]), 1)
            EN("vector", _neg)
            m_carry = vv.n

            def _exp(s):
                s.wait_ge(vv.h, m_carry)
                ss.inc(s.activation(out=eT[:, :], in_=sT[:, :], func=AF.Exp, bias=negm[:, :]), 1)
            EN("scalar", _exp)
            m_eT = ss.n

            def _em(v):
                v.wait_ge(ss.h, m_eT)
                v.tensor_tensor(out=eT[:, :], in0=eT[:, :], in1=maskT[:, :], op=ALU.mult)
                v.reduce_sum(erow[:, :], eT[:, :], axis=AX)
                vv.inc(v.tensor_copy(out=erowb[:, :], in_=erow[:, :]), 1)
            EN("vector", _em)
            m_er = vv.n

            def _zmm(t):
                t.wait_ge(vv.h, m_er)
                t.matmul(PS[5][0:1, 4:5], onesc[:, :], erowb[:, :], start=True, stop=True)
                t.matmul(PS[6][:, 0:16], triLE[:, :], maskT[:, :], start=True, stop=False)
                t.matmul(PS[6][:, 0:16], ones[:, :], carryh[0:1, 0:16], start=False, stop=False)
                pp.inc(t.matmul(PS[6][:, 0:16], ones[:, :], carryh[0:1, 16:32], start=False, stop=True), 1)
            EN("tensor", _zmm)
            m_Z = pp.n

            def _rz(v):
                v.wait_ge(pp.h, m_Z)
                v.tensor_copy(out=wz[0:1, 0:1], in_=PS[5][0:1, 4:5])
                v.reciprocal(out=wz[0:1, 1:2], in_=wz[0:1, 0:1])
                vv.inc(v.tensor_copy(out=SA[0:1, 256:257], in_=wz[0:1, 1:2]), 1)
                v.tensor_copy(out=cT[:, :], in_=PS[6][:, 0:16])
                v.tensor_scalar(out=res16[:, :], in0=cT[:, :], scalar1=1.0, scalar2=0.0, op0=ALU.subtract, op1=ALU.max)
                v.tensor_copy(out=ogat[:, :], in_=res16[:, :])
                v.tensor_scalar(out=res16[:, :], in0=cT[:, :], scalar1=1.0, scalar2=None, op0=ALU.subtract)
                v.tensor_tensor(out=res16[:, :], in0=res16[:, :], in1=maskT[:, :], op=ALU.mult)
                v.scalar_tensor_tensor(out=oscf[:, :], in0=nmT[:, :], scalar=4096.0, in1=res16[:, :],
                                       op0=ALU.mult, op1=ALU.add)
                vv.inc(v.tensor_copy(out=osca[:, :], in_=oscf[:, :]), 1)
            EN("vector", _rz)
            m_rzv = vv.n

            def _rzb(t):
                t.wait_ge(vv.h, m_rzv)
                pp.inc(t.matmul(PS[5][:, 2:3], ones[:, :], SA[0:1, 256:257], start=True, stop=True), 1)
            EN("tensor", _rzb)
            m_rzb = pp.n

            def _mw(v):
                v.wait_ge(pp.h, m_rzb)
                vv.inc(v.tensor_scalar(out=mwT[:, :], in0=eT[:, :], scalar1=PS[5][:, 2:3], scalar2=None,
                                       op0=ALU.mult), 1)
            EN("vector", _mw)
            C_OFF = vv.n
            CUT("C")

            if debug:
                def _dbc(q):
                    q.wait_ge(vv.h, C_OFF)
                    dX.inc(q.dma_start(out=dbg["scores"][r:r + 1, :].rearrange("o (c p) -> p (o c)", p=128), in_=sT[:, :], allow_slow_non_contiguous=True), 16)
                    dX.inc(q.dma_start(out=dbg["rank"][r:r + 1, :].rearrange("o (c p) -> p (o c)", p=128), in_=rankT[:, :], allow_slow_non_contiguous=True), 16)
                    dX.inc(q.dma_start(out=dbg["mw"][r:r + 1, :].rearrange("o (c p) -> p (o c)", p=128), in_=mwT[:, :], allow_slow_non_contiguous=True), 16)
                    dX.inc(q.dma_start(out=dbg["c"][r:r + 1, :].rearrange("o (c p) -> p (o c)", p=128), in_=cT[:, :], allow_slow_non_contiguous=True), 16)
                EN("sync", _dbc)

            def _sca(q):
                q.wait_ge(vv.h, C_OFF)
                for tt in range(NT):
                    sSC.inc(q.indirect_dma_start(
                        out=tokd[0:1, :], out_offset=bass.IndirectOffsetOnAxis(osca[:, tt:tt + 1], 0),
                        in_=hisb(tt), in_offset=None,
                        bounds_check=BCREG[0], oob_is_err=False), 16)
            EN("gpsimd", _sca)
            D_dG = sSC.n
            CUT("D-scatter")

            if debug:
                def _dbt(q):
                    q.wait_ge(sSC.h, D_dG)
                    dX.inc(q.dma_start(out=dbg["tok"][r], in_=tokd[:, :], allow_slow_non_contiguous=True), 16)
                EN("sync", _dbt)

            gate_l1 = []
            for tt in range(NT):
                par = tt % 4
                gps = PS[[5, 6, 4, 7][par]][:, 8:16]
                glt = lgt[:, 8 * par:8 * par + 8]
                gls = lambda a, b, par=par: lgs[:, 48 * par + a:48 * par + b]

                def _lmm(t, tt=tt, par=par, gps=gps):
                    if tt < 4:
                        t.wait_ge(vv.h, C_OFF)
                    else:
                        t.wait_ge(vv.h, gate_l1[tt - 4])
                    first = True
                    for k in range(DC):
                        t.matmul(gps, hiT(k)[:, 128 * tt:128 * (tt + 1)], gwsb[:, 8 * k:8 * (k + 1)],
                                 start=first, stop=False)
                        first = False
                    t.matmul(gps, ones[:, :], gbh[0:1, 0:8], start=False, stop=False)
                    pp.inc(t.matmul(gps, ones[:, :], gbh[0:1, 8:16], start=False, stop=True), 1)
                EN("tensor", _lmm)
                mmv = pp.n

                def _l1(v, tt=tt, mmv=mmv, gps=gps, glt=glt, gls=gls):
                    v.wait_ge(pp.h, mmv)
                    v.tensor_copy(out=glt, in_=gps)
                    vv.inc(v.reduce_max(gls(0, 1), glt, axis=AX, negate=True), 1)
                EN("vector", _l1)
                gate_l1.append(vv.n)
                m1 = vv.n

                def _l2(s, tt=tt, m1=m1, glt=glt, gls=gls):
                    s.wait_ge(vv.h, m1)
                    ss.inc(s.activation(out=gls(8, 16), in_=glt, func=AF.Exp, bias=gls(0, 1)), 1)
                EN("scalar", _l2)
                m2 = ss.n

                def _l3(v, tt=tt, m2=m2, gls=gls):
                    v.wait_ge(ss.h, m2)
                    v.reduce_sum(gls(1, 2), gls(8, 16), axis=AX)
                    v.reciprocal(out=gls(2, 3), in_=gls(1, 2))
                    v.tensor_scalar(out=gls(16, 24), in0=gls(8, 16), scalar1=gls(2, 3), scalar2=None, op0=ALU.mult)
                    v.tensor_scalar(out=gls(24, 32), in0=gls(16, 24), scalar1=0.1, scalar2=None, op0=ALU.is_ge)
                    v.tensor_tensor(out=gls(32, 40), in0=gls(16, 24), in1=gls(24, 32), op=ALU.mult)
                    v.reduce_sum(gls(3, 4), gls(32, 40), axis=AX)
                    v.reciprocal(out=gls(4, 5), in_=gls(3, 4))
                    vv.inc(v.tensor_scalar(out=wgtt[:, 8 * tt:8 * (tt + 1)], in0=gls(32, 40), scalar1=gls(4, 5),
                                           scalar2=None, op0=ALU.mult), 1)
                EN("vector", _l3)
            E_WGT = vv.n
            CUT("E-gate")

            def _ldt(q):
                q.wait_ge(sSC.h, D_dG)
                for st in range(NSEL):
                    dX.inc(q.dma_start(out=TOKB[:, 1024 * st:1024 * (st + 1)],
                                       in_=tokd[128 * st:128 * (st + 1), :]), 16)
            EN("sync", _ldt)
            F_TOK = dX.n

            def emit_f2_tile(st):
                def _n1a(v, st=st):
                    if st == 0:
                        v.wait_ge(dX.h, F_TOK)
                    tokt = TOKB[:, 1024 * st:1024 * (st + 1)]
                    v.reduce_sum(stats[:, 0:1], tokt, axis=AX)
                    vv.inc(v.tensor_scalar(out=stats[:, 1:2], in0=stats[:, 0:1], scalar1=1.0 / D, scalar2=None, op0=ALU.mult), 1)
                EN("vector", _n1a)
                m1 = vv.n

                def _n1b(s, st=st, m1=m1):
                    s.wait_ge(vv.h, m1)
                    ss.inc(s.activation(out=SA[:, 0:1024], in_=TOKB[:, 1024 * st:1024 * (st + 1)], func=AF.Square,
                                        accum_out=stats[:, 2:3]), 1)
                EN("scalar", _n1b)
                m2 = ss.n

                def _n1c(v, st=st, m2=m2):
                    v.wait_ge(ss.h, m2)
                    v.tensor_scalar(out=stats[:, 3:4], in0=stats[:, 2:3], scalar1=1.0 / D, scalar2=None, op0=ALU.mult)
                    v.tensor_tensor(out=stats[:, 4:5], in0=stats[:, 1:2], in1=stats[:, 1:2], op=ALU.mult)
                    v.tensor_tensor(out=stats[:, 5:6], in0=stats[:, 3:4], in1=stats[:, 4:5], op=ALU.subtract)
                    vv.inc(v.tensor_scalar(out=stats[:, 5:6], in0=stats[:, 5:6], scalar1=EPS, scalar2=None, op0=ALU.add), 1)
                EN("vector", _n1c)
                m3 = vv.n

                def _n1d_(s, st=st, m3=m3):
                    s.wait_ge(vv.h, m3)
                    s.activation(out=stats[:, 6:7], in_=stats[:, 5:6], func=AF.Ln)
                    ss.inc(s.activation(out=stats[:, 7:8], in_=stats[:, 6:7], func=AF.Exp, scale=-0.5), 1)
                EN("scalar", _n1d_)
                m4 = ss.n

                def _n1e(v, st=st, m4=m4):
                    v.wait_ge(ss.h, m4)
                    tokt = TOKB[:, 1024 * st:1024 * (st + 1)]
                    v.tensor_tensor(out=stats[:, 8:9], in0=stats[:, 1:2], in1=stats[:, 7:8], op=ALU.mult)
                    v.tensor_scalar(out=SA[:, 0:1024], in0=tokt, scalar1=stats[:, 7:8], scalar2=stats[:, 8:9],
                                    op0=ALU.mult, op1=ALU.subtract)
                    v.tensor_tensor(out=SA[:, 1024:2048], in0=SA[:, 0:1024], in1=lnw1B[:, :], op=ALU.mult)
                    vv.inc(v.tensor_tensor(out=SA[:, 1024:2048], in0=SA[:, 1024:2048], in1=lnb1B[:, :], op=ALU.add), 1)
                EN("vector", _n1e)
                m5 = vv.n

                def _n1s(q, st=st, m5=m5):
                    q.wait_ge(vv.h, m5)
                    dX.inc(q.dma_start(out=n1d[128 * st:128 * (st + 1), :], in_=SA[:, 1024:2048]), 16)
                EN("sync", _n1s)
                spl = dX.n

                def _n1w(v, spl=spl):
                    v.wait_ge(dX.h, spl)
                    vv.inc(v.engine_nop(), 1)
                EN("vector", _n1w)

            if debug:
                def _dbw(q):
                    q.wait_ge(vv.h, E_WGT)
                    dX.inc(q.dma_start(out=dbg["wgt"][r].rearrange("(tt p) e -> p tt e", p=128), in_=wgtt[:, :].rearrange("p (tt e) -> p tt e", tt=16), allow_slow_non_contiguous=True), 16)
                EN("sync", _dbw)

            WeV = lambda i: GB0[:, 16384 + 4096 * (i % 2):16384 + 4096 * (i % 2) + 4096]
            facc = lambda mt: GF0[:, 512 * mt:512 * (mt + 1)]
            ffn_mm = []
            ffn_ev = []
            ffn_sil = {}
            for ex in range(2 * E):
                nd, e = ex // E, ex % E

                def _lw(q, ex=ex, nd=nd, e=e):
                    if ex == 0:
                        q.wait_ge(pp.h, m_sT)
                    if len(ffn_mm) >= 2:
                        q.wait_ge(pp.h, ffn_mm[-2])
                    dG.inc(q.dma_start(out=WeV(ex).rearrange("p (c m) -> p c m", c=8),
                                       in_=w_ffn[e][:, 512 * nd:512 * (nd + 1)]
                                       .rearrange("(c p) m -> p c m", p=128)), 16)
                    dG.inc(q.dma_start(out=fbrow[0:1, 512 * (ex % 2):512 * (ex % 2) + 512],
                                       in_=b_ffn[e:e + 1, 512 * nd:512 * (nd + 1)]), 16)
                EN("gpsimd", _lw)
                lw = dG.n

                for mt in range(NT):
                    gi = ex * NT + mt

                    def _mm(t, ex=ex, nd=nd, e=e, mt=mt, gi=gi, lw=lw):
                        if mt == 0:
                            t.wait_ge(dG.h, lw)
                        bank = [0, 1, 2, 3, 4, 7][gi % 6]
                        if gi in (4, 5):
                            t.wait_ge(vv.h, gate_l1[-1])
                        if gi >= 6:
                            t.wait_ge(vv.h, ffn_ev[gi - 6])
                        t.matmul(PS[bank][:, 0:512], ones[:, :],
                                 fbrow[0:1, 512 * (ex % 2):512 * (ex % 2) + 512], start=True, stop=False)
                        for k in range(DC):
                            mm = t.matmul(PS[bank][:, 0:512], hiT(k)[:, 128 * mt:128 * (mt + 1)],
                                          WeV(ex)[:, 512 * k:512 * (k + 1)],
                                          start=False, stop=(k == DC - 1))
                        pp.inc(mm, 1)
                    EN("tensor", _mm)
                    if mt == NT - 1:
                        ffn_mm.append(pp.n)
                    mmv = pp.n

                    def _ev(v, ex=ex, e=e, mt=mt, gi=gi, mmv=mmv):
                        v.wait_ge(pp.h, mmv)
                        if ex == E and e == 0:
                            v.wait_ge(ss.h, ffn_sil[mt])
                        bank = [0, 1, 2, 3, 4, 7][gi % 6]
                        wsl = wgtt[:, 8 * mt + e:8 * mt + e + 1]
                        if e == 0:
                            vv.inc(v.tensor_scalar(out=facc(mt), in0=PS[bank][:, 0:512], scalar1=wsl,
                                                   scalar2=None, op0=ALU.mult), 1)
                        else:
                            vv.inc(v.scalar_tensor_tensor(out=facc(mt), in0=PS[bank][:, 0:512], scalar=wsl,
                                                          in1=facc(mt), op0=ALU.mult, op1=ALU.add), 1)
                    EN("vector", _ev)
                    ffn_ev.append(vv.n)

                if ex < NSEL:
                    emit_f2_tile(ex)

                if e == E - 1:
                    for mt in range(NT):
                        def _sg(s, mt=mt, nd=nd):
                            s.wait_ge(vv.h, ffn_ev[(nd * E + E - 1) * NT + mt])
                            if mt > 0:
                                s.wait_ge(dX.h, dX.n)
                            ss.inc(s.activation(out=gtile[:, :], in_=facc(mt), func=AF.Silu), 1)
                        EN("scalar", _sg)
                        m1 = ss.n
                        if nd == 0:
                            ffn_sil[mt] = m1

                        def _sp(q, mt=mt, nd=nd, m1=m1):
                            q.wait_ge(ss.h, m1)
                            dX.inc(q.dma_start(out=gd[128 * mt:128 * (mt + 1), 512 * nd:512 * (nd + 1)],
                                               in_=gtile[:, :]), 16)
                        EN("sync", _sp)

                if ex == NSEL - 1:
                    F_N1 = dX.n

                    def _n1t(s, F_N1=F_N1):
                        s.wait_ge(dX.h, F_N1)
                        s.wait_ge(pp.h, m_sT)
                        for c in range(DC):
                            dS.inc(s.dma_start(out=GB0[:, 24576 + 1024 * c:24576 + 1024 * (c + 1)],
                                               in_=n1d[:, 128 * c:128 * (c + 1)], transpose=True), 16)
                    EN("scalar", _n1t)
                    F_N1T = dS.n
            E_G = dX.n
            E_FFNMM = pp.n

            if debug:
                def _dbw3(q, m0=ffn_mm[0]):
                    q.wait_ge(pp.h, m0)
                    dX.inc(q.dma_start(out=dbg["wgt2"][r], in_=wgtt[:, :]), 16)
                    q.wait_ge(dX.h, E_G)
                    dX.inc(q.dma_start(out=dbg["wgt3"][r], in_=wgtt[:, :]), 16)
                EN("sync", _dbw3)
            CUT("E-experts")

            if debug:
                def _dbg2(q):
                    q.wait_ge(dX.h, E_G)
                    dX.inc(q.dma_start(out=dbg["g"][r], in_=gd[:, :], allow_slow_non_contiguous=True), 16)
                EN("sync", _dbg2)

            if debug:
                def _dbn1(q):
                    q.wait_ge(dS.h, F_N1T)
                    dX.inc(q.dma_start(out=dbg["n1"][r], in_=n1d[:, :], allow_slow_non_contiguous=True), 16)
                EN("sync", _dbn1)
            CUT("F2-n1")
            n1T = lambda c: GB0[:, 24576 + 1024 * c:24576 + 1024 * (c + 1)]

            minslot = lambda i: GB2[:, 2048 * (i % 2):2048 * (i % 2) + 2048]
            uch = lambda c: GB1[:, 1024 * c:1024 * (c + 1)]
            uv_ld = []
            uv_ev = []

            def emit_uv_block(Mg):
                def _lb(q, Mg=Mg):
                    if Mg == 0:
                        q.wait_ge(pp.h, m_sT)
                    if len(uv_ld) >= 2:
                        q.wait_ge(pp.h, uv_ld[-2][1])
                    dG.inc(q.dma_start(out=minslot(Mg).rearrange("p (c m) -> p c m", c=8),
                                       in_=w_m_in[:, 256 * Mg:256 * (Mg + 1)]
                                       .rearrange("(c p) m -> p c m", p=128)), 16)
                EN("gpsimd", _lb)
                lb = dG.n

                for sub in range(4):
                    m2_, rt = sub // 2, sub % 2
                    gi = Mg * 4 + sub
                    Mglob = Mg * 2 + m2_

                    def _mm(t, Mg=Mg, m2_=m2_, rt=rt, gi=gi, lb=lb):
                        if Mg == 0 and sub == 0:
                            t.wait_ge(dS.h, F_N1T)
                            t.wait_ge(vv.h, ffn_ev[-1])
                        t.wait_ge(dG.h, lb)
                        bank = gi % 4
                        if gi >= 4:
                            t.wait_ge(vv.h, uv_ev[gi - 4][0])
                            t.wait_ge(ss.h, uv_ev[gi - 4][1])
                        for k in range(DC):
                            mm = t.matmul(PS[bank][:, 0:512], minslot(Mg)[:, 256 * k + 128 * m2_:256 * k + 128 * (m2_ + 1)],
                                          n1T(k)[:, 512 * rt:512 * (rt + 1)], start=(k == 0), stop=(k == DC - 1))
                        pp.inc(mm, 1)
                    EN("tensor", _mm)
                    mmv = pp.n
                    if sub == 3:
                        uv_ld.append((Mg, mmv))

                    if Mglob < 16:
                        def _evu(v, Mglob=Mglob, rt=rt, gi=gi, mmv=mmv):
                            v.wait_ge(pp.h, mmv)
                            vv.inc(v.tensor_scalar(out=uch(Mglob)[:, 512 * rt:512 * (rt + 1)],
                                                   in0=PS[gi % 4][:, 0:512],
                                                   scalar1=convw[:, Mglob:Mglob + 1],
                                                   scalar2=convb[:, Mglob:Mglob + 1],
                                                   op0=ALU.mult, op1=ALU.add), 1)
                        EN("vector", _evu)
                        uv_ev.append((vv.n, ss.n))
                    else:
                        yc = Mglob - 16

                        def _evr(s, gi=gi, mmv=mmv):
                            s.wait_ge(pp.h, mmv)
                            s.wait_ge(vv.h, vv.n)
                            ss.inc(s.activation(out=silur[:, 0:512], in_=PS[gi % 4][:, 0:512], func=AF.Silu), 1)
                        EN("scalar", _evr)
                        mse = ss.n

                        def _evy(v, yc=yc, rt=rt, mse=mse):
                            v.wait_ge(ss.h, mse)
                            v.tensor_tensor(out=SA[:, 0:512], in0=silur[:, 0:512], in1=zbs[:, 512 * rt:512 * (rt + 1)],
                                            op=ALU.mult)
                            vv.inc(v.tensor_tensor(out=uch(yc)[:, 512 * rt:512 * (rt + 1)],
                                                   in0=SA[:, 0:512], in1=uch(yc)[:, 512 * rt:512 * (rt + 1)],
                                                   op=ALU.mult), 1)
                        EN("vector", _evy)
                        uv_ev.append((vv.n, ss.n))

            for Mg in range(8):
                emit_uv_block(Mg)
            F_U = vv.n
            CUT("F4-u")

            if debug:
                def _dbu(q):
                    q.wait_ge(vv.h, F_U)
                    for c2 in range(16):
                        dX.inc(q.dma_start(out=dbg["u"][r, 128 * c2:128 * (c2 + 1), :], in_=uch(c2), allow_slow_non_contiguous=True), 16)
                EN("sync", _dbu)

            for rt in range(2):
                def _zmm(t, rt=rt):
                    if rt == 0:
                        t.wait_ge(vv.h, F_U)
                    else:
                        t.wait_ge(ss.h, ss.n)
                    for k in range(16):
                        mm1 = t.matmul(PS[4][0:1, 0:512], xpB[:, k:k + 1], uch(k)[:, 512 * rt:512 * (rt + 1)],
                                       start=(k == 0), stop=(k == 15))
                    pp.inc(mm1, 1)
                    for k in range(16):
                        mm2 = t.matmul(PS[5][0:1, 0:512], xpC[:, k:k + 1], uch(k)[:, 512 * rt:512 * (rt + 1)],
                                       start=(k == 0), stop=(k == 15))
                    pp.inc(mm2, 1)
                EN("tensor", _zmm)
                mmv = pp.n

                def _zev(v, rt=rt, mmv=mmv):
                    v.wait_ge(pp.h, mmv)
                    v.tensor_copy(out=zzr[:, :], in_=PS[4][0:1, 0:512])
                    v.tensor_tensor(out=zzr[:, :], in0=zzr[:, :], in1=PS[5][0:1, 0:512], op=ALU.mult)
                    vv.inc(v.tensor_copy(out=zrow[0:1, 512 * rt:512 * (rt + 1)], in_=zzr[:, :]), 1)
                EN("vector", _zev)
                mze = vv.n

                def _zbc(t, rt=rt, mze=mze):
                    t.wait_ge(vv.h, mze)
                    pp.inc(t.matmul(PS[4][:, 0:512], ones[:, :], zrow[0:1, 512 * rt:512 * (rt + 1)], start=True, stop=True), 1)
                EN("tensor", _zbc)
                mzb = pp.n

                def _zse(s, rt=rt, mzb=mzb):
                    s.wait_ge(pp.h, mzb)
                    ss.inc(s.copy(out=zbs[:, 512 * rt:512 * (rt + 1)], in_=PS[4][:, 0:512]), 1)
                EN("scalar", _zse)

            for Mg in range(8, 16):
                emit_uv_block(Mg)
            F_Y = vv.n
            CUT("F5-y")

            if debug:
                def _dby(q):
                    q.wait_ge(vv.h, F_Y)
                    for c2 in range(16):
                        dX.inc(q.dma_start(out=dbg["y"][r, 128 * c2:128 * (c2 + 1), :], in_=uch(c2), allow_slow_non_contiguous=True), 16)
                EN("sync", _dby)

            mosl = lambda i: GB2[:, 4096 + 1024 * (i % 2):4096 + 1024 * (i % 2) + 1024]
            at_ld = []
            at_ev = []
            tok2t = lambda st: GF0[:, 1024 * st:1024 * (st + 1)]
            for it in range(4):
                nd, mrg = it // 2, it % 2
                for sg in range(8):
                    def _lmo(q, it=it, nd=nd, sg=sg):
                        gl = it * 8 + sg
                        if gl == 0:
                            q.wait_ge(vv.h, F_Y)
                        if len(at_ld) >= 2:
                            q.wait_ge(pp.h, at_ld[-2])
                        dG.inc(q.dma_start(out=mosl(gl).rearrange("p (c m) -> p c m", c=2),
                                           in_=w_mout[256 * sg:256 * (sg + 1), 512 * nd:512 * (nd + 1)]
                                           .rearrange("(c p) m -> p c m", p=128)), 16)
                    EN("gpsimd", _lmo)
                    lmo = dG.n

                    def _amm(t, it=it, nd=nd, mrg=mrg, sg=sg, lmo=lmo):
                        gl = it * 8 + sg
                        t.wait_ge(dG.h, lmo)
                        if gl == 0:
                            t.wait_ge(vv.h, F_Y)
                        if sg == 0:
                            for j in range(4):
                                g = 4 * it + j
                                if g >= 7:
                                    t.wait_ge(vv.h, at_ev[g - 7])
                        for kk in range(2):
                            kc = 2 * sg + kk
                            for j in range(4):
                                Mr = mrg * 4 + j
                                bank = (4 * it + j) % 7
                                mm = t.matmul(PS[bank][:, 0:512], uch(kc)[:, 128 * Mr:128 * (Mr + 1)],
                                              mosl(gl)[:, 512 * kk:512 * (kk + 1)],
                                              start=(sg == 0 and kk == 0), stop=(sg == 7 and kk == 1))
                        pp.inc(mm, 1)
                    EN("tensor", _amm)
                    at_ld.append(pp.n)
                mmv = pp.n

                for j in range(4):
                    def _aev(v, it=it, nd=nd, mrg=mrg, j=j, mmv=mmv):
                        v.wait_ge(pp.h, mmv)
                        if it == 0 and j == 0:
                            v.wait_ge(ss.h, ss.n)
                        Mr = mrg * 4 + j
                        bank = (4 * it + j) % 7
                        vv.inc(v.tensor_tensor(out=tok2t(Mr)[:, 512 * nd:512 * (nd + 1)],
                                               in0=TOKB[:, 1024 * Mr + 512 * nd:1024 * Mr + 512 * nd + 512],
                                               in1=PS[bank][:, 0:512], op=ALU.add), 1)
                        at_ev.append(vv.n)
                    EN("vector", _aev)
            F_T2 = vv.n
            CUT("F6-attn")

            if debug:
                def _dbt2(q):
                    q.wait_ge(vv.h, F_T2)
                    for st in range(NSEL):
                        dX.inc(q.dma_start(out=dbg["tok2"][r, 128 * st:128 * (st + 1), :], in_=tok2t(st), allow_slow_non_contiguous=True), 16)
                EN("sync", _dbt2)

            f7s = lambda st, a, b: lgs[:, 12 * st + a:12 * st + b]
            f7v0 = vv.n

            def _n2a(v):
                for st in range(NSEL):
                    v.reduce_sum(f7s(st, 0, 1), tok2t(st), axis=AX)
                    vv.inc(v.tensor_scalar(out=f7s(st, 1, 2), in0=f7s(st, 0, 1), scalar1=1.0 / D,
                                           scalar2=None, op0=ALU.mult), 1)
            EN("vector", _n2a)
            f7a = [f7v0 + st + 1 for st in range(NSEL)]

            def _n2b(s):
                for st in range(NSEL):
                    s.wait_ge(vv.h, f7a[st])
                    ss.inc(s.activation(out=SA[:, 0:1024], in_=tok2t(st), func=AF.Square,
                                        accum_out=f7s(st, 2, 3)), 1)
            EN("scalar", _n2b)
            f7b = [ss.n - NSEL + st + 1 for st in range(NSEL)]

            def _n2c(v):
                for st in range(NSEL):
                    v.wait_ge(ss.h, f7b[st])
                    v.tensor_scalar(out=f7s(st, 3, 4), in0=f7s(st, 2, 3), scalar1=1.0 / D, scalar2=None, op0=ALU.mult)
                    v.tensor_tensor(out=f7s(st, 4, 5), in0=f7s(st, 1, 2), in1=f7s(st, 1, 2), op=ALU.mult)
                    v.tensor_tensor(out=f7s(st, 5, 6), in0=f7s(st, 3, 4), in1=f7s(st, 4, 5), op=ALU.subtract)
                    vv.inc(v.tensor_scalar(out=f7s(st, 5, 6), in0=f7s(st, 5, 6), scalar1=EPS, scalar2=None, op0=ALU.add), 1)
            EN("vector", _n2c)
            f7c = [vv.n - NSEL + st + 1 for st in range(NSEL)]

            def _n2d(s):
                for st in range(NSEL):
                    s.wait_ge(vv.h, f7c[st])
                    s.activation(out=f7s(st, 6, 7), in_=f7s(st, 5, 6), func=AF.Ln)
                    ss.inc(s.activation(out=f7s(st, 7, 8), in_=f7s(st, 6, 7), func=AF.Exp, scale=-0.5), 1)
            EN("scalar", _n2d)
            f7d = [ss.n - NSEL + st + 1 for st in range(NSEL)]

            f7spl = []
            for st in range(NSEL):
                def _n2e(v, st=st):
                    v.wait_ge(ss.h, f7d[st])
                    if st >= 2:
                        v.wait_ge(dX.h, f7spl[st - 2])
                    sa = SA[:, 1024 * (st % 2):1024 * (st % 2) + 1024]
                    v.tensor_tensor(out=f7s(st, 8, 9), in0=f7s(st, 1, 2), in1=f7s(st, 7, 8), op=ALU.mult)
                    v.tensor_scalar(out=sa, in0=tok2t(st), scalar1=f7s(st, 7, 8), scalar2=f7s(st, 8, 9),
                                    op0=ALU.mult, op1=ALU.subtract)
                    v.tensor_tensor(out=sa, in0=sa, in1=lnw2B[:, :], op=ALU.mult)
                    vv.inc(v.tensor_tensor(out=sa, in0=sa, in1=lnb2B[:, :], op=ALU.add), 1)
                EN("vector", _n2e)
                m5 = vv.n

                def _n2s(q, st=st, m5=m5):
                    q.wait_ge(vv.h, m5)
                    dX.inc(q.dma_start(out=n2d[128 * st:128 * (st + 1), :],
                                       in_=SA[:, 1024 * (st % 2):1024 * (st % 2) + 1024]), 16)
                EN("sync", _n2s)
                f7spl.append(dX.n)
            F_N2 = dX.n

            if debug:
                def _dbn2(q):
                    q.wait_ge(dX.h, F_N2)
                    dX.inc(q.dma_start(out=dbg["n2"][r], in_=n2d[:, :], allow_slow_non_contiguous=True), 16)
                EN("sync", _dbn2)

            def _n2t(s):
                s.wait_ge(dX.h, F_N2)
                for c in range(DC):
                    dS.inc(s.dma_start(out=TOKB[:, 1024 * c:1024 * (c + 1)], in_=n2d[:, 128 * c:128 * (c + 1)],
                                       transpose=True), 16)
            EN("scalar", _n2t)
            F_N2T = dS.n
            CUT("F7-n2")
            n2T = lambda c: TOKB[:, 1024 * c:1024 * (c + 1)]

            rout_l1 = []
            f8v0 = vv.n
            f8s0 = ss.n
            for st in range(NSEL):
                par = st % 4
                gps = PS[[5, 6, 4, 7][par]][:, 8:16]
                glt = lgt[:, 8 * par:8 * par + 8]
                gls = lambda a, b, par=par: lgs[:, 48 * par + a:48 * par + b]

                def _rmm(t, st=st, gps=gps):
                    if st < 4:
                        t.wait_ge(dS.h, F_N2T)
                        t.wait_ge(vv.h, f8v0)
                        t.wait_ge(ss.h, f8s0)
                    else:
                        t.wait_ge(vv.h, rout_l1[st - 4])
                    for k in range(DC):
                        t.matmul(gps, n2T(k)[:, 128 * st:128 * (st + 1)], mrw[:, 8 * k:8 * (k + 1)],
                                 start=(k == 0), stop=False)
                    t.matmul(gps, ones[:, :], mrbh[0:1, 0:8], start=False, stop=False)
                    pp.inc(t.matmul(gps, ones[:, :], mrbh[0:1, 8:16], start=False, stop=True), 1)
                EN("tensor", _rmm)
                mmv = pp.n

                def _r1(v, st=st, mmv=mmv, gps=gps, glt=glt, gls=gls):
                    v.wait_ge(pp.h, mmv)
                    v.tensor_copy(out=glt, in_=gps)
                    v.max(out=gls(8, 16), in_=glt)
                    vv.inc(v.tensor_scalar(out=gls(0, 1), in0=gls(8, 9), scalar1=-1.0, scalar2=None,
                                           op0=ALU.mult), 1)
                EN("vector", _r1)
                rout_l1.append(vv.n)
                m1 = vv.n

                def _r2(s, st=st, m1=m1, glt=glt, gls=gls):
                    s.wait_ge(vv.h, m1)
                    ss.inc(s.activation(out=gls(16, 24), in_=glt, func=AF.Exp, bias=gls(0, 1)), 1)
                EN("scalar", _r2)
                m2 = ss.n

                def _r3(v, st=st, m2=m2, glt=glt, gls=gls):
                    v.wait_ge(ss.h, m2)
                    v.tensor_scalar(out=gls(24, 32), in0=glt, scalar1=gls(9, 10), scalar2=None, op0=ALU.is_ge)
                    v.tensor_tensor(out=gls(32, 40), in0=gls(16, 24), in1=gls(24, 32), op=ALU.mult)
                    v.reduce_sum(gls(1, 2), gls(32, 40), axis=AX)
                    v.reciprocal(out=gls(2, 3), in_=gls(1, 2))
                    vv.inc(v.tensor_scalar(out=gatet[:, 8 * st:8 * (st + 1)], in0=gls(32, 40), scalar1=gls(2, 3),
                                           scalar2=None, op0=ALU.mult), 1)
                EN("vector", _r3)
            F_GATE = vv.n
            CUT("F8-router")

            if debug:
                def _dbgt(q):
                    q.wait_ge(vv.h, F_GATE)
                    dX.inc(q.dma_start(out=dbg["gate"][r].rearrange("(st p) e -> p st e", p=128), in_=gatet[:, :].rearrange("p (st e) -> p st e", st=8), allow_slow_non_contiguous=True), 16)
                EN("sync", _dbgt)

            w1sl = lambda i: GB2[:, 2048 * (i % 2):2048 * (i % 2) + 2048]
            w2sl = lambda i: GB2[:, 4096 + 2048 * (i % 2):4096 + 2048 * (i % 2) + 2048]
            h1g = lambda g, kc: GB0[:, 16384 * g + 512 * kc:16384 * g + 512 * (kc + 1)]
            moe_w1ld = []
            moe_h1ev = []
            moe_eomm = []
            moe_eoev = []
            moe_b2rdy = []
            eo_prev_base = None
            for e in range(E):
                def _lb2(q, e=e):
                    if e >= 2:
                        q.wait_ge(pp.h, moe_b2rdy[e - 2])
                    dG.inc(q.dma_start(out=b2row[0:1, 1024 * (e % 2):1024 * (e % 2) + 1024],
                                       in_=b2_moe[e:e + 1, :]), 16)
                EN("gpsimd", _lb2)
                b2ld = dG.n

                for Mg in range(16):
                    def _lw1(q, e=e, Mg=Mg):
                        gl = e * 16 + Mg
                        if gl == 0:
                            q.wait_ge(pp.h, at_ld[-1])
                        if len(moe_w1ld) >= 2:
                            q.wait_ge(pp.h, moe_w1ld[-2])
                        dG.inc(q.dma_start(out=w1sl(gl).rearrange("p (c m) -> p c m", c=8),
                                           in_=w1_moe[e][:, 256 * Mg:256 * (Mg + 1)]
                                           .rearrange("(c p) m -> p c m", p=128)), 16)
                    EN("gpsimd", _lw1)
                    lw1 = dG.n

                    for sub in range(4):
                        g, m2_ = sub // 2, sub % 2
                        gi = (e * 16 + Mg) * 4 + sub
                        Mh = 2 * Mg + m2_

                        def _hmm(t, e=e, Mg=Mg, g=g, m2_=m2_, gi=gi, lw1=lw1, sub=sub):
                            if sub == 0:
                                t.wait_ge(dG.h, lw1)
                            if gi == 0:
                                t.wait_ge(vv.h, F_GATE)
                            if e > 0 and Mg == 0 and sub == 0:
                                t.wait_ge(vv.h, eo_prev_base + 4)
                            bank = gi % 4
                            if gi >= 4:
                                t.wait_ge(ss.h, moe_h1ev[gi - 4])
                            gl = e * 16 + Mg
                            for k in range(DC):
                                mm = t.matmul(PS[bank][:, 0:512],
                                              w1sl(gl)[:, 256 * k + 128 * m2_:256 * k + 128 * (m2_ + 1)],
                                              n2T(k)[:, 512 * g:512 * (g + 1)], start=(k == 0), stop=(k == DC - 1))
                            pp.inc(mm, 1)
                        EN("tensor", _hmm)
                        mmv = pp.n
                        if sub == 3:
                            moe_w1ld.append(mmv)

                        def _hev(s, e=e, g=g, Mh=Mh, gi=gi, mmv=mmv):
                            s.wait_ge(pp.h, mmv)
                            if gi == 0:
                                s.wait_ge(dS.h, F_N2T)
                                s.wait_ge(pp.h, E_FFNMM)
                            ss.inc(s.activation(out=h1g(g, Mh), in_=PS[gi % 4][:, 0:512],
                                                func=AF.Gelu, bias=b1sb[:, 32 * e + Mh:32 * e + Mh + 1]), 1)
                        EN("scalar", _hev)
                        moe_h1ev.append(ss.n)
                H1_DONE = ss.n

                for g in range(2):
                    for j in range(16):
                        glw = len(moe_eomm)

                        def _lw2(q, e=e, j=j, glw=glw):
                            if glw == 0:
                                q.wait_ge(pp.h, at_ld[-1])
                            if glw >= 2:
                                q.wait_ge(pp.h, moe_eomm[glw - 2])
                            sW2.inc(glw, q.dma_start(out=w2sl(glw).rearrange("p (c m) -> p c m", c=2),
                                                     in_=w2_moe[e][256 * j:256 * (j + 1), :]
                                                     .rearrange("(c p) m -> p c m", p=128)), 16)
                        EN("gpsimd", _lw2)
                        lw2 = (glw, sW2.val(glw))

                        def _eomm(t, e=e, g=g, j=j, glw=glw, lw2=lw2, b2ld=b2ld, prev=eo_prev_base):
                            t.wait_ge(sW2.sem(lw2[0]).h, lw2[1])
                            if j == 0:
                                t.wait_ge(ss.h, H1_DONE)
                                if prev is None:
                                    t.wait_ge(vv.h, F_GATE)
                                if g == 0:
                                    t.wait_ge(dG.h, b2ld)
                            for c in range(2):
                                kc = 2 * j + c
                                for Mr in range(4):
                                    for dh in range(2):
                                        bank = 2 * Mr + dh
                                        pv = PS[bank][:, 0:512]
                                        if j == 0 and c == 0:
                                            if prev is not None:
                                                t.wait_ge(vv.h, prev + bank + 1)
                                            t.matmul(pv, ones[:, :],
                                                     b2row[0:1, 1024 * (e % 2) + 512 * dh:1024 * (e % 2) + 512 * dh + 512],
                                                     start=True, stop=False)
                                        mm = t.matmul(pv, h1g(g, kc)[:, 128 * Mr:128 * (Mr + 1)],
                                                      w2sl(glw)[:, 1024 * c + 512 * dh:1024 * c + 512 * (dh + 1)],
                                                      start=False, stop=(j == 15 and c == 1))
                            pp.inc(mm, 1)
                        EN("tensor", _eomm)
                        moe_eomm.append(pp.n)
                    mmv = pp.n

                    eo_prev_base = vv.n
                    for Mr in range(4):
                        for dh in range(2):
                            def _eoev(v, e=e, g=g, Mr=Mr, dh=dh, mmv=mmv):
                                v.wait_ge(pp.h, mmv)
                                bank = 2 * Mr + dh
                                st = 4 * g + Mr
                                acc = GF0[:, 1024 * st + 512 * dh:1024 * st + 512 * dh + 512]
                                vv.inc(v.scalar_tensor_tensor(out=acc, in0=PS[bank][:, 0:512],
                                                              scalar=gatet[:, 8 * st + e:8 * st + e + 1],
                                                              in1=acc, op0=ALU.mult, op1=ALU.add), 1)
                            EN("vector", _eoev)
                    moe_eoev.append(vv.n)
                moe_b2rdy.append(pp.n)
            F_MOE = vv.n
            CUT("F9-moe")

            if debug:
                def _dbt3(q):
                    q.wait_ge(vv.h, F_MOE)
                    for st in range(NSEL):
                        dX.inc(q.dma_start(out=dbg["tok3"][r, 128 * st:128 * (st + 1), :], in_=tok2t(st), allow_slow_non_contiguous=True), 16)
                EN("sync", _dbt3)

            def _tfs(q):
                q.wait_ge(vv.h, F_MOE)
                dG.inc(q.dma_start(out=tokfd.rearrange("(st p) d -> p st d", p=128), in_=GF0[:, :].rearrange("p (st d) -> p st d", st=8)), 16)
            EN("gpsimd", _tfs)
            G_TF = dG.n

            ssel = lambda i: GB2[:, 1024 * (i % 2):1024 * (i % 2) + 1024]
            hirl = lambda i: GB2[:, 2048 + 1024 * (i % 2):2048 + 1024 * (i % 2) + 1024]
            grl = lambda i: GB2[:, 4096 + 1024 * (i % 2):4096 + 1024 * (i % 2) + 1024]
            xrl = lambda i: GF0[:, 1024 * (i % 2):1024 * (i % 2) + 1024]
            orl = lambda i: GF0[:, 2048 + 1024 * (i % 2):2048 + 1024 * (i % 2) + 1024]
            cp_ev = []
            for tt in range(NT):
                def _gsl(q, tt=tt):
                    if tt == 0:
                        q.wait_ge(dG.h, G_TF)
                        q.wait_ge(pp.h, moe_b2rdy[-1])
                    if tt >= 2:
                        q.wait_ge(vv.h, cp_ev[tt - 2])
                    dG.inc(q.indirect_dma_start(
                        out=ssel(tt), out_offset=None,
                        in_=tokfd[0:1, :], in_offset=bass.IndirectOffsetOnAxis(ogat[:, tt:tt + 1], 0)), 16)
                EN("gpsimd", _gsl)
                gsl = dG.n

                def _xrl(q, tt=tt):
                    if tt == 0:
                        q.wait_ge(dG.h, G_TF)
                        q.wait_ge(dX.h, E_G)
                    if tt >= 2:
                        q.wait_ge(vv.h, cp_ev[tt - 2])
                    dX.inc(q.dma_start(out=xrl(tt), in_=xrow[128 * tt:128 * (tt + 1), :]), 16)
                    dX.inc(q.dma_start(out=hirl(tt), in_=hid[128 * tt:128 * (tt + 1), :]), 16)
                    dX.inc(q.dma_start(out=grl(tt), in_=gd[128 * tt:128 * (tt + 1), :]), 16)
                EN("sync", _xrl)
                xld = dX.n

                def _cmp(v, tt=tt, gsl=gsl, xld=xld):
                    v.wait_ge(dG.h, gsl)
                    v.wait_ge(dX.h, xld)
                    v.tensor_scalar(out=SA[:, 0:1024], in0=ssel(tt), scalar1=mwT[:, tt:tt + 1], scalar2=None,
                                    op0=ALU.mult)
                    v.scalar_tensor_tensor(out=SA[:, 1024:2048], in0=hirl(tt), scalar=nmT[:, tt:tt + 1],
                                           in1=SA[:, 0:1024], op0=ALU.mult, op1=ALU.add)
                    v.tensor_tensor(out=SA[:, 0:1024], in0=SA[:, 1024:2048], in1=grl(tt), op=ALU.mult)
                    vv.inc(v.tensor_tensor(out=orl(tt), in0=xrl(tt), in1=SA[:, 0:1024], op=ALU.add), 1)
                EN("vector", _cmp)
                cpm = vv.n

                def _ost(q, tt=tt, cpm=cpm):
                    q.wait_ge(vv.h, cpm)
                    dX.inc(q.dma_start(out=outrow[128 * tt:128 * (tt + 1), :], in_=orl(tt)), 16)
                EN("sync", _ost)
                ostm = dX.n

                def _cpw(v, ostm=ostm):
                    v.wait_ge(dX.h, ostm)
                    vv.inc(v.engine_nop(), 1)
                EN("vector", _cpw)
                cp_ev.append(vv.n)

            jx, jg, js2, jv, jsc, jp = dX.n, dG.n, dS.n, vv.n, ss.n, pp.n

            def _j(q):
                q.wait_ge(dX.h, jx); q.wait_ge(dG.h, jg); q.wait_ge(dS.h, js2)
                q.wait_ge(vv.h, jv); q.wait_ge(ss.h, jsc); q.wait_ge(pp.h, jp)
            EN("sync", _j)

            def _jg(q):
                q.wait_ge(dX.h, jx); q.wait_ge(dG.h, jg); q.wait_ge(dS.h, js2)
                q.wait_ge(vv.h, jv); q.wait_ge(ss.h, jsc); q.wait_ge(pp.h, jp)
            EN("gpsimd", _jg)

            def _jv(v):
                v.wait_ge(dX.h, jx); v.wait_ge(dG.h, jg); v.wait_ge(dS.h, js2)
                v.wait_ge(ss.h, jsc); v.wait_ge(pp.h, jp)
                vv.inc(v.engine_nop(), 1)
            EN("vector", _jv)
            rowjoin_v = vv.n
            CUT("G-compose")

            def _js(sx):
                sx.wait_ge(dX.h, jx); sx.wait_ge(dG.h, jg); sx.wait_ge(dS.h, js2)
                sx.wait_ge(vv.h, rowjoin_v); sx.wait_ge(pp.h, jp)
            EN("scalar", _js)

            def _jt(t):
                t.wait_ge(dX.h, jx); t.wait_ge(dG.h, jg); t.wait_ge(dS.h, js2)
                t.wait_ge(vv.h, rowjoin_v); t.wait_ge(ss.h, jsc)
            EN("tensor", _jt)

      except StopEmission as e:
        print("EMISSION CUT AT:", e)
      finally:
        jx, jg, js2, jv, jsc, jp = dX.n, dG.n, dS.n, vv.n, ss.n, pp.n

        def _fj(q):
            q.wait_ge(dX.h, jx); q.wait_ge(dG.h, jg); q.wait_ge(dS.h, js2)
            q.wait_ge(vv.h, jv); q.wait_ge(ss.h, jsc); q.wait_ge(pp.h, jp)
            q.wait_ge(sSP.h, sSP.n); q.wait_ge(sSC.h, sSC.n)
            for _sl in (sxr, sW2):
                for _s in _sl.s:
                    q.wait_ge(_s.h, _s.n)
        EN("sync", _fj)
    ctx.close()
    return nc, dbg



def _consts():
    q = np.arange(128)
    return {
        "cONES": np.ones((1, 128), dtype=ml_dtypes.bfloat16),
        "cONESC": np.ones((128, 1), dtype=ml_dtypes.bfloat16),
        "cTRIle": (q[:, None] <= q[None, :]).astype(ml_dtypes.bfloat16),
        "cTRIst": (q[None, :] < q[:, None]).astype(ml_dtypes.bfloat16),
        "cID": np.eye(128, dtype=ml_dtypes.bfloat16),
    }


_NC_CACHE = {}


def _get_nc(debug=False):
    if debug not in _NC_CACHE:
        _NC_CACHE[debug] = build_nc(debug)
    return _NC_CACHE[debug]


def kernel(**inputs):
    nc, _ = _get_nc(False)
    consts = _consts()
    x = np.ascontiguousarray(np.asarray(inputs["L0_attn_x"], dtype=np.float32))
    B = x.shape[0]
    percore = B // 8
    wmap = {
        "m_in_w": "L0_attn_m_in_w", "m_conv_w": "L0_attn_m_conv_w", "m_conv_b": "L0_attn_m_conv_b",
        "m_xproj_w": "L0_attn_m_xproj_w", "m_out_w": "L0_attn_m_out_w",
        "modr_w1": "L0_attn_modr_w1", "modr_b1": "L0_attn_modr_b1", "modr_w2": "L0_attn_modr_w2",
        "modr_b2": "L0_attn_modr_b2",
        "ln1_w": "L0_attn_ln1_w", "ln1_b": "L0_attn_ln1_b", "ln2_w": "L0_attn_ln2_w", "ln2_b": "L0_attn_ln2_b",
        "moe_r_w": "L0_attn_moe_r_w", "moe_r_b": "L0_attn_moe_r_b",
        "moe_w1": "L0_attn_moe_w1", "moe_b1": "L0_attn_moe_b1",
        "moe_w2": "L0_attn_moe_w2", "moe_b2": "L0_attn_moe_b2",
        "gate_w": "L0_ffn_gate_w", "gate_b": "L0_ffn_gate_b",
        "expert_w": "L0_ffn_expert_w", "expert_b": "L0_ffn_expert_b",
    }
    base = {k: np.ascontiguousarray(np.asarray(inputs[v], dtype=np.float32)) for k, v in wmap.items()}
    f32c = lambda a_: np.ascontiguousarray(np.asarray(a_, dtype=np.float32))
    base["h_conv3"] = f32c(base.pop("m_conv_w")[:, 3].reshape(16, 128).T)
    base["h_convb"] = f32c(base.pop("m_conv_b").reshape(16, 128).T)
    base["h_r1b"] = f32c(base.pop("modr_b1").reshape(2, 128).T)
    base["h_r2"] = f32c(base.pop("modr_w2").reshape(2, 128).T)
    base["h_b1"] = f32c(base.pop("moe_b1").reshape(8, 32, 128).transpose(2, 0, 1).reshape(128, 256))
    base["h_mrw"] = f32c(base.pop("moe_r_w").reshape(8, 128, 8).transpose(1, 0, 2).reshape(128, 64))
    base["h_gw"] = f32c(base.pop("gate_w").reshape(8, 128, 8).transpose(1, 0, 2).reshape(128, 64))
    base.update(consts)
    in_maps = []
    for c in range(8):
        m = dict(base)
        m["xin"] = np.ascontiguousarray(x[c * percore:(c + 1) * percore].reshape(percore * S, D))
        in_maps.append(m)
    res = run_bass_kernel_spmd(nc, in_maps, core_ids=list(range(8)))
    outs = [res.results[c]["out"].reshape(percore, S, D) for c in range(8)]
    return np.concatenate(outs, axis=0)


# version 11 (final)
# speedup vs baseline: 1.2135x; 1.0782x over previous
import numpy as np
import ml_dtypes
import concourse.bass as bass
import concourse.mybir as mybir
from concourse.bass_utils import run_bass_kernel_spmd

F32 = mybir.dt.float32
BF16 = mybir.dt.bfloat16
I32 = mybir.dt.int32
AF = mybir.ActivationFunctionType
ALU = mybir.AluOpType
AX = mybir.AxisListType.X

R = 4
S = 2048
D = 1024
SEL = 1024
IN2 = 2048
HM = 4096
E = 8
NT = S // 128
NSEL = SEL // 128
DC = D // 128
EPS = 1e-5


_GUARD_OPS = {
    "tensor_tensor", "tensor_add", "tensor_sub", "tensor_mul", "tensor_max",
    "tensor_scalar", "tensor_single_scalar", "tensor_scalar_add", "tensor_scalar_sub",
    "tensor_scalar_mul", "tensor_scalar_max", "tensor_scalar_min", "tensor_relu",
    "tensor_reduce", "reduce_max", "reduce_sum", "scalar_tensor_tensor",
    "tensor_copy", "memset", "memzero", "reciprocal", "max", "max_index",
    "select", "copy_predicated", "activation", "copy", "add", "mul", "square",
    "sqrt", "sign",
}
_OUT_KW = ("out", "accum_out", "out_max", "out_indices", "dest", "ap")


class Guard:

    def __init__(self, eng):
        self._e = eng
        self._dirty = {}

    @staticmethod
    def _rng(a):
        t = getattr(a, "tensor", None)
        nm = getattr(t, "name", None)
        if nm is None:
            return None
        try:
            pat = a.ap
            bp = int(a.base_partition())
            off = int(a.offset) - bp * int(pat[0][0])
            npart = int(pat[0][1])
            span = 1
            for st, n in pat[1:]:
                span += (int(n) - 1) * int(st)
            if off < 0:
                return (nm, 0, 1 << 30, 0, 1 << 30)
            return (nm, bp, bp + npart, off, off + span)
        except Exception:
            return (nm, 0, 1 << 30, 0, 1 << 30)

    def __getattr__(self, k):
        fn = getattr(self._e, k)
        if k not in _GUARD_OPS:
            return fn

        def wrap(*args, **kw):
            outs, ins = [], []
            for i, a_ in enumerate(args):
                rg = Guard._rng(a_)
                if rg is None:
                    continue
                (outs if i == 0 else ins).append(rg)
            for kk, a_ in kw.items():
                rg = Guard._rng(a_)
                if rg is None:
                    continue
                (outs if kk in _OUT_KW else ins).append(rg)
            hit = False
            for nm, p0, p1, c0, c1 in ins:
                for (q0, q1, d0, d1) in self._dirty.get(nm, ()):
                    if p0 < q1 and q0 < p1 and c0 < d1 and d0 < c1:
                        hit = True
                        break
                if hit:
                    break
            if hit:
                self._e.drain()
                self._dirty.clear()
            r = fn(*args, **kw)
            for nm, p0, p1, c0, c1 in outs:
                lst = self._dirty.setdefault(nm, [])
                lst.append((p0, p1, c0, c1))
                if len(lst) > 48:
                    del lst[:]
                    lst.append((0, 1 << 30, 0, 1 << 30))
            return r

        return wrap


class Sem:
    def __init__(self, h):
        self.h = h
        self.n = 0

    def inc(self, instr, k):
        instr.then_inc(self.h, k)
        self.n += k
        return instr


class StopEmission(Exception):
    pass


class SlotSem:
    def __init__(self, mk, name, n=2):
        self.s = [mk(name + str(i)) for i in range(n)]
        self.nslots = n

    def sem(self, i):
        return self.s[i % self.nslots]

    def inc(self, i, instr, k=16):
        return self.s[i % self.nslots].inc(instr, k)

    def val(self, i):
        return self.s[i % self.nslots].n


def build_nc(debug=False, maxphase=999):
    import contextlib
    nc = bass.Bass("TRN2", target_bir_lowering=False)
    ctx = contextlib.ExitStack()

    _phc = [0]

    def CUT(label=""):
        _phc[0] += 1
        if _phc[0] > maxphase:
            raise StopEmission(label)

    dti = lambda name, shape, dtype=F32: nc.dram_tensor(name, shape, dtype, kind="ExternalInput").ap()
    xin = dti("xin", [R * S, D])
    w_m_in = dti("m_in_w", [D, 2 * IN2])
    h_conv3 = dti("h_conv3", [128, 16])
    h_convb = dti("h_convb", [128, 16])
    w_xp = dti("m_xproj_w", [IN2, 33])
    w_mout = dti("m_out_w", [IN2, D])
    w_r1 = dti("modr_w1", [D, 256])
    b_r2 = dti("modr_b2", [1])
    h_r1b = dti("h_r1b", [128, 2])
    h_r2 = dti("h_r2", [128, 2])
    ln1w = dti("ln1_w", [D]); ln1b = dti("ln1_b", [D])
    ln2w = dti("ln2_w", [D]); ln2b = dti("ln2_b", [D])
    h_mrw = dti("h_mrw", [128, 64]); b_mr = dti("moe_r_b", [E])
    w1_moe = dti("moe_w1", [E, D, HM]); h_b1 = dti("h_b1", [128, 256])
    w2_moe = dti("moe_w2", [E, HM, D]); b2_moe = dti("moe_b2", [E, D])
    h_gw = dti("h_gw", [128, 64]); b_gate = dti("gate_b", [E])
    w_ffn = dti("expert_w", [E, D, D]); b_ffn = dti("expert_b", [E, D])
    cONES = nc.dram_tensor("cONES", [1, 128], BF16, kind="ExternalInput").ap()
    cONESC = nc.dram_tensor("cONESC", [128, 1], BF16, kind="ExternalInput").ap()
    cTRIle = nc.dram_tensor("cTRIle", [128, 128], BF16, kind="ExternalInput").ap()
    cTRIst = nc.dram_tensor("cTRIst", [128, 128], BF16, kind="ExternalInput").ap()
    cID = nc.dram_tensor("cID", [128, 128], BF16, kind="ExternalInput").ap()

    out = nc.dram_tensor("out", [R * S, D], F32, kind="ExternalOutput").ap()

    dbg = {}
    if debug:
        def DB(name, shape, dtype=F32):
            dbg[name] = nc.dram_tensor("dbg_" + name, shape, dtype, kind="ExternalOutput").ap()
        DB("scores", [R, S]); DB("rank", [R, S]); DB("mw", [R, S]); DB("c", [R, S])
        DB("wgt", [R, S, E]); DB("gate", [R, SEL, E]); DB("hid", [R, S, D], BF16)
        DB("tok", [R, SEL, D], BF16); DB("n1", [R, SEL, D], BF16); DB("n2", [R, SEL, D], BF16)
        DB("hrh", [R, 128, 4096], BF16); DB("hrm", [R, 128, 4096], BF16); DB("mid", [R, S, D], BF16)
        DB("w1s", [128, 4096], BF16); DB("hiT", [R, 128, 16384], BF16); DB("midT", [R, 128, 16384], BF16)
        DB("lnB", [4, 128, D], BF16); DB("wgt2", [R, 128, 128]); DB("wgt3", [R, 128, 128])
        DB("srh", [R, 1, 2048], BF16); DB("srm", [R, 1, 2048], BF16); DB("sT", [R, 128, 16])
        DB("u", [R, IN2, SEL], BF16); DB("y", [R, IN2, SEL], BF16)
        DB("tok2", [R, SEL, D]); DB("tok3", [R, SEL, D]); DB("g", [R, S, D], BF16)

    scr = lambda name, shape, dtype: nc.dram_tensor(name, shape, dtype, kind="Internal").ap()
    hid = scr("hid", [S, D], BF16)
    midd = scr("midd", [S, D], BF16)
    w2bf = scr("w2bf", [E, HM, D], BF16)
    n1d = scr("n1d", [SEL, D], BF16)
    n2d = scr("n2d", [SEL, D], BF16)
    tokd = scr("tokd", [SEL, D], BF16)
    tokfd = scr("tokfd", [SEL, D], BF16)
    gd = scr("gd", [S, D], BF16)

    sb = lambda name, shape, dtype: ctx.enter_context(nc.sbuf_tensor(name, shape, dtype))[:, :]

    GB0 = sb("GB0", [128, 32768], BF16)
    GB1 = sb("GB1", [128, 16384], BF16)
    GB2 = sb("GB2", [128, 8192], BF16)
    GF0 = sb("GF0", [128, 8192], F32)
    TOKB = sb("TOKB", [128, 8192], BF16)
    SA = sb("SA", [128, 2048], BF16)

    ones = sb("ones", [1, 128], BF16)
    onesc = sb("onesc", [128, 1], BF16)
    triLE = sb("triLE", [128, 128], BF16)
    triST = sb("triST", [128, 128], BF16)
    idmat = sb("idmat", [128, 128], BF16)
    lnw1B = sb("lnw1B", [128, D], BF16)
    lnb1B = sb("lnb1B", [128, D], BF16)
    lnw2B = sb("lnw2B", [128, D], BF16)
    lnb2B = sb("lnb2B", [128, D], BF16)
    w1hi = sb("w1hi", [128, 2048], BF16)
    w1lo = sb("w1lo", [128, 2048], BF16)
    w1f = GF0[:, 0:2048]
    w2hl = sb("w2hl", [128, 4], BF16)
    w2f = sb("w2f", [128, 2], F32)
    b1r = sb("b1r", [128, 2], F32)
    xpf = sb("xpf", [128, 33], F32)
    xpB = sb("xpB", [128, 16], BF16)
    xpC = sb("xpC", [128, 16], BF16)
    convw = sb("convw", [128, 16], F32)
    convb = sb("convb", [128, 16], F32)
    mrw = sb("mrw", [128, 64], BF16)
    mrbf = sb("mrbf", [1, 8], F32)
    mrbh = sb("mrbh", [1, 16], BF16)
    gwsb = sb("gwsb", [128, 64], BF16)
    gbf = sb("gbf", [1, 8], F32)
    r2bf = sb("r2bf", [1, 1], F32)
    r2bh = sb("r2bh", [1, 2], BF16)
    onesr = sb("onesr", [1, 512], BF16)
    gbh = sb("gbh", [1, 16], BF16)
    b1sb = sb("b1sb", [128, 256], F32)
    b2row = sb("b2row", [1, 2048], BF16)
    fbrow = sb("fbrow", [1, 1024], BF16)
    lnstg = GF0[0:1, 0:1024]
    lnsplit = SA[0:1, 0:2048]

    sT = sb("sT", [128, 16], F32)
    s12T = sb("s12T", [128, 16], F32)
    sTh = sb("sTh", [128, 16], BF16)
    sTm = sb("sTm", [128, 16], BF16)
    res16 = sb("res16", [128, 16], F32)
    srhi = sb("srhi", [1, 2048], BF16)
    srmid = sb("srmid", [1, 2048], BF16)
    srr = None
    rankT = sb("rankT", [128, 16], F32)
    maskT = sb("maskT", [128, 16], BF16)
    nmT = sb("nmT", [128, 16], BF16)
    mwT = sb("mwT", [128, 16], F32)
    eT = sb("eT", [128, 16], F32)
    erow = sb("erow", [128, 1], F32)
    erowb = sb("erowb", [128, 1], BF16)
    negm = sb("negm", [128, 1], F32)
    wz = sb("wz", [1, 4], F32)
    carry = sb("carry", [1, 48], F32)
    carryh = sb("carryh", [1, 32], BF16)
    cT = sb("cT", [128, 16], F32)
    ogat = sb("ogat", [128, 16], I32)
    osca = sb("osca", [128, 16], I32)
    oscf = sb("oscf", [128, 16], F32)
    stats = sb("stats", [128, 16], F32)
    zrow = sb("zrow", [1, 1024], BF16)
    zzr = sb("zzr", [1, 512], F32)
    zbs = sb("zbs", [128, 1024], BF16)
    silur = sb("silur", [128, 512], BF16)
    gatet = sb("gatet", [128, 64], F32)
    wgtt = sb("wgtt", [128, 128], F32)
    lgt = sb("lgt", [128, 32], F32)
    lgs = sb("lgs", [128, 192], F32)
    gtile = None

    gtile = zbs[:, 0:512]
    PS = [ctx.enter_context(nc.psum_tensor("PS%d" % i, [128, 512], F32)) for i in range(8)]

    sm = lambda name: Sem(ctx.enter_context(nc.semaphore(name)))
    dX = sm("dX")
    dG = sm("dG")
    dS = sm("dS")
    vv = sm("vv")
    ss = sm("ss")
    pp = sm("pp")
    sxr = SlotSem(sm, "sxr")
    sSP = sm("sSP")
    sSC = sm("sSC")
    sWe = SlotSem(sm, "sWe")
    sMI = SlotSem(sm, "sMI")
    sMO = SlotSem(sm, "sMO")
    sW1 = SlotSem(sm, "sW1")
    sW2 = SlotSem(sm, "sW2")
    dW2C = sm("dW2C")
    sB2 = SlotSem(sm, "sB2")
    sGA = SlotSem(sm, "sGA")
    sXR = SlotSem(sm, "sXR")

    blk = [None]

    guards = {}

    def EN(which, fn):
        if which in ("vector", "scalar"):
            getattr(blk[0], which)(lambda eng: fn(guards.setdefault(which, Guard(eng))))
        else:
            getattr(blk[0], which)(fn)

    with nc.Block() as block:
      try:
        blk[0] = block

        def _pro_sync(q):
            dX.inc(q.dma_start(out=ones[:, :], in_=cONES[:, :]), 16)
            dX.inc(q.dma_start(out=onesc[:, :], in_=cONESC[:, :]), 16)
            dX.inc(q.dma_start(out=triLE[:, :], in_=cTRIle[:, :]), 16)
            dX.inc(q.dma_start(out=triST[:, :], in_=cTRIst[:, :]), 16)
            dX.inc(q.dma_start(out=idmat[:, :], in_=cID[:, :]), 16)
            dX.inc(q.dma_start(out=w1f.rearrange("p (c m) -> p c m", c=8), in_=w_r1.rearrange("(c p) m -> p c m", p=128)), 16)
            dX.inc(q.dma_start(out=w2f[:, :], in_=h_r2[:, :]), 16)
            dX.inc(q.dma_start(out=b1r[:, :], in_=h_r1b[:, :]), 16)
            dX.inc(q.dma_start(out=convw[:, :], in_=h_conv3[:, :]), 16)
            dX.inc(q.dma_start(out=convb[:, :], in_=h_convb[:, :]), 16)
            dX.inc(q.dma_start(out=mrbf[:, :], in_=b_mr.rearrange("(o e) -> o e", o=1)), 16)
            dX.inc(q.dma_start(out=r2bf[:, :], in_=b_r2.rearrange("(o e) -> o e", o=1)), 16)
            dX.inc(q.dma_start(out=gbf[:, :], in_=b_gate.rearrange("(o e) -> o e", o=1)), 16)
            dX.inc(q.dma_start(out=b1sb[:, :], in_=h_b1[:, :]), 16)
        EN("sync", _pro_sync)
        pro_dX0 = dX.n

        BCREG = [None]

        def _pro_g(q):
            BCREG[0] = q.to_reg(SEL - 1)
            q.memset(onesr[:, :], 1.0)
            dG.inc(q.dma_start(out=mrw[:, :], in_=h_mrw[:, :]), 16)
            dG.inc(q.dma_start(out=gwsb[:, :], in_=h_gw[:, :]), 16)
        EN("gpsimd", _pro_g)
        pro_dG0 = dG.n

        def _pro_v(v):
            v.wait_ge(dX.h, pro_dX0)
            v.wait_ge(dG.h, pro_dG0)
            import os as _os2
            if "V" in _os2.environ.get("PSKIP", ""):
                vv.inc(v.tensor_copy(out=w1hi[:, 0:8], in_=w1f[:, 0:8]), 1)
                return
            v.tensor_copy(out=w1hi[:, :], in_=w1f[:, :])
            v.tensor_tensor(out=w1f[:, :], in0=w1f[:, :], in1=w1hi[:, :], op=ALU.subtract)
            v.tensor_copy(out=w1lo[:, :], in_=w1f[:, :])
            v.tensor_copy(out=w2hl[:, 0:2], in_=w2f[:, :])
            v.tensor_tensor(out=w2f[:, :], in0=w2f[:, :], in1=w2hl[:, 0:2], op=ALU.subtract)
            v.tensor_copy(out=w2hl[:, 2:4], in_=w2f[:, :])
            v.tensor_copy(out=mrbh[:, 0:8], in_=mrbf[:, :])
            v.tensor_tensor(out=mrbf[:, :], in0=mrbf[:, :], in1=mrbh[:, 0:8], op=ALU.subtract)
            v.tensor_copy(out=mrbh[:, 8:16], in_=mrbf[:, :])
            v.tensor_copy(out=r2bh[:, 0:1], in_=r2bf[:, :])
            v.tensor_tensor(out=r2bf[:, :], in0=r2bf[:, :], in1=r2bh[:, 0:1], op=ALU.subtract)
            v.tensor_copy(out=r2bh[:, 1:2], in_=r2bf[:, :])
            v.tensor_copy(out=gbh[:, 0:8], in_=gbf[:, :])
            v.tensor_tensor(out=gbf[:, :], in0=gbf[:, :], in1=gbh[:, 0:8], op=ALU.subtract)
            vv.inc(v.tensor_copy(out=gbh[:, 8:16], in_=gbf[:, :]), 1)
        EN("vector", _pro_v)

        import os as _os
        _SKIP = _os.environ.get("PSKIP", "")
        for li, (src, dst) in enumerate([] if "L" in _SKIP else [(ln1w, lnw1B), (ln1b, lnb1B), (ln2w, lnw2B), (ln2b, lnb2B)]):
            def _ld(q, src=src, li=li):
                q.wait_ge(vv.h, vv.n)
                dX.inc(q.dma_start(out=lnstg[:, :], in_=src.rearrange("(o n) -> o n", o=1)), 16)
            EN("sync", _ld)
            ld = dX.n

            def _sp(v, ld=ld, ppn=pp.n):
                v.wait_ge(dX.h, ld)
                v.wait_ge(pp.h, ppn)
                v.tensor_copy(out=lnsplit[0:1, 0:1024], in_=lnstg[:, :])
                v.tensor_tensor(out=lnstg[:, :], in0=lnstg[:, :], in1=lnsplit[0:1, 0:1024], op=ALU.subtract)
                vv.inc(v.tensor_copy(out=lnsplit[0:1, 1024:2048], in_=lnstg[:, :]), 1)
            EN("vector", _sp)
            spm = vv.n

            for half in range(2):
                def _mm(t, half=half, spm=spm, ssn=ss.n):
                    t.wait_ge(vv.h, spm)
                    t.wait_ge(ss.h, ssn)
                    t.matmul(PS[6][:, 0:512], ones[:, :], lnsplit[0:1, 512 * half:512 * (half + 1)],
                             start=True, stop=False)
                    pp.inc(t.matmul(PS[6][:, 0:512], ones[:, :],
                                    lnsplit[0:1, 1024 + 512 * half:1024 + 512 * (half + 1)],
                                    start=False, stop=True), 1)
                EN("tensor", _mm)
                mmv = pp.n

                def _ev(s, half=half, dst=dst, mmv=mmv):
                    s.wait_ge(pp.h, mmv)
                    ss.inc(s.copy(out=dst[:, 512 * half:512 * (half + 1)], in_=PS[6][:, 0:512]), 1)
                EN("scalar", _ev)

        for c in ([] if "X" in _SKIP else range(16)):
            def _ld(q, c=c):
                q.wait_ge(vv.h, vv.n)
                dX.inc(q.dma_start(out=xpf[:, :], in_=w_xp[128 * c:128 * (c + 1), :]), 16)
            EN("sync", _ld)
            ld = dX.n

            def _pr(v, c=c, ld=ld):
                v.wait_ge(dX.h, ld)
                v.reduce_sum(stats[:, 0:1], xpf[:, 16:32], axis=AX)
                v.tensor_copy(out=xpB[:, c:c + 1], in_=stats[:, 0:1])
                vv.inc(v.tensor_copy(out=xpC[:, c:c + 1], in_=xpf[:, 32:33]), 1)
            EN("vector", _pr)

        PRO_V = vv.n
        CUT("prologue")
        PRO_S = ss.n

        for r in range(R):
            xrow = xin[r * S:(r + 1) * S, :]
            outrow = out[r * S:(r + 1) * S, :]

            hiT = lambda c: GB0[:, 2048 * c:2048 * (c + 1)]
            midT = lambda c: GB0[:, 16384 + 2048 * c:16384 + 2048 * (c + 1)]
            hisb = lambda tt: GB1[:, 1024 * tt:1024 * (tt + 1)]
            hrhi = lambda c: GB2[:, 2048 * c:2048 * (c + 1)]
            hrmid = lambda c: GB2[:, 4096 + 2048 * c:4096 + 2048 * (c + 1)]
            sjb = GF0[:, 0:2048]
            tbuf = GF0[:, 2048:4096]
            xtile = lambda i: GF0[:, 4096 + 1024 * (i % 2):4096 + 1024 * (i % 2) + 1024]
            hrr = GF0[:, 6144:6656]
            srr = GF0[0:1, 6656:7168]
            mscr = GF0[:, 7168:8192]

            vA0 = vv.n
            import os
            NTA = int(os.environ.get('NTA', NT))
            for tt in range(NTA):
                def _ldx(q, tt=tt):
                    if tt >= 2:
                        q.wait_ge(vv.h, vA0 + 5 * (tt - 1))
                    sxr.inc(tt, q.dma_start(out=xtile(tt), in_=xrow[128 * tt:128 * (tt + 1), :]), 16)
                EN("sync", _ldx)
                ld = sxr.val(tt)

                def _v1(v, tt=tt, ld=ld):
                    v.wait_ge(sxr.sem(tt).h, ld)
                    if tt == 0 and r == 0:
                        v.wait_ge(ss.h, PRO_S)
                    v.reduce_sum(stats[:, 0:1], xtile(tt), axis=AX)
                    vv.inc(v.tensor_scalar(out=stats[:, 1:2], in0=stats[:, 0:1], scalar1=1.0 / D, scalar2=None, op0=ALU.mult), 1)
                EN("vector", _v1)
                m1 = vv.n

                def _s1(s, tt=tt, m1=m1):
                    s.wait_ge(vv.h, m1)
                    ss.inc(s.activation(out=SA[:, 0:1024], in_=xtile(tt), func=AF.Square,
                                        accum_out=stats[:, 2:3]), 1)
                EN("scalar", _s1)
                m2 = ss.n

                def _v2(v, tt=tt, m2=m2):
                    v.wait_ge(ss.h, m2)
                    v.tensor_scalar(out=stats[:, 3:4], in0=stats[:, 2:3], scalar1=1.0 / D, scalar2=None, op0=ALU.mult)
                    v.tensor_tensor(out=stats[:, 4:5], in0=stats[:, 1:2], in1=stats[:, 1:2], op=ALU.mult)
                    v.tensor_tensor(out=stats[:, 5:6], in0=stats[:, 3:4], in1=stats[:, 4:5], op=ALU.subtract)
                    vv.inc(v.tensor_scalar(out=stats[:, 5:6], in0=stats[:, 5:6], scalar1=EPS, scalar2=None, op0=ALU.add), 1)
                EN("vector", _v2)
                m3 = vv.n

                def _s2(s, tt=tt, m3=m3):
                    s.wait_ge(vv.h, m3)
                    s.activation(out=stats[:, 6:7], in_=stats[:, 5:6], func=AF.Ln)
                    ss.inc(s.activation(out=stats[:, 7:8], in_=stats[:, 6:7], func=AF.Exp, scale=-0.5), 1)
                EN("scalar", _s2)
                m4 = ss.n

                def _v3(v, tt=tt, m4=m4):
                    v.wait_ge(ss.h, m4)
                    v.tensor_tensor(out=stats[:, 8:9], in0=stats[:, 7:8], in1=stats[:, 7:8], op=ALU.mult)
                    v.tensor_tensor(out=stats[:, 9:10], in0=stats[:, 8:9], in1=stats[:, 5:6], op=ALU.mult)
                    v.tensor_scalar(out=stats[:, 9:10], in0=stats[:, 9:10], scalar1=-0.5, scalar2=1.5,
                                    op0=ALU.mult, op1=ALU.add)
                    v.tensor_tensor(out=stats[:, 10:11], in0=stats[:, 7:8], in1=stats[:, 9:10], op=ALU.mult)
                    v.tensor_tensor(out=stats[:, 11:12], in0=stats[:, 1:2], in1=stats[:, 10:11], op=ALU.mult)
                    v.tensor_scalar(out=mscr, in0=xtile(tt), scalar1=stats[:, 10:11], scalar2=stats[:, 11:12],
                                    op0=ALU.mult, op1=ALU.subtract)
                    vv.inc(v.tensor_copy(out=hisb(tt), in_=mscr), 1)
                    v.tensor_tensor(out=mscr, in0=mscr, in1=hisb(tt), op=ALU.subtract)
                    vv.inc(v.tensor_copy(out=SA[:, 1024:2048], in_=mscr), 1)
                EN("vector", _v3)
                m5 = vv.n

                def _spl(q, tt=tt, m5=m5):
                    q.wait_ge(vv.h, m5)
                    sSP.inc(q.dma_start(out=hid[128 * tt:128 * (tt + 1), :], in_=hisb(tt)), 16)
                    sSP.inc(q.dma_start(out=midd[128 * tt:128 * (tt + 1), :], in_=SA[:, 1024:2048]), 16)
                EN("sync", _spl)
                spl = sSP.n

                def _vw(v, spl=spl):
                    v.wait_ge(sSP.h, spl)
                    vv.inc(v.engine_nop(), 1)
                EN("vector", _vw)
            A_dX = dX.n
            A_SP = sSP.n
            assert NTA == NT or maxphase <= 2
            CUT("A")

            if debug:
                def _dbh(q):
                    q.wait_ge(sSP.h, A_SP)
                    dX.inc(q.dma_start(out=dbg["hid"][r], in_=hid[:, :], allow_slow_non_contiguous=True), 16)
                    dX.inc(q.dma_start(out=dbg["mid"][r], in_=midd[:, :]), 16)
                EN("sync", _dbh)

            import os as _os
            _NOTR = int(_os.environ.get("NOTR", "0"))

            def _trl(s):
                s.wait_ge(sSP.h, A_SP)
                for c in range(DC):
                    if _NOTR:
                        dS.inc(s.dma_start(out=hiT(c)[:, 0:1024], in_=hid[0:128, :]), 16)
                        dS.inc(s.dma_start(out=midT(c)[:, 0:1024], in_=midd[0:128, :]), 16)
                    else:
                        dS.inc(s.dma_start(out=hiT(c), in_=hid[:, 128 * c:128 * (c + 1)], transpose=True), 16)
                        dS.inc(s.dma_start(out=midT(c), in_=midd[:, 128 * c:128 * (c + 1)], transpose=True), 16)
            EN("scalar", _trl)
            B_dS = dS.n

            if debug:
                def _dbtr(q):
                    q.wait_ge(dS.h, B_dS)
                    if r == 0:
                        dX.inc(q.dma_start(out=dbg["w1s"][:, 0:2048], in_=w1hi[:, :]), 16)
                        dX.inc(q.dma_start(out=dbg["w1s"][:, 2048:4096], in_=w1lo[:, :]), 16)
                        for li2, t2 in enumerate([lnw1B, lnb1B, lnw2B, lnb2B]):
                            dX.inc(q.dma_start(out=dbg["lnB"][li2], in_=t2[:, :]), 16)
                    dX.inc(q.dma_start(out=dbg["hiT"][r], in_=GB0[:, 0:16384]), 16)
                    dX.inc(q.dma_start(out=dbg["midT"][r], in_=GB0[:, 16384:32768]), 16)
                EN("sync", _dbtr)
                dbtr = dX.n

                def _dbtrw(t, dbtr=dbtr):
                    t.wait_ge(dX.h, dbtr)
                EN("tensor", _dbtrw)
            CUT("trl")

            prodW = [(0, 0), (0, 1), (1, 0), (1, 1)]
            evhr = []
            for jm in range(8):
                jt, mc = jm // 2, jm % 2

                def _mm(t, jt=jt, mc=mc, jm=jm):
                    if jm == 0:
                        t.wait_ge(dS.h, B_dS)
                        t.wait_ge(vv.h, PRO_V)
                        if r > 0:
                            t.wait_ge(vv.h, rowjoin_v)
                    if jm >= 2:
                        t.wait_ge(vv.h, evhr[jm - 2])
                    bank = jm % 2
                    first = True
                    for k in range(DC):
                        for (wp, xp) in prodW:
                            w = w1hi if wp == 0 else w1lo
                            xs = hiT(k) if xp == 0 else midT(k)
                            last = (k == DC - 1 and wp == 1 and xp == 1)
                            mm = t.matmul(PS[bank][:, 0:512], w[:, 256 * k + 128 * mc:256 * k + 128 * (mc + 1)],
                                          xs[:, 512 * jt:512 * (jt + 1)], start=first, stop=last)
                            first = False
                    pp.inc(mm, 1)
                EN("tensor", _mm)
                mmv = pp.n
                import os as _os3
                _HRSKIP = _os3.environ.get("HRSKIP", "")

                def _ev(s, jt=jt, mc=mc, jm=jm, mmv=mmv):
                    s.wait_ge(pp.h, mmv)
                    if jm >= 1:
                        s.wait_ge(vv.h, vv.n)
                    if "E" in _HRSKIP:
                        ss.inc(s.copy(out=hrr, in_=PS[jm % 2][:, 0:512]), 1)
                    elif "B" in _HRSKIP:
                        ss.inc(s.activation(out=hrr, in_=PS[jm % 2][:, 0:512], func=AF.Relu), 1)
                    else:
                        ss.inc(s.activation(out=hrr, in_=PS[jm % 2][:, 0:512], func=AF.Relu,
                                            bias=b1r[:, mc:mc + 1]), 1)
                EN("scalar", _ev)
                evv = ss.n

                def _spv(v, jt=jt, mc=mc, evv=evv):
                    v.wait_ge(ss.h, evv)
                    if "S" in _HRSKIP:
                        vv.inc(v.tensor_copy(out=hrhi(mc)[:, 512 * jt:512 * (jt + 1)], in_=hrr), 1)
                        return
                    v.tensor_copy(out=hrhi(mc)[:, 512 * jt:512 * (jt + 1)], in_=hrr)
                    v.tensor_tensor(out=hrr, in0=hrr, in1=hrhi(mc)[:, 512 * jt:512 * (jt + 1)], op=ALU.subtract)
                    vv.inc(v.tensor_copy(out=hrmid(mc)[:, 512 * jt:512 * (jt + 1)], in_=hrr), 1)
                EN("vector", _spv)
                evhr.append(vv.n)
                CUT("hr%d" % jm)
            B_HR = vv.n
            CUT("hr")

            prodS = [(0, 0), (0, 1), (1, 0), (1, 1)]
            for jt in range(4):
                def _mm(t, jt=jt):
                    if jt == 0:
                        t.wait_ge(vv.h, B_HR)
                    else:
                        t.wait_ge(ss.h, ss.n)
                    first = True
                    for c in range(2):
                        for (wp, hp) in prodS:
                            hsrc = hrhi(c) if hp == 0 else hrmid(c)
                            t.matmul(PS[2][0:1, 0:512], w2hl[:, 2 * wp + c:2 * wp + c + 1],
                                     hsrc[:, 512 * jt:512 * (jt + 1)], start=first, stop=False)
                            first = False
                    t.matmul(PS[2][0:1, 0:512], r2bh[0:1, 0:1], onesr[0:1, 0:512],
                             start=False, stop=False)
                    mm = t.matmul(PS[2][0:1, 0:512], r2bh[0:1, 1:2], onesr[0:1, 0:512],
                                  start=False, stop=True)
                    pp.inc(mm, 1)
                EN("tensor", _mm)
                mmv = pp.n

                def _ev(s, jt=jt, mmv=mmv):
                    s.wait_ge(pp.h, mmv)
                    if jt > 0:
                        s.wait_ge(vv.h, vv.n)
                    ss.inc(s.copy(out=srr[:, :], in_=PS[2][0:1, 0:512]), 1)
                EN("scalar", _ev)
                evv = ss.n

                def _spv(v, jt=jt, evv=evv):
                    v.wait_ge(ss.h, evv)
                    v.tensor_copy(out=srhi[0:1, 512 * jt:512 * (jt + 1)], in_=srr)
                    v.tensor_tensor(out=srr, in0=srr, in1=srhi[0:1, 512 * jt:512 * (jt + 1)], op=ALU.subtract)
                    vv.inc(v.tensor_copy(out=srmid[0:1, 512 * jt:512 * (jt + 1)], in_=srr), 1)
                EN("vector", _spv)

            def _mmB(t):
                for tt in range(NT):
                    first = True
                    for c in range(2):
                        for (wp, hp) in prodS:
                            hsrc = hrhi(c) if hp == 0 else hrmid(c)
                            t.matmul(PS[3][:, tt:tt + 1], hsrc[:, 128 * tt:128 * (tt + 1)],
                                     w2hl[:, 2 * wp + c:2 * wp + c + 1], start=first, stop=False)
                            first = False
                    t.matmul(PS[3][:, tt:tt + 1], ones[:, :], r2bh[0:1, 0:1], start=False, stop=False)
                    mm = t.matmul(PS[3][:, tt:tt + 1], ones[:, :], r2bh[0:1, 1:2], start=False, stop=True)
                pp.inc(mm, 1)
            EN("tensor", _mmB)
            m_sT = pp.n

            def _evB(s):
                s.wait_ge(pp.h, m_sT)
                ss.inc(s.copy(out=sT[:, :], in_=PS[3][:, 0:16]), 1)
            EN("scalar", _evB)
            m_sTe = ss.n

            def _spB(v):
                v.wait_ge(ss.h, m_sTe)
                v.tensor_copy(out=sTh[:, :], in_=sT[:, :])
                v.tensor_tensor(out=res16[:, :], in0=sT[:, :], in1=sTh[:, :], op=ALU.subtract)
                v.tensor_copy(out=sTm[:, :], in_=res16[:, :])
                v.tensor_tensor(out=res16[:, :], in0=res16[:, :], in1=sTm[:, :], op=ALU.subtract)
                vv.inc(v.tensor_tensor(out=s12T[:, :], in0=sT[:, :], in1=res16[:, :], op=ALU.subtract), 1)
            EN("vector", _spB)
            B_S12 = vv.n
            CUT("scores")

            for jt in range(4):
                def _mm(t, jt=jt):
                    if jt == 0:
                        t.wait_ge(vv.h, B_S12)
                    else:
                        t.wait_ge(ss.h, ss.n)
                    t.matmul(PS[4][:, 0:512], ones[:, :], srhi[0:1, 512 * jt:512 * (jt + 1)], start=True, stop=False)
                    pp.inc(t.matmul(PS[4][:, 0:512], ones[:, :], srmid[0:1, 512 * jt:512 * (jt + 1)],
                                    start=False, stop=True), 1)
                EN("tensor", _mm)
                mmv = pp.n

                def _ev(s, jt=jt, mmv=mmv):
                    s.wait_ge(pp.h, mmv)
                    ss.inc(s.copy(out=sjb[:, 512 * jt:512 * (jt + 1)], in_=PS[4][:, 0:512]), 1)
                EN("scalar", _ev)
            B_SJB = ss.n

            if debug:
                def _dbhr(q):
                    q.wait_ge(ss.h, B_SJB)
                    dX.inc(q.dma_start(out=dbg["hrh"][r], in_=GB2[:, 0:4096]), 16)
                    dX.inc(q.dma_start(out=dbg["hrm"][r], in_=GB2[:, 4096:8192]), 16)
                    dX.inc(q.dma_start(out=dbg["srh"][r], in_=srhi[:, :]), 16)
                    dX.inc(q.dma_start(out=dbg["srm"][r], in_=srmid[:, :]), 16)
                    dX.inc(q.dma_start(out=dbg["sT"][r], in_=sT[:, :]), 16)
                EN("sync", _dbhr)

            def _rank(v):
                v.wait_ge(ss.h, B_SJB)
                for ic in range(NT):
                    v.tensor_scalar(out=tbuf, in0=sjb, scalar1=s12T[:, ic:ic + 1], scalar2=None, op0=ALU.subtract)
                    v.tensor_scalar(out=SA[:, 0:2048], in0=tbuf, scalar1=0.0, scalar2=0.0,
                                    op0=ALU.is_gt, op1=ALU.add, accum_out=rankT[:, ic:ic + 1])
                    if ic > 0:
                        v.tensor_scalar(out=SA[:, 0:128 * ic], in0=tbuf[:, 0:128 * ic], scalar1=0.0, scalar2=0.0,
                                        op0=ALU.is_equal, op1=ALU.add, accum_out=stats[:, 0:1])
                        v.tensor_tensor(out=rankT[:, ic:ic + 1], in0=rankT[:, ic:ic + 1], in1=stats[:, 0:1], op=ALU.add)
                    v.scalar_tensor_tensor(out=SA[:, 0:128], in0=tbuf[:, 128 * ic:128 * (ic + 1)], scalar=0.0,
                                           in1=triST[:, :], op0=ALU.is_equal, op1=ALU.mult, accum_out=stats[:, 1:2])
                    v.tensor_tensor(out=rankT[:, ic:ic + 1], in0=rankT[:, ic:ic + 1], in1=stats[:, 1:2], op=ALU.add)
                v.tensor_scalar(out=maskT[:, :], in0=rankT[:, :], scalar1=float(SEL), scalar2=None, op0=ALU.is_lt)
                vv.inc(v.tensor_scalar(out=nmT[:, :], in0=maskT[:, :], scalar1=-1.0, scalar2=1.0, op0=ALU.mult, op1=ALU.add), 1)
            EN("vector", _rank)
            B_RANK = vv.n
            CUT("rank")

            def _gmb(t):
                t.wait_ge(vv.h, B_RANK)
                pp.inc(t.matmul(PS[5][0:1, 16:32], onesc[:, :], maskT[:, :], start=True, stop=True), 1)
            EN("tensor", _gmb)
            m_gmb = pp.n

            def _neg(v):
                v.wait_ge(pp.h, m_gmb)
                vv.inc(v.reduce_max(negm[:, :], sjb, axis=AX, negate=True), 1)
                v.tensor_copy(out=carry[0:1, 0:16], in_=PS[5][0:1, 16:32])
                cur, oth = 0, 16
                for sh in (1, 2, 4, 8):
                    v.tensor_copy(out=carry[0:1, oth:oth + sh], in_=carry[0:1, cur:cur + sh])
                    v.tensor_tensor(out=carry[0:1, oth + sh:oth + 16], in0=carry[0:1, cur + sh:cur + 16],
                                    in1=carry[0:1, cur:cur + 16 - sh], op=ALU.add)
                    cur, oth = oth, cur
                v.memset(carry[0:1, oth:oth + 1], 0.0)
                v.tensor_copy(out=carry[0:1, oth + 1:oth + 16], in_=carry[0:1, cur:cur + 15])
                v.tensor_copy(out=carryh[0:1, 0:16], in_=carry[0:1, oth:oth + 16])
                v.tensor_tensor(out=carry[0:1, 32:48], in0=carry[0:1, oth:oth + 16], in1=carryh[0:1, 0:16],
                                op=ALU.subtract)
                vv.inc(v.tensor_copy(out=carryh[0:1, 16:32], in_=carry[0:1, 32:48]), 1)
            EN("vector", _neg)
            m_carry = vv.n

            def _exp(s):
                s.wait_ge(vv.h, m_carry)
                ss.inc(s.activation(out=eT[:, :], in_=sT[:, :], func=AF.Exp, bias=negm[:, :]), 1)
            EN("scalar", _exp)
            m_eT = ss.n

            def _em(v):
                v.wait_ge(ss.h, m_eT)
                v.tensor_tensor(out=eT[:, :], in0=eT[:, :], in1=maskT[:, :], op=ALU.mult)
                v.reduce_sum(erow[:, :], eT[:, :], axis=AX)
                vv.inc(v.tensor_copy(out=erowb[:, :], in_=erow[:, :]), 1)
            EN("vector", _em)
            m_er = vv.n

            def _zmm(t):
                t.wait_ge(vv.h, m_er)
                t.matmul(PS[5][0:1, 4:5], onesc[:, :], erowb[:, :], start=True, stop=True)
                t.matmul(PS[6][:, 0:16], triLE[:, :], maskT[:, :], start=True, stop=False)
                t.matmul(PS[6][:, 0:16], ones[:, :], carryh[0:1, 0:16], start=False, stop=False)
                pp.inc(t.matmul(PS[6][:, 0:16], ones[:, :], carryh[0:1, 16:32], start=False, stop=True), 1)
            EN("tensor", _zmm)
            m_Z = pp.n

            def _rz(v):
                v.wait_ge(pp.h, m_Z)
                v.tensor_copy(out=wz[0:1, 0:1], in_=PS[5][0:1, 4:5])
                v.reciprocal(out=wz[0:1, 1:2], in_=wz[0:1, 0:1])
                vv.inc(v.tensor_copy(out=SA[0:1, 256:257], in_=wz[0:1, 1:2]), 1)
                v.tensor_copy(out=cT[:, :], in_=PS[6][:, 0:16])
                v.tensor_scalar(out=res16[:, :], in0=cT[:, :], scalar1=1.0, scalar2=0.0, op0=ALU.subtract, op1=ALU.max)
                v.tensor_copy(out=ogat[:, :], in_=res16[:, :])
                v.tensor_scalar(out=res16[:, :], in0=cT[:, :], scalar1=1.0, scalar2=None, op0=ALU.subtract)
                v.tensor_tensor(out=res16[:, :], in0=res16[:, :], in1=maskT[:, :], op=ALU.mult)
                v.scalar_tensor_tensor(out=oscf[:, :], in0=nmT[:, :], scalar=4096.0, in1=res16[:, :],
                                       op0=ALU.mult, op1=ALU.add)
                vv.inc(v.tensor_copy(out=osca[:, :], in_=oscf[:, :]), 1)
            EN("vector", _rz)
            m_rzv = vv.n

            def _rzb(t):
                t.wait_ge(vv.h, m_rzv)
                pp.inc(t.matmul(PS[5][:, 2:3], ones[:, :], SA[0:1, 256:257], start=True, stop=True), 1)
            EN("tensor", _rzb)
            m_rzb = pp.n

            def _mw(v):
                v.wait_ge(pp.h, m_rzb)
                vv.inc(v.tensor_scalar(out=mwT[:, :], in0=eT[:, :], scalar1=PS[5][:, 2:3], scalar2=None,
                                       op0=ALU.mult), 1)
            EN("vector", _mw)
            C_OFF = vv.n
            CUT("C")

            if debug:
                def _dbc(q):
                    q.wait_ge(vv.h, C_OFF)
                    dX.inc(q.dma_start(out=dbg["scores"][r:r + 1, :].rearrange("o (c p) -> p (o c)", p=128), in_=sT[:, :], allow_slow_non_contiguous=True), 16)
                    dX.inc(q.dma_start(out=dbg["rank"][r:r + 1, :].rearrange("o (c p) -> p (o c)", p=128), in_=rankT[:, :], allow_slow_non_contiguous=True), 16)
                    dX.inc(q.dma_start(out=dbg["mw"][r:r + 1, :].rearrange("o (c p) -> p (o c)", p=128), in_=mwT[:, :], allow_slow_non_contiguous=True), 16)
                    dX.inc(q.dma_start(out=dbg["c"][r:r + 1, :].rearrange("o (c p) -> p (o c)", p=128), in_=cT[:, :], allow_slow_non_contiguous=True), 16)
                EN("sync", _dbc)

            def _sca(q):
                q.wait_ge(vv.h, C_OFF)
                for tt in range(NT):
                    sSC.inc(q.indirect_dma_start(
                        out=tokd[0:1, :], out_offset=bass.IndirectOffsetOnAxis(osca[:, tt:tt + 1], 0),
                        in_=hisb(tt), in_offset=None,
                        bounds_check=BCREG[0], oob_is_err=False), 16)
            EN("gpsimd", _sca)
            D_dG = sSC.n
            CUT("D-scatter")

            if debug:
                def _dbt(q):
                    q.wait_ge(sSC.h, D_dG)
                    dX.inc(q.dma_start(out=dbg["tok"][r], in_=tokd[:, :], allow_slow_non_contiguous=True), 16)
                EN("sync", _dbt)

            gate_l1 = []
            for tt in range(NT):
                par = tt % 4
                gps = PS[[5, 6, 4, 7][par]][:, 8:16]
                glt = lgt[:, 8 * par:8 * par + 8]
                gls = lambda a, b, par=par: lgs[:, 48 * par + a:48 * par + b]

                def _lmm(t, tt=tt, par=par, gps=gps):
                    if tt < 4:
                        t.wait_ge(vv.h, C_OFF)
                    else:
                        t.wait_ge(vv.h, gate_l1[tt - 4])
                    first = True
                    for k in range(DC):
                        t.matmul(gps, hiT(k)[:, 128 * tt:128 * (tt + 1)], gwsb[:, 8 * k:8 * (k + 1)],
                                 start=first, stop=False)
                        first = False
                    t.matmul(gps, ones[:, :], gbh[0:1, 0:8], start=False, stop=False)
                    pp.inc(t.matmul(gps, ones[:, :], gbh[0:1, 8:16], start=False, stop=True), 1)
                EN("tensor", _lmm)
                mmv = pp.n

                def _l1(v, tt=tt, mmv=mmv, gps=gps, glt=glt, gls=gls):
                    v.wait_ge(pp.h, mmv)
                    v.tensor_copy(out=glt, in_=gps)
                    vv.inc(v.reduce_max(gls(0, 1), glt, axis=AX, negate=True), 1)
                EN("vector", _l1)
                gate_l1.append(vv.n)
                m1 = vv.n

                def _l2(s, tt=tt, m1=m1, glt=glt, gls=gls):
                    s.wait_ge(vv.h, m1)
                    ss.inc(s.activation(out=gls(8, 16), in_=glt, func=AF.Exp, bias=gls(0, 1)), 1)
                EN("scalar", _l2)
                m2 = ss.n

                def _l3(v, tt=tt, m2=m2, gls=gls):
                    v.wait_ge(ss.h, m2)
                    v.reduce_sum(gls(1, 2), gls(8, 16), axis=AX)
                    v.reciprocal(out=gls(2, 3), in_=gls(1, 2))
                    v.tensor_scalar(out=gls(16, 24), in0=gls(8, 16), scalar1=gls(2, 3), scalar2=None, op0=ALU.mult)
                    v.tensor_scalar(out=gls(24, 32), in0=gls(16, 24), scalar1=0.1, scalar2=None, op0=ALU.is_ge)
                    v.tensor_tensor(out=gls(32, 40), in0=gls(16, 24), in1=gls(24, 32), op=ALU.mult)
                    v.reduce_sum(gls(3, 4), gls(32, 40), axis=AX)
                    v.reciprocal(out=gls(4, 5), in_=gls(3, 4))
                    vv.inc(v.tensor_scalar(out=wgtt[:, 8 * tt:8 * (tt + 1)], in0=gls(32, 40), scalar1=gls(4, 5),
                                           scalar2=None, op0=ALU.mult), 1)
                EN("vector", _l3)
            E_WGT = vv.n
            CUT("E-gate")

            def _ldt(q):
                q.wait_ge(sSC.h, D_dG)
                for st in range(NSEL):
                    dX.inc(q.dma_start(out=TOKB[:, 1024 * st:1024 * (st + 1)],
                                       in_=tokd[128 * st:128 * (st + 1), :]), 16)
            EN("sync", _ldt)
            F_TOK = dX.n

            def emit_f2_tile(st):
                def _n1a(v, st=st):
                    if st == 0:
                        v.wait_ge(dX.h, F_TOK)
                    tokt = TOKB[:, 1024 * st:1024 * (st + 1)]
                    v.reduce_sum(stats[:, 0:1], tokt, axis=AX)
                    vv.inc(v.tensor_scalar(out=stats[:, 1:2], in0=stats[:, 0:1], scalar1=1.0 / D, scalar2=None, op0=ALU.mult), 1)
                EN("vector", _n1a)
                m1 = vv.n

                def _n1b(s, st=st, m1=m1):
                    s.wait_ge(vv.h, m1)
                    ss.inc(s.activation(out=SA[:, 0:1024], in_=TOKB[:, 1024 * st:1024 * (st + 1)], func=AF.Square,
                                        accum_out=stats[:, 2:3]), 1)
                EN("scalar", _n1b)
                m2 = ss.n

                def _n1c(v, st=st, m2=m2):
                    v.wait_ge(ss.h, m2)
                    v.tensor_scalar(out=stats[:, 3:4], in0=stats[:, 2:3], scalar1=1.0 / D, scalar2=None, op0=ALU.mult)
                    v.tensor_tensor(out=stats[:, 4:5], in0=stats[:, 1:2], in1=stats[:, 1:2], op=ALU.mult)
                    v.tensor_tensor(out=stats[:, 5:6], in0=stats[:, 3:4], in1=stats[:, 4:5], op=ALU.subtract)
                    vv.inc(v.tensor_scalar(out=stats[:, 5:6], in0=stats[:, 5:6], scalar1=EPS, scalar2=None, op0=ALU.add), 1)
                EN("vector", _n1c)
                m3 = vv.n

                def _n1d_(s, st=st, m3=m3):
                    s.wait_ge(vv.h, m3)
                    s.activation(out=stats[:, 6:7], in_=stats[:, 5:6], func=AF.Ln)
                    ss.inc(s.activation(out=stats[:, 7:8], in_=stats[:, 6:7], func=AF.Exp, scale=-0.5), 1)
                EN("scalar", _n1d_)
                m4 = ss.n

                def _n1e(v, st=st, m4=m4):
                    v.wait_ge(ss.h, m4)
                    tokt = TOKB[:, 1024 * st:1024 * (st + 1)]
                    v.tensor_tensor(out=stats[:, 8:9], in0=stats[:, 1:2], in1=stats[:, 7:8], op=ALU.mult)
                    v.tensor_scalar(out=SA[:, 0:1024], in0=tokt, scalar1=stats[:, 7:8], scalar2=stats[:, 8:9],
                                    op0=ALU.mult, op1=ALU.subtract)
                    v.tensor_tensor(out=SA[:, 1024:2048], in0=SA[:, 0:1024], in1=lnw1B[:, :], op=ALU.mult)
                    vv.inc(v.tensor_tensor(out=SA[:, 1024:2048], in0=SA[:, 1024:2048], in1=lnb1B[:, :], op=ALU.add), 1)
                EN("vector", _n1e)
                m5 = vv.n

                def _n1s(q, st=st, m5=m5):
                    q.wait_ge(vv.h, m5)
                    dX.inc(q.dma_start(out=n1d[128 * st:128 * (st + 1), :], in_=SA[:, 1024:2048]), 16)
                EN("sync", _n1s)
                spl = dX.n

                def _n1w(v, spl=spl):
                    v.wait_ge(dX.h, spl)
                    vv.inc(v.engine_nop(), 1)
                EN("vector", _n1w)

            if debug:
                def _dbw(q):
                    q.wait_ge(vv.h, E_WGT)
                    dX.inc(q.dma_start(out=dbg["wgt"][r].rearrange("(tt p) e -> p tt e", p=128), in_=wgtt[:, :].rearrange("p (tt e) -> p tt e", tt=16), allow_slow_non_contiguous=True), 16)
                EN("sync", _dbw)

            WeV = lambda i: GB0[:, 16384 + 4096 * (i % 2):16384 + 4096 * (i % 2) + 4096]
            facc = lambda mt: GF0[:, 512 * mt:512 * (mt + 1)]
            ffn_mm = []
            ffn_ev = []
            ffn_sil = {}
            for ex in range(2 * E):
                nd, e = ex // E, ex % E

                def _lw(q, ex=ex, nd=nd, e=e):
                    if ex == 0:
                        q.wait_ge(pp.h, m_sT)
                    if len(ffn_mm) >= 2:
                        q.wait_ge(pp.h, ffn_mm[-2])
                    dG.inc(q.dma_start(out=WeV(ex).rearrange("p (c m) -> p c m", c=8),
                                       in_=w_ffn[e][:, 512 * nd:512 * (nd + 1)]
                                       .rearrange("(c p) m -> p c m", p=128)), 16)
                    dG.inc(q.dma_start(out=fbrow[0:1, 512 * (ex % 2):512 * (ex % 2) + 512],
                                       in_=b_ffn[e:e + 1, 512 * nd:512 * (nd + 1)]), 16)
                EN("gpsimd", _lw)
                lw = dG.n

                if r == 0:
                    def _w2c(q, ex=ex):
                        for k in (2 * ex, 2 * ex + 1):
                            e_c, cc = divmod(k, 4)
                            dW2C.inc(q.dma_start(out=w2bf[e_c][1024 * cc:1024 * (cc + 1), :],
                                                 in_=w2_moe[e_c][1024 * cc:1024 * (cc + 1), :]), 16)
                    EN("gpsimd", _w2c)

                for mt in range(NT):
                    gi = ex * NT + mt

                    def _mm(t, ex=ex, nd=nd, e=e, mt=mt, gi=gi, lw=lw):
                        if mt == 0:
                            t.wait_ge(dG.h, lw)
                        bank = [0, 1, 2, 3, 4, 7][gi % 6]
                        if gi in (4, 5):
                            t.wait_ge(vv.h, gate_l1[-1])
                        if gi >= 6:
                            t.wait_ge(vv.h, ffn_ev[gi - 6])
                        t.matmul(PS[bank][:, 0:512], ones[:, :],
                                 fbrow[0:1, 512 * (ex % 2):512 * (ex % 2) + 512], start=True, stop=False)
                        for k in range(DC):
                            mm = t.matmul(PS[bank][:, 0:512], hiT(k)[:, 128 * mt:128 * (mt + 1)],
                                          WeV(ex)[:, 512 * k:512 * (k + 1)],
                                          start=False, stop=(k == DC - 1))
                        pp.inc(mm, 1)
                    EN("tensor", _mm)
                    if mt == NT - 1:
                        ffn_mm.append(pp.n)
                    mmv = pp.n

                    def _ev(v, ex=ex, e=e, mt=mt, gi=gi, mmv=mmv):
                        v.wait_ge(pp.h, mmv)
                        if ex == E and e == 0:
                            v.wait_ge(ss.h, ffn_sil[mt])
                        bank = [0, 1, 2, 3, 4, 7][gi % 6]
                        wsl = wgtt[:, 8 * mt + e:8 * mt + e + 1]
                        if e == 0:
                            vv.inc(v.tensor_scalar(out=facc(mt), in0=PS[bank][:, 0:512], scalar1=wsl,
                                                   scalar2=None, op0=ALU.mult), 1)
                        else:
                            vv.inc(v.scalar_tensor_tensor(out=facc(mt), in0=PS[bank][:, 0:512], scalar=wsl,
                                                          in1=facc(mt), op0=ALU.mult, op1=ALU.add), 1)
                    EN("vector", _ev)
                    ffn_ev.append(vv.n)

                if ex < NSEL:
                    emit_f2_tile(ex)

                if e == E - 1:
                    for mt in range(NT):
                        def _sg(s, mt=mt, nd=nd):
                            s.wait_ge(vv.h, ffn_ev[(nd * E + E - 1) * NT + mt])
                            if mt > 0:
                                s.wait_ge(dX.h, dX.n)
                            ss.inc(s.activation(out=gtile[:, :], in_=facc(mt), func=AF.Silu), 1)
                        EN("scalar", _sg)
                        m1 = ss.n
                        if nd == 0:
                            ffn_sil[mt] = m1

                        def _sp(q, mt=mt, nd=nd, m1=m1):
                            q.wait_ge(ss.h, m1)
                            dX.inc(q.dma_start(out=gd[128 * mt:128 * (mt + 1), 512 * nd:512 * (nd + 1)],
                                               in_=gtile[:, :]), 16)
                        EN("sync", _sp)

                if ex == NSEL - 1:
                    F_N1 = dX.n

                    def _n1t(s, F_N1=F_N1):
                        s.wait_ge(dX.h, F_N1)
                        s.wait_ge(pp.h, m_sT)
                        for c in range(DC):
                            dS.inc(s.dma_start(out=GB0[:, 24576 + 1024 * c:24576 + 1024 * (c + 1)],
                                               in_=n1d[:, 128 * c:128 * (c + 1)], transpose=True), 16)
                    EN("scalar", _n1t)
                    F_N1T = dS.n
            E_G = dX.n
            E_FFNMM = pp.n

            if debug:
                def _dbw3(q, m0=ffn_mm[0]):
                    q.wait_ge(pp.h, m0)
                    dX.inc(q.dma_start(out=dbg["wgt2"][r], in_=wgtt[:, :]), 16)
                    q.wait_ge(dX.h, E_G)
                    dX.inc(q.dma_start(out=dbg["wgt3"][r], in_=wgtt[:, :]), 16)
                EN("sync", _dbw3)
            CUT("E-experts")

            if debug:
                def _dbg2(q):
                    q.wait_ge(dX.h, E_G)
                    dX.inc(q.dma_start(out=dbg["g"][r], in_=gd[:, :], allow_slow_non_contiguous=True), 16)
                EN("sync", _dbg2)

            if debug:
                def _dbn1(q):
                    q.wait_ge(dS.h, F_N1T)
                    dX.inc(q.dma_start(out=dbg["n1"][r], in_=n1d[:, :], allow_slow_non_contiguous=True), 16)
                EN("sync", _dbn1)
            CUT("F2-n1")
            n1T = lambda c: GB0[:, 24576 + 1024 * c:24576 + 1024 * (c + 1)]

            minslot = lambda i: GB2[:, 2048 * (i % 2):2048 * (i % 2) + 2048]
            uch = lambda c: GB1[:, 1024 * c:1024 * (c + 1)]
            uv_ld = []
            uv_ev = []

            def emit_uv_block(Mg):
                def _lb(q, Mg=Mg):
                    if Mg == 0:
                        q.wait_ge(pp.h, m_sT)
                    if len(uv_ld) >= 2:
                        q.wait_ge(pp.h, uv_ld[-2][1])
                    dG.inc(q.dma_start(out=minslot(Mg).rearrange("p (c m) -> p c m", c=8),
                                       in_=w_m_in[:, 256 * Mg:256 * (Mg + 1)]
                                       .rearrange("(c p) m -> p c m", p=128)), 16)
                EN("gpsimd", _lb)
                lb = dG.n

                for sub in range(4):
                    m2_, rt = sub // 2, sub % 2
                    gi = Mg * 4 + sub
                    Mglob = Mg * 2 + m2_

                    def _mm(t, Mg=Mg, m2_=m2_, rt=rt, gi=gi, lb=lb):
                        if Mg == 0 and sub == 0:
                            t.wait_ge(dS.h, F_N1T)
                            t.wait_ge(vv.h, ffn_ev[-1])
                        t.wait_ge(dG.h, lb)
                        bank = gi % 4
                        if gi >= 4:
                            t.wait_ge(vv.h, uv_ev[gi - 4][0])
                            t.wait_ge(ss.h, uv_ev[gi - 4][1])
                        for k in range(DC):
                            mm = t.matmul(PS[bank][:, 0:512], minslot(Mg)[:, 256 * k + 128 * m2_:256 * k + 128 * (m2_ + 1)],
                                          n1T(k)[:, 512 * rt:512 * (rt + 1)], start=(k == 0), stop=(k == DC - 1))
                        pp.inc(mm, 1)
                    EN("tensor", _mm)
                    mmv = pp.n
                    if sub == 3:
                        uv_ld.append((Mg, mmv))

                    if Mglob < 16:
                        def _evu(v, Mglob=Mglob, rt=rt, gi=gi, mmv=mmv):
                            v.wait_ge(pp.h, mmv)
                            vv.inc(v.tensor_scalar(out=uch(Mglob)[:, 512 * rt:512 * (rt + 1)],
                                                   in0=PS[gi % 4][:, 0:512],
                                                   scalar1=convw[:, Mglob:Mglob + 1],
                                                   scalar2=convb[:, Mglob:Mglob + 1],
                                                   op0=ALU.mult, op1=ALU.add), 1)
                        EN("vector", _evu)
                        uv_ev.append((vv.n, ss.n))
                    else:
                        yc = Mglob - 16

                        def _evr(s, gi=gi, mmv=mmv):
                            s.wait_ge(pp.h, mmv)
                            s.wait_ge(vv.h, vv.n)
                            ss.inc(s.activation(out=silur[:, 0:512], in_=PS[gi % 4][:, 0:512], func=AF.Silu), 1)
                        EN("scalar", _evr)
                        mse = ss.n

                        def _evy(v, yc=yc, rt=rt, mse=mse):
                            v.wait_ge(ss.h, mse)
                            v.tensor_tensor(out=SA[:, 0:512], in0=silur[:, 0:512], in1=zbs[:, 512 * rt:512 * (rt + 1)],
                                            op=ALU.mult)
                            vv.inc(v.tensor_tensor(out=uch(yc)[:, 512 * rt:512 * (rt + 1)],
                                                   in0=SA[:, 0:512], in1=uch(yc)[:, 512 * rt:512 * (rt + 1)],
                                                   op=ALU.mult), 1)
                        EN("vector", _evy)
                        uv_ev.append((vv.n, ss.n))

            for Mg in range(8):
                emit_uv_block(Mg)
            F_U = vv.n
            CUT("F4-u")

            if debug:
                def _dbu(q):
                    q.wait_ge(vv.h, F_U)
                    for c2 in range(16):
                        dX.inc(q.dma_start(out=dbg["u"][r, 128 * c2:128 * (c2 + 1), :], in_=uch(c2), allow_slow_non_contiguous=True), 16)
                EN("sync", _dbu)

            for rt in range(2):
                def _zmm(t, rt=rt):
                    if rt == 0:
                        t.wait_ge(vv.h, F_U)
                    else:
                        t.wait_ge(ss.h, ss.n)
                    for k in range(16):
                        mm1 = t.matmul(PS[4][0:1, 0:512], xpB[:, k:k + 1], uch(k)[:, 512 * rt:512 * (rt + 1)],
                                       start=(k == 0), stop=(k == 15))
                    pp.inc(mm1, 1)
                    for k in range(16):
                        mm2 = t.matmul(PS[5][0:1, 0:512], xpC[:, k:k + 1], uch(k)[:, 512 * rt:512 * (rt + 1)],
                                       start=(k == 0), stop=(k == 15))
                    pp.inc(mm2, 1)
                EN("tensor", _zmm)
                mmv = pp.n

                def _zev(v, rt=rt, mmv=mmv):
                    v.wait_ge(pp.h, mmv)
                    v.tensor_copy(out=zzr[:, :], in_=PS[4][0:1, 0:512])
                    v.tensor_tensor(out=zzr[:, :], in0=zzr[:, :], in1=PS[5][0:1, 0:512], op=ALU.mult)
                    vv.inc(v.tensor_copy(out=zrow[0:1, 512 * rt:512 * (rt + 1)], in_=zzr[:, :]), 1)
                EN("vector", _zev)
                mze = vv.n

                def _zbc(t, rt=rt, mze=mze):
                    t.wait_ge(vv.h, mze)
                    pp.inc(t.matmul(PS[4][:, 0:512], ones[:, :], zrow[0:1, 512 * rt:512 * (rt + 1)], start=True, stop=True), 1)
                EN("tensor", _zbc)
                mzb = pp.n

                def _zse(s, rt=rt, mzb=mzb):
                    s.wait_ge(pp.h, mzb)
                    ss.inc(s.copy(out=zbs[:, 512 * rt:512 * (rt + 1)], in_=PS[4][:, 0:512]), 1)
                EN("scalar", _zse)

            for Mg in range(8, 16):
                emit_uv_block(Mg)
            F_Y = vv.n
            CUT("F5-y")

            if debug:
                def _dby(q):
                    q.wait_ge(vv.h, F_Y)
                    for c2 in range(16):
                        dX.inc(q.dma_start(out=dbg["y"][r, 128 * c2:128 * (c2 + 1), :], in_=uch(c2), allow_slow_non_contiguous=True), 16)
                EN("sync", _dby)

            mosl = lambda i: GB2[:, 4096 + 1024 * (i % 2):4096 + 1024 * (i % 2) + 1024]
            at_ld = []
            at_ev = []
            tok2t = lambda st: GF0[:, 1024 * st:1024 * (st + 1)]
            for it in range(4):
                nd, mrg = it // 2, it % 2
                for sg in range(8):
                    def _lmo(q, it=it, nd=nd, sg=sg):
                        gl = it * 8 + sg
                        if gl == 0:
                            q.wait_ge(vv.h, F_Y)
                        if len(at_ld) >= 2:
                            q.wait_ge(pp.h, at_ld[-2])
                        dG.inc(q.dma_start(out=mosl(gl).rearrange("p (c m) -> p c m", c=2),
                                           in_=w_mout[256 * sg:256 * (sg + 1), 512 * nd:512 * (nd + 1)]
                                           .rearrange("(c p) m -> p c m", p=128)), 16)
                    EN("gpsimd", _lmo)
                    lmo = dG.n

                    def _amm(t, it=it, nd=nd, mrg=mrg, sg=sg, lmo=lmo):
                        gl = it * 8 + sg
                        t.wait_ge(dG.h, lmo)
                        if gl == 0:
                            t.wait_ge(vv.h, F_Y)
                        if sg == 0:
                            for j in range(4):
                                g = 4 * it + j
                                if g >= 7:
                                    t.wait_ge(vv.h, at_ev[g - 7])
                        for kk in range(2):
                            kc = 2 * sg + kk
                            for j in range(4):
                                Mr = mrg * 4 + j
                                bank = (4 * it + j) % 7
                                mm = t.matmul(PS[bank][:, 0:512], uch(kc)[:, 128 * Mr:128 * (Mr + 1)],
                                              mosl(gl)[:, 512 * kk:512 * (kk + 1)],
                                              start=(sg == 0 and kk == 0), stop=(sg == 7 and kk == 1))
                        pp.inc(mm, 1)
                    EN("tensor", _amm)
                    at_ld.append(pp.n)
                mmv = pp.n

                for j in range(4):
                    def _aev(v, it=it, nd=nd, mrg=mrg, j=j, mmv=mmv):
                        v.wait_ge(pp.h, mmv)
                        if it == 0 and j == 0:
                            v.wait_ge(ss.h, ss.n)
                        Mr = mrg * 4 + j
                        bank = (4 * it + j) % 7
                        vv.inc(v.tensor_tensor(out=tok2t(Mr)[:, 512 * nd:512 * (nd + 1)],
                                               in0=TOKB[:, 1024 * Mr + 512 * nd:1024 * Mr + 512 * nd + 512],
                                               in1=PS[bank][:, 0:512], op=ALU.add), 1)
                        at_ev.append(vv.n)
                    EN("vector", _aev)
            F_T2 = vv.n
            CUT("F6-attn")

            if debug:
                def _dbt2(q):
                    q.wait_ge(vv.h, F_T2)
                    for st in range(NSEL):
                        dX.inc(q.dma_start(out=dbg["tok2"][r, 128 * st:128 * (st + 1), :], in_=tok2t(st), allow_slow_non_contiguous=True), 16)
                EN("sync", _dbt2)

            f7s = lambda st, a, b: lgs[:, 12 * st + a:12 * st + b]
            f7v0 = vv.n

            def _n2a(v):
                for st in range(NSEL):
                    v.reduce_sum(f7s(st, 0, 1), tok2t(st), axis=AX)
                    vv.inc(v.tensor_scalar(out=f7s(st, 1, 2), in0=f7s(st, 0, 1), scalar1=1.0 / D,
                                           scalar2=None, op0=ALU.mult), 1)
            EN("vector", _n2a)
            f7a = [f7v0 + st + 1 for st in range(NSEL)]

            def _n2b(s):
                for st in range(NSEL):
                    s.wait_ge(vv.h, f7a[st])
                    ss.inc(s.activation(out=SA[:, 0:1024], in_=tok2t(st), func=AF.Square,
                                        accum_out=f7s(st, 2, 3)), 1)
            EN("scalar", _n2b)
            f7b = [ss.n - NSEL + st + 1 for st in range(NSEL)]

            def _n2c(v):
                for st in range(NSEL):
                    v.wait_ge(ss.h, f7b[st])
                    v.tensor_scalar(out=f7s(st, 3, 4), in0=f7s(st, 2, 3), scalar1=1.0 / D, scalar2=None, op0=ALU.mult)
                    v.tensor_tensor(out=f7s(st, 4, 5), in0=f7s(st, 1, 2), in1=f7s(st, 1, 2), op=ALU.mult)
                    v.tensor_tensor(out=f7s(st, 5, 6), in0=f7s(st, 3, 4), in1=f7s(st, 4, 5), op=ALU.subtract)
                    vv.inc(v.tensor_scalar(out=f7s(st, 5, 6), in0=f7s(st, 5, 6), scalar1=EPS, scalar2=None, op0=ALU.add), 1)
            EN("vector", _n2c)
            f7c = [vv.n - NSEL + st + 1 for st in range(NSEL)]

            def _n2d(s):
                for st in range(NSEL):
                    s.wait_ge(vv.h, f7c[st])
                    s.activation(out=f7s(st, 6, 7), in_=f7s(st, 5, 6), func=AF.Ln)
                    ss.inc(s.activation(out=f7s(st, 7, 8), in_=f7s(st, 6, 7), func=AF.Exp, scale=-0.5), 1)
            EN("scalar", _n2d)
            f7d = [ss.n - NSEL + st + 1 for st in range(NSEL)]

            f7spl = []
            for st in range(NSEL):
                def _n2e(v, st=st):
                    v.wait_ge(ss.h, f7d[st])
                    if st >= 2:
                        v.wait_ge(dX.h, f7spl[st - 2])
                    sa = SA[:, 1024 * (st % 2):1024 * (st % 2) + 1024]
                    v.tensor_tensor(out=f7s(st, 8, 9), in0=f7s(st, 1, 2), in1=f7s(st, 7, 8), op=ALU.mult)
                    v.tensor_scalar(out=sa, in0=tok2t(st), scalar1=f7s(st, 7, 8), scalar2=f7s(st, 8, 9),
                                    op0=ALU.mult, op1=ALU.subtract)
                    v.tensor_tensor(out=sa, in0=sa, in1=lnw2B[:, :], op=ALU.mult)
                    vv.inc(v.tensor_tensor(out=sa, in0=sa, in1=lnb2B[:, :], op=ALU.add), 1)
                EN("vector", _n2e)
                m5 = vv.n

                def _n2s(q, st=st, m5=m5):
                    q.wait_ge(vv.h, m5)
                    dX.inc(q.dma_start(out=n2d[128 * st:128 * (st + 1), :],
                                       in_=SA[:, 1024 * (st % 2):1024 * (st % 2) + 1024]), 16)
                EN("sync", _n2s)
                f7spl.append(dX.n)
            F_N2 = dX.n

            if debug:
                def _dbn2(q):
                    q.wait_ge(dX.h, F_N2)
                    dX.inc(q.dma_start(out=dbg["n2"][r], in_=n2d[:, :], allow_slow_non_contiguous=True), 16)
                EN("sync", _dbn2)

            def _n2t(s):
                s.wait_ge(dX.h, F_N2)
                for c in range(DC):
                    dS.inc(s.dma_start(out=TOKB[:, 1024 * c:1024 * (c + 1)], in_=n2d[:, 128 * c:128 * (c + 1)],
                                       transpose=True), 16)
            EN("scalar", _n2t)
            F_N2T = dS.n
            CUT("F7-n2")
            n2T = lambda c: TOKB[:, 1024 * c:1024 * (c + 1)]

            rout_l1 = []
            f8v0 = vv.n
            f8s0 = ss.n
            for st in range(NSEL):
                par = st % 4
                gps = PS[[5, 6, 4, 7][par]][:, 8:16]
                glt = lgt[:, 8 * par:8 * par + 8]
                gls = lambda a, b, par=par: lgs[:, 48 * par + a:48 * par + b]

                def _rmm(t, st=st, gps=gps):
                    if st < 4:
                        t.wait_ge(dS.h, F_N2T)
                        t.wait_ge(vv.h, f8v0)
                        t.wait_ge(ss.h, f8s0)
                    else:
                        t.wait_ge(vv.h, rout_l1[st - 4])
                    for k in range(DC):
                        t.matmul(gps, n2T(k)[:, 128 * st:128 * (st + 1)], mrw[:, 8 * k:8 * (k + 1)],
                                 start=(k == 0), stop=False)
                    t.matmul(gps, ones[:, :], mrbh[0:1, 0:8], start=False, stop=False)
                    pp.inc(t.matmul(gps, ones[:, :], mrbh[0:1, 8:16], start=False, stop=True), 1)
                EN("tensor", _rmm)
                mmv = pp.n

                def _r1(v, st=st, mmv=mmv, gps=gps, glt=glt, gls=gls):
                    v.wait_ge(pp.h, mmv)
                    v.tensor_copy(out=glt, in_=gps)
                    v.max(out=gls(8, 16), in_=glt)
                    vv.inc(v.tensor_scalar(out=gls(0, 1), in0=gls(8, 9), scalar1=-1.0, scalar2=None,
                                           op0=ALU.mult), 1)
                EN("vector", _r1)
                rout_l1.append(vv.n)
                m1 = vv.n

                def _r2(s, st=st, m1=m1, glt=glt, gls=gls):
                    s.wait_ge(vv.h, m1)
                    ss.inc(s.activation(out=gls(16, 24), in_=glt, func=AF.Exp, bias=gls(0, 1)), 1)
                EN("scalar", _r2)
                m2 = ss.n

                def _r3(v, st=st, m2=m2, glt=glt, gls=gls):
                    v.wait_ge(ss.h, m2)
                    v.tensor_scalar(out=gls(24, 32), in0=glt, scalar1=gls(9, 10), scalar2=None, op0=ALU.is_ge)
                    v.tensor_tensor(out=gls(32, 40), in0=gls(16, 24), in1=gls(24, 32), op=ALU.mult)
                    v.reduce_sum(gls(1, 2), gls(32, 40), axis=AX)
                    v.reciprocal(out=gls(2, 3), in_=gls(1, 2))
                    vv.inc(v.tensor_scalar(out=gatet[:, 8 * st:8 * (st + 1)], in0=gls(32, 40), scalar1=gls(2, 3),
                                           scalar2=None, op0=ALU.mult), 1)
                EN("vector", _r3)
            F_GATE = vv.n
            CUT("F8-router")

            if debug:
                def _dbgt(q):
                    q.wait_ge(vv.h, F_GATE)
                    dX.inc(q.dma_start(out=dbg["gate"][r].rearrange("(st p) e -> p st e", p=128), in_=gatet[:, :].rearrange("p (st e) -> p st e", st=8), allow_slow_non_contiguous=True), 16)
                EN("sync", _dbgt)

            w1sl = lambda i: GB2[:, 2048 * (i % 2):2048 * (i % 2) + 2048]
            w2sl = lambda i: GB2[:, 4096 + 2048 * (i % 2):4096 + 2048 * (i % 2) + 2048]
            h1g = lambda g, kc: GB0[:, 16384 * g + 512 * kc:16384 * g + 512 * (kc + 1)]
            moe_w1ld = []
            moe_h1ev = []
            moe_eomm = []
            moe_eoev = []
            moe_b2rdy = []
            eo_prev_base = None
            for e in range(E):
                def _lb2(q, e=e):
                    if e >= 2:
                        q.wait_ge(pp.h, moe_b2rdy[e - 2])
                    dG.inc(q.dma_start(out=b2row[0:1, 1024 * (e % 2):1024 * (e % 2) + 1024],
                                       in_=b2_moe[e:e + 1, :]), 16)
                EN("gpsimd", _lb2)
                b2ld = dG.n

                for Mg in range(16):
                    def _lw1(q, e=e, Mg=Mg):
                        gl = e * 16 + Mg
                        if gl == 0:
                            q.wait_ge(pp.h, at_ld[-1])
                        if len(moe_w1ld) >= 2:
                            q.wait_ge(pp.h, moe_w1ld[-2])
                        dG.inc(q.dma_start(out=w1sl(gl).rearrange("p (c m) -> p c m", c=8),
                                           in_=w1_moe[e][:, 256 * Mg:256 * (Mg + 1)]
                                           .rearrange("(c p) m -> p c m", p=128)), 16)
                    EN("gpsimd", _lw1)
                    lw1 = dG.n

                    for sub in range(4):
                        g, m2_ = sub // 2, sub % 2
                        gi = (e * 16 + Mg) * 4 + sub
                        Mh = 2 * Mg + m2_

                        def _hmm(t, e=e, Mg=Mg, g=g, m2_=m2_, gi=gi, lw1=lw1, sub=sub):
                            if sub == 0:
                                t.wait_ge(dG.h, lw1)
                            if gi == 0:
                                t.wait_ge(vv.h, F_GATE)
                            if e > 0 and Mg == 0 and sub == 0:
                                t.wait_ge(vv.h, eo_prev_base + 4)
                            bank = gi % 4
                            if gi >= 4:
                                t.wait_ge(ss.h, moe_h1ev[gi - 4])
                            gl = e * 16 + Mg
                            for k in range(DC):
                                mm = t.matmul(PS[bank][:, 0:512],
                                              w1sl(gl)[:, 256 * k + 128 * m2_:256 * k + 128 * (m2_ + 1)],
                                              n2T(k)[:, 512 * g:512 * (g + 1)], start=(k == 0), stop=(k == DC - 1))
                            pp.inc(mm, 1)
                        EN("tensor", _hmm)
                        mmv = pp.n
                        if sub == 3:
                            moe_w1ld.append(mmv)

                        def _hev(s, e=e, g=g, Mh=Mh, gi=gi, mmv=mmv):
                            s.wait_ge(pp.h, mmv)
                            if gi == 0:
                                s.wait_ge(dS.h, F_N2T)
                                s.wait_ge(pp.h, E_FFNMM)
                            ss.inc(s.activation(out=h1g(g, Mh), in_=PS[gi % 4][:, 0:512],
                                                func=AF.Gelu, bias=b1sb[:, 32 * e + Mh:32 * e + Mh + 1]), 1)
                        EN("scalar", _hev)
                        moe_h1ev.append(ss.n)
                H1_DONE = ss.n

                for g in range(2):
                    for j in range(16):
                        glw = len(moe_eomm)

                        def _lw2(q, e=e, j=j, glw=glw):
                            if glw == 0:
                                q.wait_ge(pp.h, at_ld[-1])
                            q.wait_ge(dW2C.h, 16 * 32)
                            if glw >= 2:
                                q.wait_ge(pp.h, moe_eomm[glw - 2])
                            sW2.inc(glw, q.dma_start(out=w2sl(glw).rearrange("p (c m) -> p c m", c=2),
                                                     in_=w2bf[e][256 * j:256 * (j + 1), :]
                                                     .rearrange("(c p) m -> p c m", p=128)), 16)
                        EN("sync", _lw2)
                        lw2 = (glw, sW2.val(glw))

                        def _eomm(t, e=e, g=g, j=j, glw=glw, lw2=lw2, b2ld=b2ld, prev=eo_prev_base):
                            t.wait_ge(sW2.sem(lw2[0]).h, lw2[1])
                            if j == 0:
                                t.wait_ge(ss.h, H1_DONE)
                                if prev is None:
                                    t.wait_ge(vv.h, F_GATE)
                                if g == 0:
                                    t.wait_ge(dG.h, b2ld)
                            for c in range(2):
                                kc = 2 * j + c
                                for Mr in range(4):
                                    for dh in range(2):
                                        bank = 2 * Mr + dh
                                        pv = PS[bank][:, 0:512]
                                        if j == 0 and c == 0:
                                            if prev is not None:
                                                t.wait_ge(vv.h, prev + bank + 1)
                                            t.matmul(pv, ones[:, :],
                                                     b2row[0:1, 1024 * (e % 2) + 512 * dh:1024 * (e % 2) + 512 * dh + 512],
                                                     start=True, stop=False)
                                        mm = t.matmul(pv, h1g(g, kc)[:, 128 * Mr:128 * (Mr + 1)],
                                                      w2sl(glw)[:, 1024 * c + 512 * dh:1024 * c + 512 * (dh + 1)],
                                                      start=False, stop=(j == 15 and c == 1))
                            pp.inc(mm, 1)
                        EN("tensor", _eomm)
                        moe_eomm.append(pp.n)
                    mmv = pp.n

                    eo_prev_base = vv.n
                    for Mr in range(4):
                        for dh in range(2):
                            def _eoev(v, e=e, g=g, Mr=Mr, dh=dh, mmv=mmv):
                                v.wait_ge(pp.h, mmv)
                                bank = 2 * Mr + dh
                                st = 4 * g + Mr
                                acc = GF0[:, 1024 * st + 512 * dh:1024 * st + 512 * dh + 512]
                                vv.inc(v.scalar_tensor_tensor(out=acc, in0=PS[bank][:, 0:512],
                                                              scalar=gatet[:, 8 * st + e:8 * st + e + 1],
                                                              in1=acc, op0=ALU.mult, op1=ALU.add), 1)
                            EN("vector", _eoev)
                    moe_eoev.append(vv.n)
                moe_b2rdy.append(pp.n)
            F_MOE = vv.n
            CUT("F9-moe")

            if debug:
                def _dbt3(q):
                    q.wait_ge(vv.h, F_MOE)
                    for st in range(NSEL):
                        dX.inc(q.dma_start(out=dbg["tok3"][r, 128 * st:128 * (st + 1), :], in_=tok2t(st), allow_slow_non_contiguous=True), 16)
                EN("sync", _dbt3)

            def _tfs(q):
                q.wait_ge(vv.h, F_MOE)
                dG.inc(q.dma_start(out=tokfd.rearrange("(st p) d -> p st d", p=128), in_=GF0[:, :].rearrange("p (st d) -> p st d", st=8)), 16)
            EN("gpsimd", _tfs)
            G_TF = dG.n

            ssel = lambda i: GB2[:, 1024 * (i % 2):1024 * (i % 2) + 1024]
            hirl = lambda i: GB2[:, 2048 + 1024 * (i % 2):2048 + 1024 * (i % 2) + 1024]
            grl = lambda i: GB2[:, 4096 + 1024 * (i % 2):4096 + 1024 * (i % 2) + 1024]
            xrl = lambda i: GF0[:, 1024 * (i % 2):1024 * (i % 2) + 1024]
            orl = lambda i: GF0[:, 2048 + 1024 * (i % 2):2048 + 1024 * (i % 2) + 1024]
            cp_ev = []
            for tt in range(NT):
                def _gsl(q, tt=tt):
                    if tt == 0:
                        q.wait_ge(dG.h, G_TF)
                        q.wait_ge(pp.h, moe_b2rdy[-1])
                    if tt >= 2:
                        q.wait_ge(vv.h, cp_ev[tt - 2])
                    dG.inc(q.indirect_dma_start(
                        out=ssel(tt), out_offset=None,
                        in_=tokfd[0:1, :], in_offset=bass.IndirectOffsetOnAxis(ogat[:, tt:tt + 1], 0)), 16)
                EN("gpsimd", _gsl)
                gsl = dG.n

                def _xrl(q, tt=tt):
                    if tt == 0:
                        q.wait_ge(dG.h, G_TF)
                        q.wait_ge(dX.h, E_G)
                    if tt >= 2:
                        q.wait_ge(vv.h, cp_ev[tt - 2])
                    dX.inc(q.dma_start(out=xrl(tt), in_=xrow[128 * tt:128 * (tt + 1), :]), 16)
                    dX.inc(q.dma_start(out=hirl(tt), in_=hid[128 * tt:128 * (tt + 1), :]), 16)
                    dX.inc(q.dma_start(out=grl(tt), in_=gd[128 * tt:128 * (tt + 1), :]), 16)
                EN("sync", _xrl)
                xld = dX.n

                def _cmp(v, tt=tt, gsl=gsl, xld=xld):
                    v.wait_ge(dG.h, gsl)
                    v.wait_ge(dX.h, xld)
                    v.tensor_scalar(out=SA[:, 0:1024], in0=ssel(tt), scalar1=mwT[:, tt:tt + 1], scalar2=None,
                                    op0=ALU.mult)
                    v.scalar_tensor_tensor(out=SA[:, 1024:2048], in0=hirl(tt), scalar=nmT[:, tt:tt + 1],
                                           in1=SA[:, 0:1024], op0=ALU.mult, op1=ALU.add)
                    v.tensor_tensor(out=SA[:, 0:1024], in0=SA[:, 1024:2048], in1=grl(tt), op=ALU.mult)
                    vv.inc(v.tensor_tensor(out=orl(tt), in0=xrl(tt), in1=SA[:, 0:1024], op=ALU.add), 1)
                EN("vector", _cmp)
                cpm = vv.n

                def _ost(q, tt=tt, cpm=cpm):
                    q.wait_ge(vv.h, cpm)
                    dX.inc(q.dma_start(out=outrow[128 * tt:128 * (tt + 1), :], in_=orl(tt)), 16)
                EN("sync", _ost)
                ostm = dX.n

                def _cpw(v, ostm=ostm):
                    v.wait_ge(dX.h, ostm)
                    vv.inc(v.engine_nop(), 1)
                EN("vector", _cpw)
                cp_ev.append(vv.n)

            jx, jg, js2, jv, jsc, jp = dX.n, dG.n, dS.n, vv.n, ss.n, pp.n

            def _j(q):
                q.wait_ge(dX.h, jx); q.wait_ge(dG.h, jg); q.wait_ge(dS.h, js2)
                q.wait_ge(vv.h, jv); q.wait_ge(ss.h, jsc); q.wait_ge(pp.h, jp)
            EN("sync", _j)

            def _jg(q):
                q.wait_ge(dX.h, jx); q.wait_ge(dG.h, jg); q.wait_ge(dS.h, js2)
                q.wait_ge(vv.h, jv); q.wait_ge(ss.h, jsc); q.wait_ge(pp.h, jp)
            EN("gpsimd", _jg)

            def _jv(v):
                v.wait_ge(dX.h, jx); v.wait_ge(dG.h, jg); v.wait_ge(dS.h, js2)
                v.wait_ge(ss.h, jsc); v.wait_ge(pp.h, jp)
                vv.inc(v.engine_nop(), 1)
            EN("vector", _jv)
            rowjoin_v = vv.n
            CUT("G-compose")

            def _js(sx):
                sx.wait_ge(dX.h, jx); sx.wait_ge(dG.h, jg); sx.wait_ge(dS.h, js2)
                sx.wait_ge(vv.h, rowjoin_v); sx.wait_ge(pp.h, jp)
            EN("scalar", _js)

            def _jt(t):
                t.wait_ge(dX.h, jx); t.wait_ge(dG.h, jg); t.wait_ge(dS.h, js2)
                t.wait_ge(vv.h, rowjoin_v); t.wait_ge(ss.h, jsc)
            EN("tensor", _jt)

      except StopEmission as e:
        print("EMISSION CUT AT:", e)
      finally:
        jx, jg, js2, jv, jsc, jp = dX.n, dG.n, dS.n, vv.n, ss.n, pp.n

        def _fj(q):
            q.wait_ge(dX.h, jx); q.wait_ge(dG.h, jg); q.wait_ge(dS.h, js2)
            q.wait_ge(vv.h, jv); q.wait_ge(ss.h, jsc); q.wait_ge(pp.h, jp)
            q.wait_ge(sSP.h, sSP.n); q.wait_ge(sSC.h, sSC.n)
            for _sl in (sxr, sW2):
                for _s in _sl.s:
                    q.wait_ge(_s.h, _s.n)
        EN("sync", _fj)
    ctx.close()
    return nc, dbg



def _consts():
    q = np.arange(128)
    return {
        "cONES": np.ones((1, 128), dtype=ml_dtypes.bfloat16),
        "cONESC": np.ones((128, 1), dtype=ml_dtypes.bfloat16),
        "cTRIle": (q[:, None] <= q[None, :]).astype(ml_dtypes.bfloat16),
        "cTRIst": (q[None, :] < q[:, None]).astype(ml_dtypes.bfloat16),
        "cID": np.eye(128, dtype=ml_dtypes.bfloat16),
    }


_NC_CACHE = {}


def _get_nc(debug=False):
    if debug not in _NC_CACHE:
        _NC_CACHE[debug] = build_nc(debug)
    return _NC_CACHE[debug]


def kernel(**inputs):
    nc, _ = _get_nc(False)
    consts = _consts()
    x = np.ascontiguousarray(np.asarray(inputs["L0_attn_x"], dtype=np.float32))
    B = x.shape[0]
    percore = B // 8
    wmap = {
        "m_in_w": "L0_attn_m_in_w", "m_conv_w": "L0_attn_m_conv_w", "m_conv_b": "L0_attn_m_conv_b",
        "m_xproj_w": "L0_attn_m_xproj_w", "m_out_w": "L0_attn_m_out_w",
        "modr_w1": "L0_attn_modr_w1", "modr_b1": "L0_attn_modr_b1", "modr_w2": "L0_attn_modr_w2",
        "modr_b2": "L0_attn_modr_b2",
        "ln1_w": "L0_attn_ln1_w", "ln1_b": "L0_attn_ln1_b", "ln2_w": "L0_attn_ln2_w", "ln2_b": "L0_attn_ln2_b",
        "moe_r_w": "L0_attn_moe_r_w", "moe_r_b": "L0_attn_moe_r_b",
        "moe_w1": "L0_attn_moe_w1", "moe_b1": "L0_attn_moe_b1",
        "moe_w2": "L0_attn_moe_w2", "moe_b2": "L0_attn_moe_b2",
        "gate_w": "L0_ffn_gate_w", "gate_b": "L0_ffn_gate_b",
        "expert_w": "L0_ffn_expert_w", "expert_b": "L0_ffn_expert_b",
    }
    base = {k: np.ascontiguousarray(np.asarray(inputs[v], dtype=np.float32)) for k, v in wmap.items()}
    f32c = lambda a_: np.ascontiguousarray(np.asarray(a_, dtype=np.float32))
    base["h_conv3"] = f32c(base.pop("m_conv_w")[:, 3].reshape(16, 128).T)
    base["h_convb"] = f32c(base.pop("m_conv_b").reshape(16, 128).T)
    base["h_r1b"] = f32c(base.pop("modr_b1").reshape(2, 128).T)
    base["h_r2"] = f32c(base.pop("modr_w2").reshape(2, 128).T)
    base["h_b1"] = f32c(base.pop("moe_b1").reshape(8, 32, 128).transpose(2, 0, 1).reshape(128, 256))
    base["h_mrw"] = f32c(base.pop("moe_r_w").reshape(8, 128, 8).transpose(1, 0, 2).reshape(128, 64))
    base["h_gw"] = f32c(base.pop("gate_w").reshape(8, 128, 8).transpose(1, 0, 2).reshape(128, 64))
    base.update(consts)
    in_maps = []
    for c in range(8):
        m = dict(base)
        m["xin"] = np.ascontiguousarray(x[c * percore:(c + 1) * percore].reshape(percore * S, D))
        in_maps.append(m)
    res = run_bass_kernel_spmd(nc, in_maps, core_ids=list(range(8)))
    outs = [res.results[c]["out"].reshape(percore, S, D) for c in range(8)]
    return np.concatenate(outs, axis=0)
